# version 10
# baseline (speedup 1.0000x reference)
import numpy as np
import ml_dtypes
import concourse.bass as bass
import concourse.mybir as mybir
from concourse.bass_utils import run_bass_kernel_spmd
from contextlib import ExitStack

F32 = mybir.dt.float32
BF16 = mybir.dt.bfloat16
AF = mybir.ActivationFunctionType
ALU = mybir.AluOpType
AX = mybir.AxisListType

NCORES = 8
T = 2048
D = 1024
DC = 8
NT = 4
G = 512
BIGNEG = -1e30

ENGS = ("sync", "gpsimd", "vector", "scalar", "tensor")


class Sem:
    def __init__(self, handle, owner, is_dma=False):
        self.h = handle
        self.owner = owner
        self.is_dma = is_dma
        self.n = 0


class Buf:

    def __init__(self, ctx):
        self.ctx = ctx
        self.w = None
        self.readers = []


class Ctx:
    def __init__(self, nc, st):
        e = st.enter_context
        self.nc = nc
        self.q = {en: [] for en in ENGS}
        self.s = {
            "sync": Sem(e(nc.semaphore("s_sync")), "sync", True),
            "out": Sem(e(nc.semaphore("s_out")), "sync", True),
            "gpsimd": Sem(e(nc.semaphore("s_gp")), "gpsimd", True),
            "ws0": Sem(e(nc.semaphore("s_ws0")), "gpsimd", True),
            "ws1": Sem(e(nc.semaphore("s_ws1")), "gpsimd", True),
            "tensor": Sem(e(nc.semaphore("s_te")), "tensor"),
            "vector": Sem(e(nc.semaphore("s_ve")), "vector"),
            "scalar": Sem(e(nc.semaphore("s_sc")), "scalar"),
        }
        self.ap = {}

    def _wait(self, eng, sem, tgt):
        if tgt > 0:
            self.q[eng].append(lambda e, s=sem.h, t=tgt: e.wait_ge(s, t))

    def _hazard_waits(self, eng, reads, writes):
        seen = set()

        def w(sem, tgt):
            key = (id(sem), tgt)
            if key in seen:
                return
            seen.add(key)
            self._wait(eng, sem, tgt)

        def wv(sem, tgt):
            w(sem, sem.n if sem.is_dma else tgt)

        for b in reads:
            if b.w is not None:
                wv(*b.w)
        for b in writes:
            if b.w is not None:
                wv(*b.w)
            for sem, tgt in b.readers:
                wv(sem, tgt)

    def _hazard_update(self, sem, tgt, reads, writes):
        for b in reads:
            b.readers.append((sem, tgt))
        for b in writes:
            b.w = (sem, tgt)
            b.readers = []

    def step(self, eng, fn, reads=(), writes=(), extra_waits=()):
        sem = self.s[eng]
        self._hazard_waits(eng, reads, writes)
        for s2, t2 in extra_waits:
            self._wait(eng, s2, t2)
        sem.n += 1
        tgt = sem.n
        self.q[eng].append(lambda e, f=fn, s=sem.h: f(e).then_inc(s, 1))
        self._hazard_update(sem, tgt, reads, writes)
        return tgt

    def dma(self, eng, out_ap, in_ap, reads=(), writes=(), sem=None, **kw):
        sem = sem or (self.s["out"] if eng == "sync" and any(writes) is False and False else None)
        sem = self.s[eng] if sem is None else sem
        self._hazard_waits(eng, reads, writes)
        sem.n += 16
        tgt = sem.n
        self.q[eng].append(lambda e, o=out_ap, i=in_ap, s=sem.h, k=dict(kw):
                           e.dma_start(out=o, in_=i, **k).then_inc(s, 16))
        self._hazard_update(sem, tgt, reads, writes)
        return tgt

    def barrier(self, bufs=None, exclude=()):
        targets = [(s, s.n) for k, s in self.s.items() if s.n > 0 and k not in exclude]
        for eng in ENGS:
            for s, t in targets:
                if s.owner != eng:
                    self._wait(eng, s, t)
        if bufs:
            for b in bufs.values():
                b.w = None
                b.readers = []

    def selfwait(self, eng):
        sem = self.s[eng]
        self._wait(eng, sem, sem.n)

    def raw_wait(self, eng, sem, tgt):
        self._wait(eng, sem, tgt)

    def inp(self, name, shape, dt=F32):
        self.ap[name] = self.nc.dram_tensor(name, list(shape), dt, kind="ExternalInput").ap()
        return self.ap[name]

    def outp(self, name, shape, dt=F32):
        self.ap[name] = self.nc.dram_tensor(name, list(shape), dt, kind="ExternalOutput").ap()
        return self.ap[name]

    def finish(self, block):
        self._wait("sync", self.s["out"], self.s["out"].n)
        q = self.q

        @block.sync
        def _(sync):
            for fn in q["sync"]:
                fn(sync)

        @block.gpsimd
        def _(gpsimd):
            for fn in q["gpsimd"]:
                fn(gpsimd)

        @block.vector
        def _(vector):
            for fn in q["vector"]:
                fn(vector)

        @block.scalar
        def _(scalar):
            for fn in q["scalar"]:
                fn(scalar)

        @block.tensor
        def _(tensor):
            for fn in q["tensor"]:
                fn(tensor)


def host_constants():
    c = {}
    c["ones"] = np.ones((128, 128), np.float32)
    c["onesb"] = np.ones((128, 128), np.float32).astype(ml_dtypes.bfloat16)
    c["I128"] = np.eye(128, dtype=np.float32)
    c["I128b"] = np.eye(128, dtype=np.float32).astype(ml_dtypes.bfloat16)
    jj, ii = np.meshgrid(np.arange(128), np.arange(128), indexing="ij")
    c["U128"] = (jj <= ii).astype(np.float32)
    c["U512"] = np.ascontiguousarray(np.tile((jj <= ii).astype(np.float32), (1, 4)))
    c["maskTneg"] = np.where(jj <= ii, 0.0, BIGNEG).astype(np.float32)
    sel8 = np.zeros((8, 8 * 128), np.float32)
    for e in range(8):
        sel8[e, e * 128:(e + 1) * 128] = 1.0
    c["sel8"] = sel8
    sel16 = np.zeros((16, 16 * 128), np.float32)
    for t_ in range(16):
        sel16[t_, t_ * 128:(t_ + 1) * 128] = 1.0
    c["sel16"] = sel16
    inv = 1.0 / (10000.0 ** (np.arange(0, 128, 2, dtype=np.float64) / 128.0))
    fr = np.outer(inv, np.arange(T, dtype=np.float64))
    cos = np.cos(fr).astype(np.float32)
    sin = np.sin(fr).astype(np.float32)
    c["cosf"] = np.ascontiguousarray(np.concatenate([cos, cos], axis=0))
    c["sinm"] = np.ascontiguousarray(np.concatenate([-sin, sin], axis=0))
    return c


def colmat(v, chunks):
    return np.ascontiguousarray(np.asarray(v, np.float32).reshape(chunks, 128).T)


def fm(dram_ap):
    return dram_ap.rearrange("(c p) t -> p c t", p=128)


def fm3(sbuf_tile, width):
    return sbuf_tile[:, :].rearrange("p (c t) -> p c t", t=width)



def build_l0a():
    nc = bass.Bass("TRN2", target_bir_lowering=False)
    with ExitStack() as st:
        cx = Ctx(nc, st)
        e = st.enter_context
        inp, outp = cx.inp, cx.outp

        inp("xT", (D, T))
        inp("ones", (128, 128)); inp("I128", (128, 128)); inp("U128", (128, 128))
        inp("onesb", (128, 128), BF16); inp("I128b", (128, 128), BF16)
        inp("WqkvT", (D, 3 * D)); inp("bq_cols", (128, 24))
        inp("WoutT", (D, D)); inp("bout_cols", (128, 8))
        inp("gamma_cols", (128, 8)); inp("beta_cols", (128, 8))
        inp("rwT", (D, 8))
        outp("xn_T", (D, T)); outp("acc_T", (D, T)); outp("cwT", (8, T))

        sb = {}

        def tile(name, shape, dt=F32):
            sb[name] = e(nc.sbuf_tensor("sb_" + name, list(shape), dt))
            return sb[name]

        x_g = tile("x_g", [128, DC * G])
        xnb_g = tile("xnb_g", [128, DC * G], BF16)
        wq_b = tile("wq_b", [128, DC * 3 * D], BF16)
        wo_b = tile("wo_b", [128, DC * D], BF16)
        rw_f = tile("rw_f", [128, DC * 8])
        ones = tile("ones", [128, 128]); I128 = tile("I128", [128, 128])
        U128 = tile("U128", [128, 128])
        onesb = tile("onesb", [128, 128], BF16); I128b = tile("I128b", [128, 128], BF16)
        bq_cols = tile("bq_cols", [128, 24]); nbq_cols = tile("nbq_cols", [128, 24])
        bout_cols = tile("bout_cols", [128, 8])
        gamma_cols = tile("gamma_cols", [128, 8]); beta_cols = tile("beta_cols", [128, 8])
        sq_sb = tile("sq_sb", [128, G])
        row_a = tile("row_a", [1, G]); row_b = tile("row_b", [1, G]); row_c = tile("row_c", [1, G])
        mb_sb = tile("mb_sb", [128, G]); rb_sb = tile("rb_sb", [128, G])
        xn_sc = tile("xn_sc", [128, G])
        phiq = tile("phiq", [64, 16 * G], BF16)
        phik = tile("phik", [64, 16 * G], BF16)
        v_fm = tile("v_fm", [128, 8 * G], BF16)
        t_r1 = tile("t_r1", [128, G]); t_r2 = tile("t_r2", [128, G])
        t_e2 = tile("t_e2", [128, G]); t_s = tile("t_s", [128, G])
        v_tm = tile("v_tm", [128, 4 * 1024], BF16)
        k_tm = tile("k_tm", [128, 4 * 1024], BF16)
        A_sb = tile("A_sb", [128, 128], BF16)
        S_all = tile("S_all", [64, 16 * 72])
        S_allb = tile("S_allb", [64, 16 * 72], BF16)
        dn_row = tile("dn_row", [1, 128]); rdn_row = tile("rdn_row", [1, 128])
        rdn_b = tile("rdn_b", [64, 128])
        ol_fm = tile("ol_fm", [128, 8 * G], BF16)
        a0_fm = tile("a0_fm", [128, 8 * G])
        lg_fm = tile("lg_fm", [8, G])
        lg_tm = tile("lg_tm", [128, 8])
        pr = {n: tile("pr_" + n, [128, w]) for n, w in
              [("m", 1), ("e", 8), ("s", 1), ("rs", 1), ("p", 8), ("p1", 1),
               ("oh", 8), ("pm", 8), ("p2", 1), ("mk", 8), ("dn", 1), ("rd", 1), ("cw", 8)]}
        cwT_sb = tile("cwT_sb", [8, 128])

        p_mm1 = e(nc.psum_tensor("p_mm1", [128, G], F32))
        p_mm2 = e(nc.psum_tensor("p_mm2", [128, G], F32))
        p_bc = e(nc.psum_tensor("p_bc", [128, G], F32))
        p_tp = e(nc.psum_tensor("p_tp", [128, 1024], BF16))
        pA = e(nc.psum_tensor("pA", [128, 128], F32))
        pN = e(nc.psum_tensor("pN", [128, 128], F32))
        pB = e(nc.psum_tensor("pB", [128, 128], F32))
        pS = e(nc.psum_tensor("pS", [128, 128], F32))

        B = {}

        def buf(name):
            if name not in B:
                B[name] = Buf(cx)
            return B[name]

        block = e(nc.Block())

        for n in ("ones", "I128", "U128", "bq_cols", "bout_cols", "gamma_cols", "beta_cols"):
            cx.dma("sync", sb[n][:, :], cx.ap[n], writes=[buf(n)], sem=cx.s["sync"])
        cx.dma("sync", sb["onesb"][:, :], cx.ap["onesb"], writes=[buf("onesb")], sem=cx.s["sync"])
        cx.dma("sync", sb["I128b"][:, :], cx.ap["I128b"], writes=[buf("I128b")], sem=cx.s["sync"])
        cx.dma("gpsimd", fm3(wq_b, 3 * D), fm(cx.ap["WqkvT"]), writes=[buf("wq_b")], sem=cx.s["gpsimd"])
        cx.dma("gpsimd", fm3(wo_b, D), fm(cx.ap["WoutT"]), writes=[buf("wo_b")], sem=cx.s["gpsimd"])
        cx.dma("sync", fm3(rw_f, 8), fm(cx.ap["rwT"]), writes=[buf("rw_f")], sem=cx.s["sync"])

        cx.step("vector", lambda e_: e_.tensor_scalar_mul(nbq_cols[:, :], bq_cols[:, :], -1.0),
                reads=[buf("bq_cols")], writes=[buf("nbq_cols")])
        cx.step("vector", lambda e_: e_.memset(S_all[:, :], 0.0), writes=[buf("S_all")])
        cx.step("vector", lambda e_: e_.memset(S_allb[:, :], 0.0), writes=[buf("S_allb")])

        bx = buf("x_g"); bxnb = buf("xnb_g"); bsq = buf("sq_sb")
        bra, brb, brc = buf("row_a"), buf("row_b"), buf("row_c")
        bmb, brbs = buf("mb_sb"), buf("rb_sb")
        bxn = buf("xn_sc")
        bp1, bp2, bpbc = buf("p_mm1"), buf("p_mm2"), buf("p_bc")
        bptp = buf("p_tp")
        bpA, bpN, bpB, bpS = buf("pA"), buf("pN"), buf("pB"), buf("pS")
        bphiq, bphik, bvfm = buf("phiq"), buf("phik"), buf("v_fm")
        br1, br2, be2, bts = buf("t_r1"), buf("t_r2"), buf("t_e2"), buf("t_s")
        bvtm, bktm = buf("v_tm"), buf("k_tm")
        bA = buf("A4_sb"); bS, bSb = buf("S_all"), buf("S_allb")
        bdn, brdn, brdnb = buf("dn4"), buf("rdn4"), buf("rdnb4")
        bol = buf("ol_fm"); ba0 = buf("a0_fm")
        blg, blgt = buf("lg_fm"), buf("lg_tm")
        bpr = {n: buf("pr_" + n) for n in pr}
        bcw = buf("cwT_sb")

        def ln_stats(src_fn, src_buf, extra_eps=1e-5):
            cx._hazard_waits("tensor", [src_buf], [bp1, bp2])
            for c in range(DC):
                src = src_fn(c)
                t_sq = cx.step("scalar", lambda e_, src=src: e_.activation(sq_sb[:, :], src, AF.Square),
                               reads=[src_buf], writes=[bsq])
                cx.step("tensor", lambda e_, src=src, c=c: e_.matmul(
                    p_mm1[0:1, :], ones[:, 0:1], src, start=(c == 0), stop=(c == DC - 1)))
                cx.step("tensor", lambda e_, c=c: e_.matmul(
                    p_mm2[0:1, :], ones[:, 0:1], sq_sb[:, :], start=(c == 0), stop=(c == DC - 1)),
                    reads=[bsq])
            cx._hazard_update(cx.s["tensor"], cx.s["tensor"].n, [src_buf], [bp1, bp2])
            cx.step("vector", lambda e_: e_.tensor_scalar_mul(row_a[0:1, :], p_mm1[0:1, :], 1.0 / D),
                    reads=[bp1], writes=[bra])
            cx.step("vector", lambda e_: e_.tensor_scalar_mul(row_b[0:1, :], p_mm2[0:1, :], 1.0 / D),
                    reads=[bp2], writes=[brb])
            cx.step("vector", lambda e_: e_.tensor_mul(row_c[0:1, :], row_a[0:1, :], row_a[0:1, :]),
                    reads=[bra], writes=[brc])
            cx.step("vector", lambda e_: e_.tensor_tensor(row_b[0:1, :], row_b[0:1, :], row_c[0:1, :], ALU.subtract),
                    reads=[brc], writes=[brb])
            cx.step("vector", lambda e_: e_.tensor_scalar_add(row_b[0:1, :], row_b[0:1, :], extra_eps),
                    writes=[brb])
            cx.step("scalar", lambda e_: e_.activation(row_c[0:1, :], row_b[0:1, :], AF.Ln),
                    reads=[brb], writes=[brc])
            cx.selfwait("scalar")
            cx.step("scalar", lambda e_: e_.activation(row_b[0:1, :], row_c[0:1, :], AF.Exp, scale=-0.5),
                    reads=[brc], writes=[brb])
            cx.step("tensor", lambda e_: e_.matmul(p_bc[:, :], ones[0:1, :], row_a[0:1, :], start=True, stop=True),
                    reads=[bra], writes=[bpbc])
            cx.step("vector", lambda e_: e_.tensor_copy(mb_sb[:, :], p_bc[:, :]),
                    reads=[bpbc], writes=[bmb])
            cx.step("tensor", lambda e_: e_.matmul(p_bc[:, :], ones[0:1, :], row_b[0:1, :], start=True, stop=True),
                    reads=[brb], writes=[bpbc])
            cx.step("vector", lambda e_: e_.tensor_copy(rb_sb[:, :], p_bc[:, :]),
                    reads=[bpbc], writes=[brbs])

        def router_tail(tl):
                cx.step("vector", lambda e_: e_.reduce_max(pr["m"][:, 0:1], lg_tm[:, :], axis=AX.X, negate=True),
                        reads=[blgt], writes=[bpr["m"]])
                cx.step("scalar", lambda e_: e_.activation(
                    pr["e"][:, :], lg_tm[:, :], AF.Exp, bias=pr["m"][:, 0:1], accum_out=pr["s"][:, 0:1]),
                    reads=[blgt, bpr["m"]], writes=[bpr["e"], bpr["s"]])
                cx.step("vector", lambda e_: e_.reciprocal(pr["rs"][:, 0:1], pr["s"][:, 0:1]),
                        reads=[bpr["s"]], writes=[bpr["rs"]])
                cx.selfwait("vector")
                cx.step("vector", lambda e_: e_.tensor_scalar_mul(pr["p"][:, :], pr["e"][:, :], pr["rs"][:, 0:1]),
                        reads=[bpr["e"], bpr["rs"]], writes=[bpr["p"]])
                cx.selfwait("vector")
                cx.step("vector", lambda e_: e_.reduce_max(pr["p1"][:, 0:1], pr["p"][:, :], axis=AX.X),
                        reads=[bpr["p"]], writes=[bpr["p1"]])
                cx.selfwait("vector")
                cx.step("vector", lambda e_: e_.tensor_scalar(
                    pr["oh"][:, :], pr["p"][:, :], pr["p1"][:, 0:1], None, ALU.is_ge),
                    reads=[bpr["p"], bpr["p1"]], writes=[bpr["oh"]])
                cx.step("vector", lambda e_: e_.scalar_tensor_tensor(
                    pr["pm"][:, :], pr["oh"][:, :], BIGNEG, pr["p"][:, :], ALU.mult, ALU.add),
                    reads=[bpr["oh"], bpr["p"]], writes=[bpr["pm"]])
                cx.selfwait("vector")
                cx.step("vector", lambda e_: e_.reduce_max(pr["p2"][:, 0:1], pr["pm"][:, :], axis=AX.X),
                        reads=[bpr["pm"]], writes=[bpr["p2"]])
                cx.selfwait("vector")
                cx.step("vector", lambda e_: e_.tensor_scalar(
                    pr["mk"][:, :], pr["p"][:, :], pr["p2"][:, 0:1], None, ALU.is_ge),
                    reads=[bpr["p"], bpr["p2"]], writes=[bpr["mk"]])
                cx.step("vector", lambda e_: e_.tensor_add(pr["dn"][:, 0:1], pr["p1"][:, 0:1], pr["p2"][:, 0:1]),
                        reads=[bpr["p1"], bpr["p2"]], writes=[bpr["dn"]])
                cx.step("vector", lambda e_: e_.tensor_scalar_add(pr["dn"][:, 0:1], pr["dn"][:, 0:1], 1e-9),
                        writes=[bpr["dn"]])
                cx.selfwait("vector")
                cx.step("vector", lambda e_: e_.reciprocal(pr["rd"][:, 0:1], pr["dn"][:, 0:1]),
                        reads=[bpr["dn"]], writes=[bpr["rd"]])
                cx.step("vector", lambda e_: e_.tensor_mul(pr["cw"][:, :], pr["p"][:, :], pr["mk"][:, :]),
                        reads=[bpr["p"], bpr["mk"]], writes=[bpr["cw"]])
                cx.selfwait("vector")
                cx.step("vector", lambda e_: e_.tensor_scalar_mul(pr["cw"][:, :], pr["cw"][:, :], pr["rd"][:, 0:1]),
                        reads=[bpr["rd"]], writes=[bpr["cw"]])
                cx.step("tensor", lambda e_: e_.transpose(p_mm1[0:8, 0:128], pr["cw"][:, :], I128[:, :]),
                        reads=[bpr["cw"]], writes=[bp1])
                cx.step("vector", lambda e_: e_.tensor_copy(
                    cwT_sb[0:8, 0:128], p_mm1[0:8, 0:128]),
                    reads=[bp1], writes=[bcw])
                cx.dma("sync", cx.ap["cwT"][0:8, tl * 128:(tl + 1) * 128], cwT_sb[0:8, 0:128],
                       reads=[bcw], sem=cx.s["out"])


        for g in range(NT):
            cx.dma("sync", fm3(x_g, G), fm(cx.ap["xT"][:, g * G:(g + 1) * G]),
                   writes=[bx], sem=cx.s["sync"])

            ln_stats(lambda c: x_g[:, c * G:(c + 1) * G], bx)
            p_rt = [pA, pN, pB, pS]
            b_rt = [bpA, bpN, bpB, bpS]
            for c in range(DC):
                cx.step("vector", lambda e_, c=c: e_.tensor_tensor(
                    xn_sc[:, :], x_g[:, c * G:(c + 1) * G], mb_sb[:, :], ALU.subtract),
                    reads=[bx, bmb], writes=[bxn])
                cx.step("vector", lambda e_: e_.tensor_tensor(xn_sc[:, :], xn_sc[:, :], rb_sb[:, :], ALU.mult),
                        reads=[brbs], writes=[bxn])
                cx.step("vector", lambda e_, c=c: e_.tensor_copy(xnb_g[:, c * G:(c + 1) * G], xn_sc[:, :]),
                        reads=[bxn], writes=[bxnb])
                cx._hazard_waits("tensor", [bxn, buf("rw_f")], [])
                for blk in range(4):
                    if c == 0:
                        cx._hazard_waits("tensor", [], [b_rt[blk]])
                    cx.step("tensor", lambda e_, c=c, blk=blk: e_.matmul(
                        p_rt[blk][:, 0:8], xn_sc[:, blk * 128:(blk + 1) * 128],
                        rw_f[:, c * 8:(c + 1) * 8], start=(c == 0), stop=(c == DC - 1)))
                    if c == DC - 1:
                        cx._hazard_update(cx.s["tensor"], cx.s["tensor"].n, [bxn, buf("rw_f")], [b_rt[blk]])
                    else:
                        b_rt[blk].readers.append((cx.s["tensor"], cx.s["tensor"].n))
                        bxn.readers.append((cx.s["tensor"], cx.s["tensor"].n))
                cx.dma("sync", cx.ap["xn_T"][c * 128:(c + 1) * 128, g * G:(g + 1) * G], xn_sc[:, :],
                       reads=[bxn], sem=cx.s["out"])
            for blk in range(4):
                tl = g * 4 + blk
                cx.step("vector", lambda e_, blk=blk: e_.tensor_copy(lg_tm[:, :], p_rt[blk][:, 0:8]),
                        reads=[b_rt[blk]], writes=[blgt])
                router_tail(tl)

            for jc in range(24):
                ps, bps = (p_mm1, bp1) if jc % 2 == 0 else (p_mm2, bp2)
                cx._hazard_waits("tensor", [bxnb, buf("wq_b")], [bps])
                for c in range(DC):
                    cx.step("tensor", lambda e_, c=c, jc=jc, ps=ps: e_.matmul(
                        ps[:, :], wq_b[:, c * 3072 + jc * 128: c * 3072 + (jc + 1) * 128],
                        xnb_g[:, c * G:(c + 1) * G], start=(c == 0), stop=(c == DC - 1)))
                cx._hazard_update(cx.s["tensor"], cx.s["tensor"].n, [bxnb, buf("wq_b")], [bps])
                if jc < 16:
                    is_q = jc < 8
                    dst, bdst, dj = (phiq, bphiq, jc) if is_q else (phik, bphik, jc - 8)
                    cx.step("scalar", lambda e_, jc=jc, ps=ps: e_.activation(
                        t_r1[:, :], ps[:, :], AF.Relu, bias=bq_cols[:, jc:jc + 1]),
                        reads=[bps, buf("bq_cols")], writes=[br1])
                    cx.step("scalar", lambda e_, jc=jc, ps=ps: e_.activation(
                        t_r2[:, :], ps[:, :], AF.Relu, bias=nbq_cols[:, jc:jc + 1], scale=-1.0),
                        reads=[bps, buf("nbq_cols")], writes=[br2])
                    cx.selfwait("scalar")
                    cx.step("scalar", lambda e_: e_.activation(t_e2[:, :], t_r2[:, :], AF.Exp, scale=-1.0),
                            reads=[br2], writes=[be2])
                    cx.step("vector", lambda e_: e_.tensor_add(t_s[:, :], t_r1[:, :], t_e2[:, :]),
                            reads=[br1, be2], writes=[bts])
                    for half in range(2):
                        h_ = dj * 2 + half
                        dsl = dst[0:64, h_ * G:(h_ + 1) * G]
                        ssl = t_s[half * 64:(half + 1) * 64, :]
                        if is_q:
                            cx.step("vector", lambda e_, dsl=dsl, ssl=ssl: e_.tensor_scalar(
                                dsl, ssl, 0.125, 1.2501250e-05, ALU.mult, ALU.add),
                                reads=[bts], writes=[bdst])
                        else:
                            cx.step("vector", lambda e_, dsl=dsl, ssl=ssl: e_.tensor_scalar_add(
                                dsl, ssl, 1e-4),
                                reads=[bts], writes=[bdst])
                else:
                    dj = jc - 16
                    cx.step("scalar", lambda e_, jc=jc, dj=dj, ps=ps: e_.activation(
                        v_fm[:, dj * G:(dj + 1) * G], ps[:, :], AF.Identity, bias=bq_cols[:, jc:jc + 1]),
                        reads=[bps, buf("bq_cols")], writes=[bvfm])

            jobs = []
            for h in range(16):
                for tc in range(4):
                    jobs.append((phik[0:64, h * G + tc * 128: h * G + (tc + 1) * 128],
                                 I128b[0:64, 0:64], 64,
                                 k_tm[:, tc * 1024 + h * 64: tc * 1024 + (h + 1) * 64],
                                 bphik, bktm))
            for fc in range(8):
                for tc in range(4):
                    jobs.append((v_fm[:, fc * G + tc * 128: fc * G + (tc + 1) * 128],
                                 I128b[:, :], 128,
                                 v_tm[:, tc * 1024 + fc * 128: tc * 1024 + (fc + 1) * 128],
                                 bvfm, bvtm))
            bi = 0
            while bi < len(jobs):
                batch, used = [], 0
                while bi < len(jobs) and used + jobs[bi][2] <= 1024:
                    batch.append((used, jobs[bi]))
                    used += jobs[bi][2]
                    bi += 1
                cx._hazard_waits("tensor", [j[4] for (_, j) in batch], [bptp])
                for off, (in_ap, id_ap, w, dst_ap, bsrc, bdst) in batch:
                    cx.step("tensor", lambda e_, in_ap=in_ap, id_ap=id_ap, off=off, w=w: e_.transpose(
                        p_tp[:, off:off + w], in_ap, id_ap))
                cx._hazard_update(cx.s["tensor"], cx.s["tensor"].n,
                                  [j[4] for (_, j) in batch], [bptp])
                for off, (in_ap, id_ap, w, dst_ap, bsrc, bdst) in batch:
                    cx.step("vector", lambda e_, dst_ap=dst_ap, off=off, w=w: e_.tensor_copy(
                        dst_ap, p_tp[:, off:off + w]),
                        reads=[bptp], writes=[bdst])

            for tc in range(4):
                for h in range(16):
                    rb0 = (h % 2) * 64
                    phiq_sl = phiq[0:64, h * G + tc * 128: h * G + (tc + 1) * 128]
                    phik_sl = phik[0:64, h * G + tc * 128: h * G + (tc + 1) * 128]
                    ktm_sl = k_tm[:, tc * 1024 + h * 64: tc * 1024 + (h + 1) * 64]
                    vtm_sl = v_tm[:, tc * 1024 + h * 64: tc * 1024 + (h + 1) * 64]
                    S_sl = S_all[0:64, h * 72: h * 72 + 65]
                    Sb_sl = S_allb[0:64, h * 72: h * 72 + 65]
                    cx.step("tensor", lambda e_, ks=phik_sl, qs=phiq_sl: e_.matmul(
                        pA[:, :], ks, qs, start=True, stop=True),
                        reads=[bphik, bphiq], writes=[bpA])
                    cx.step("vector", lambda e_: e_.tensor_mul(A_sb[:, :], pA[:, :], U128[:, :]),
                            reads=[bpA, buf("U128")], writes=[bA])
                    cx._hazard_waits("tensor", [bSb, bA, bvtm, buf("onesb")], [bpN])
                    cx.step("tensor", lambda e_, qs=phiq_sl, Sb=Sb_sl: e_.matmul(
                        pN[0:65, 0:128], Sb, qs, start=True, stop=False))
                    cx.step("tensor", lambda e_, vs=vtm_sl: e_.matmul(
                        pN[0:64, 0:128], vs, A_sb[:, :], start=False, stop=False))
                    cx.step("tensor", lambda e_: e_.matmul(
                        pN[64:65, 0:128], onesb[:, 0:1], A_sb[:, :], start=False, stop=True))
                    cx._hazard_update(cx.s["tensor"], cx.s["tensor"].n,
                                      [bSb, bA, bvtm, buf("onesb")], [bpN])
                    cx.step("vector", lambda e_: e_.tensor_scalar_add(dn_row[0:1, :], pN[64:65, 0:128], 1e-4),
                            reads=[bpN], writes=[bdn])
                    cx.selfwait("vector")
                    cx.step("vector", lambda e_: e_.reciprocal(rdn_row[0:1, :], dn_row[0:1, :]),
                            reads=[bdn], writes=[brdn])
                    cx.step("tensor", lambda e_: e_.matmul(
                        pB[0:64, 0:128], ones[0:1, 0:64], rdn_row[0:1, :], start=True, stop=True),
                        reads=[brdn], writes=[bpB])
                    cx.step("vector", lambda e_: e_.tensor_copy(rdn_b[:, :], pB[0:64, 0:128]),
                            reads=[bpB], writes=[brdnb])
                    cx.selfwait("vector")
                    cx.step("vector", lambda e_, h=h, tc=tc: e_.tensor_mul(
                        ol_fm[rb0:rb0 + 64, (h // 2) * G + tc * 128:(h // 2) * G + (tc + 1) * 128]
                        if False else
                        ol_fm[(h % 2) * 64:(h % 2) * 64 + 64,
                              (h // 2) * G + tc * 128:(h // 2) * G + (tc + 1) * 128],
                        pN[0:64, 0:128], rdn_b[:, :]),
                        reads=[bpN, brdnb], writes=[bol])
                    cx._hazard_waits("tensor", [bktm, bvtm, buf("onesb")], [bpS])
                    cx.step("tensor", lambda e_, ks=ktm_sl, vs=vtm_sl: e_.matmul(
                        pS[0:64, 0:64], ks, vs, start=True, stop=False))
                    cx.step("tensor", lambda e_, ks=ktm_sl: e_.matmul(
                        pS[0:64, 64:65], ks, onesb[:, 0:1], start=False, stop=True))
                    cx._hazard_update(cx.s["tensor"], cx.s["tensor"].n,
                                      [bktm, bvtm, buf("onesb")], [bpS])
                    cx.step("vector", lambda e_, S=S_sl: e_.tensor_add(S, S, pS[0:64, 0:65]),
                            reads=[bpS], writes=[bS])
                    cx.selfwait("vector")
                    cx.step("vector", lambda e_, S=S_sl, Sb=Sb_sl: e_.tensor_copy(Sb, S),
                            reads=[bS], writes=[bSb])

            for jc in range(DC):
                ps, bps = (p_mm1, bp1) if jc % 2 == 0 else (p_mm2, bp2)
                cx._hazard_waits("tensor", [bol, buf("wo_b")], [bps])
                for c in range(DC):
                    cx.step("tensor", lambda e_, jc=jc, c=c, ps=ps: e_.matmul(
                        ps[:, :], wo_b[:, c * D + jc * 128: c * D + (jc + 1) * 128],
                        ol_fm[:, c * G:(c + 1) * G], start=(c == 0), stop=(c == DC - 1)))
                cx._hazard_update(cx.s["tensor"], cx.s["tensor"].n, [bol, buf("wo_b")], [bps])
                cx.step("scalar", lambda e_, jc=jc, ps=ps: e_.activation(
                    a0_fm[:, jc * G:(jc + 1) * G], ps[:, :], AF.Identity, bias=bout_cols[:, jc:jc + 1]),
                    reads=[bps, buf("bout_cols")], writes=[ba0])

            ln_stats(lambda c: a0_fm[:, c * G:(c + 1) * G], ba0)
            for c in range(DC):
                sl = a0_fm[:, c * G:(c + 1) * G]
                cx.step("vector", lambda e_, sl=sl: e_.tensor_tensor(sl, sl, mb_sb[:, :], ALU.subtract),
                        reads=[bmb], writes=[ba0])
                cx.step("vector", lambda e_, sl=sl: e_.tensor_tensor(sl, sl, rb_sb[:, :], ALU.mult),
                        reads=[brbs], writes=[ba0])
                cx.step("vector", lambda e_, sl=sl, c=c: e_.tensor_scalar(
                    sl, sl, gamma_cols[:, c:c + 1], beta_cols[:, c:c + 1], ALU.mult, ALU.add),
                    reads=[buf("gamma_cols"), buf("beta_cols")], writes=[ba0])
                cx.step("vector", lambda e_, sl=sl, c=c: e_.tensor_add(
                    x_g[:, c * G:(c + 1) * G], x_g[:, c * G:(c + 1) * G], sl),
                    reads=[ba0], writes=[bx])
                cx.dma("sync", cx.ap["acc_T"][c * 128:(c + 1) * 128, g * G:(g + 1) * G],
                       x_g[:, c * G:(c + 1) * G], reads=[bx], sem=cx.s["out"])

        cx.finish(block)
    return nc


L0A_INPUTS = ["xT", "ones", "I128", "U128", "onesb", "I128b", "WqkvT", "bq_cols",
              "WoutT", "bout_cols", "gamma_cols", "beta_cols", "rwT"]


def prep_l0a_inputs(x_core, W, C):
    m = {}
    m["xT"] = np.ascontiguousarray(x_core.T)
    for n in ("ones", "I128", "U128", "onesb", "I128b"):
        m[n] = C[n]
    m["WqkvT"] = np.ascontiguousarray(W["L0_attn_Wqkv"].T)
    m["bq_cols"] = colmat(W["L0_attn_bqkv"], 24)
    m["WoutT"] = np.ascontiguousarray(W["L0_attn_Wout"].T)
    m["bout_cols"] = colmat(W["L0_attn_bout"], 8)
    m["gamma_cols"] = colmat(W["L0_attn_gamma"], 8)
    m["beta_cols"] = colmat(W["L0_attn_beta"], 8)
    m["rwT"] = np.ascontiguousarray(W["L0_ffn_router_w"].T)
    return m





def build_l0f():
    nc = bass.Bass("TRN2", target_bir_lowering=False)
    with ExitStack() as st:
        cx = Ctx(nc, st)
        e = st.enter_context
        inp, outp = cx.inp, cx.outp
        inp("xn_T", (D, T)); inp("acc_T", (D, T)); inp("cwT", (8, T))
        inp("w1", (8, D, 4 * D)); inp("w2", (8, 4 * D, D))
        inp("sel8", (8, 8 * 128))
        outp("x1_T", (D, T))

        sb = {}

        def tile(name, shape, dt=F32):
            sb[name] = e(nc.sbuf_tensor("sb_" + name, list(shape), dt))
            return sb[name]

        xnb = tile("xnb", [128, DC * T], BF16)
        acc = tile("acc", [128, DC * T])
        h_sb = tile("h_sb", [128, 32 * 1024], BF16)
        wbuf = tile("wbuf", [128, 8 * 1024], BF16)
        cw_sb = tile("cw_sb", [8, T])
        sel8 = tile("sel8", [8, 8 * 128])
        cwb_sb = tile("cwb_sb", [128, 2 * G])
        ytmp = tile("ytmp", [128, G])

        hp = [e(nc.psum_tensor("hp0", [128, G], F32)), e(nc.psum_tensor("hp1", [128, G], F32))]
        yp = [e(nc.psum_tensor("yp0", [128, G], F32)), e(nc.psum_tensor("yp1", [128, G], F32))]
        cp = e(nc.psum_tensor("cp", [128, G], F32))

        B = {}

        def buf(name):
            if name not in B:
                B[name] = Buf(cx)
            return B[name]

        block = e(nc.Block())

        bxnb, bacc, bh, bw = buf("xnb"), buf("acc"), buf("h_sb"), buf("wbuf")
        bcws, bsel, bcwb, byt = buf("cw_sb"), buf("sel8"), buf("cwb_sb"), buf("ytmp")
        bhp = [buf("hp0"), buf("hp1")]
        byp = [buf("yp0"), buf("yp1")]
        bcp = buf("cp")

        cx.dma("gpsimd", fm3(xnb, T), fm(cx.ap["xn_T"]), writes=[bxnb], sem=cx.s["gpsimd"])
        cx.dma("sync", fm3(acc, T), fm(cx.ap["acc_T"]), writes=[bacc], sem=cx.s["sync"])
        cx.dma("sync", cw_sb[:, :], cx.ap["cwT"], writes=[bcws], sem=cx.s["sync"])
        cx.dma("sync", sel8[:, :], cx.ap["sel8"], writes=[bsel], sem=cx.s["sync"])

        NTOK = 1024

        for ex in range(8):
            for nt in range(2):
                for fb in range(4):
                    cx.dma("gpsimd",
                           wbuf[:, :].rearrange("p (c f) -> p c f", f=1024),
                           cx.ap["w1"][ex, :, fb * 1024:(fb + 1) * 1024]
                           .rearrange("(c p) f -> p c f", p=128),
                           writes=[bw], sem=cx.s["gpsimd"])
                    for fc in range(8):
                        fcg = fb * 8 + fc
                        for ns in range(2):
                            ps, bps = hp[(fcg * 2 + ns) % 2], bhp[(fcg * 2 + ns) % 2]
                            cx._hazard_waits("tensor", [bw, bxnb], [bps])
                            for c in range(DC):
                                cx.step("tensor", lambda e_, c=c, fc=fc, nt=nt, ns=ns, ps=ps: e_.matmul(
                                    ps[:, :],
                                    wbuf[:, c * 1024 + fc * 128: c * 1024 + (fc + 1) * 128],
                                    xnb[:, c * T + nt * NTOK + ns * G: c * T + nt * NTOK + (ns + 1) * G],
                                    start=(c == 0), stop=(c == DC - 1)))
                            cx._hazard_update(cx.s["tensor"], cx.s["tensor"].n, [bw, bxnb], [bps])
                            cx.step("scalar", lambda e_, fcg=fcg, ns=ns, ps=ps: e_.activation(
                                h_sb[:, fcg * 1024 + ns * G: fcg * 1024 + (ns + 1) * G],
                                ps[:, :], AF.Gelu),
                                reads=[bps], writes=[bh])
                for ns in range(2):
                    cx.step("tensor", lambda e_, ex=ex, nt=nt, ns=ns: e_.matmul(
                        cp[:, :], sel8[:, ex * 128:(ex + 1) * 128],
                        cw_sb[0:8, nt * NTOK + ns * G: nt * NTOK + (ns + 1) * G],
                        start=True, stop=True),
                        reads=[bsel, bcws], writes=[bcp])
                    cx.step("vector", lambda e_, ns=ns: e_.tensor_copy(
                        cwb_sb[:, ns * G:(ns + 1) * G], cp[:, :]),
                        reads=[bcp], writes=[bcwb])
                for jb in range(4):
                    cx.dma("gpsimd",
                           wbuf[:, :].rearrange("p (c f) -> p c f", f=256),
                           cx.ap["w2"][ex, :, jb * 256:(jb + 1) * 256]
                           .rearrange("(c p) j -> p c j", p=128),
                           writes=[bw], sem=cx.s["gpsimd"])
                    for ns in range(2):
                        for jcr in range(2):
                            jc = jb * 2 + jcr
                            ps, bps = yp[jcr], byp[jcr]
                            cx._hazard_waits("tensor", [bw, bh], [bps])
                            for fc in range(32):
                                cx.step("tensor", lambda e_, fc=fc, jcr=jcr, ns=ns, ps=ps: e_.matmul(
                                    ps[:, :],
                                    wbuf[:, fc * 256 + jcr * 128: fc * 256 + (jcr + 1) * 128],
                                    h_sb[:, fc * 1024 + ns * G: fc * 1024 + (ns + 1) * G],
                                    start=(fc == 0), stop=(fc == 31)))
                            cx._hazard_update(cx.s["tensor"], cx.s["tensor"].n, [bw, bh], [bps])
                            cx.step("vector", lambda e_, ns=ns, ps=ps: e_.tensor_tensor(
                                ytmp[:, :], ps[:, :], cwb_sb[:, ns * G:(ns + 1) * G], ALU.mult),
                                reads=[bps, bcwb], writes=[byt])
                            asl = acc[:, jc * T + nt * NTOK + ns * G: jc * T + nt * NTOK + (ns + 1) * G]
                            cx.step("vector", lambda e_, asl=asl: e_.tensor_add(asl, asl, ytmp[:, :]),
                                    reads=[byt], writes=[bacc])

        for c in range(DC):
            cx.dma("sync", cx.ap["x1_T"][c * 128:(c + 1) * 128, :], acc[:, c * T:(c + 1) * T],
                   reads=[bacc], sem=cx.s["out"])
        cx.finish(block)
    return nc


def prep_l0f_inputs(xn_T, acc_T, cwT, W, C):
    return {"xn_T": xn_T, "acc_T": acc_T, "cwT": cwT,
            "w1": W["L0_ffn_w1"], "w2": W["L0_ffn_w2"], "sel8": C["sel8"]}





def build_l1a():
    nc = bass.Bass("TRN2", target_bir_lowering=False)
    with ExitStack() as st:
        cx = Ctx(nc, st)
        e = st.enter_context
        inp, outp = cx.inp, cx.outp
        inp("x1_T", (D, T))
        inp("WqT", (D, D)); inp("WkT", (D, D)); inp("WvT", (D, D)); inp("WoT", (D, D))
        inp("lng_cols", (128, 8)); inp("lnb_cols", (128, 8))
        inp("hs_row", (1, 8)); inp("hd3", (3, 8))
        inp("cos_h", (128, T)); inp("sin_h", (128, T))
        inp("ones", (128, 128)); inp("onesb", (128, 128), BF16)
        inp("I128", (128, 128)); inp("maskT3", (128, 384)); inp("sel8", (8, 8 * 128))
        outp("xn2_T", (D, T)); outp("acc1_T", (D, T))

        sb = {}

        def tile(name, shape, dt=F32):
            sb[name] = e(nc.sbuf_tensor("sb_" + name, list(shape), dt))
            return sb[name]

        xn2ab = tile("xn2ab", [128, DC * T], BF16)
        out_all = tile("out_all", [128, DC * T], BF16)
        cos_h = tile("cos_h", [128, T]); sin_h = tile("sin_h", [128, T])
        x1g = tile("x1g", [128, DC * G])
        xn2g = tile("xn2g", [128, DC * G])
        qr = tile("qr", [128, G]); t1 = tile("t1", [128, G])
        q_hd = tile("q_hd", [128, T], BF16); k_hd = tile("k_hd", [128, T], BF16)
        v_tm = tile("v_tm", [128, 16 * 128], BF16)
        PT = tile("PT", [128, 16 * 256], BF16)
        PgT = tile("PgT", [128, 16 * 256], BF16)
        om = tile("om", [128, 256]); og = tile("og", [128, 256])
        wqh = tile("wqh", [128, DC * 128], BF16); wkh = tile("wkh", [128, DC * 128], BF16)
        wvh = tile("wvh", [128, DC * 128], BF16)
        wo_b = tile("wo_b", [128, DC * 128], BF16)
        x1t = tile("x1t", [128, G])
        ones = tile("ones", [128, 128]); onesb = tile("onesb", [128, 128], BF16)
        I128 = tile("I128", [128, 128]); maskT3 = tile("maskT3", [128, 384])
        sel8 = tile("sel8", [8, 8 * 128])
        lng_cols = tile("lng_cols", [128, 8]); lnb_cols = tile("lnb_cols", [128, 8])
        hs_row = tile("hs_row", [1, 8]); hd3b = tile("hd3b", [3, 8], BF16)
        hd3f = tile("hd3f", [3, 8])
        sq_sb = tile("sq_sb", [128, G])
        row_a = tile("row_a", [1, G]); row_b = tile("row_b", [1, G]); row_c = tile("row_c", [1, G])
        mb_sb = tile("mb_sb", [128, G]); rb_sb = tile("rb_sb", [128, G])
        nm_all = tile("nm_all", [128, 16]); nm_grp = tile("nm_grp", [128, 8])
        mT_sb = tile("mT_sb", [8, 128]); mTr = tile("mTr", [8, 1])
        negm_cols = tile("negm_cols", [128, 8])
        qp_row = tile("qp_row", [1, T]); kp_row = tile("kp_row", [1, T])
        gq_mx = tile("gq_mx", [1, 8]); gq_mn = tile("gq_mn", [1, 8])
        gk_mx = tile("gk_mx", [1, 8]); gk_mn = tile("gk_mn", [1, 8])
        mg_row = tile("mg_row", [1, 8]); mg4 = tile("mg4", [1, 8])
        mg_col = tile("mg_col", [8, 1]); negmg_cols = tile("negmg_cols", [128, 8])
        Zrow = tile("Zrow", [1, 256]); rZrow = tile("rZrow", [1, 256])
        Zgrow = tile("Zgrow", [1, 256]); rZgrow = tile("rZgrow", [1, 256])
        rZb = tile("rZb", [128, 256]); rZgb = tile("rZgb", [128, 256])
        hs_col = tile("hs_col", [128, 1])

        pS1 = e(nc.psum_tensor("pS1", [128, G], F32))
        pS2 = e(nc.psum_tensor("pS2", [128, G], F32))
        pT = e(nc.psum_tensor("pT", [128, 256], F32))
        pT2 = e(nc.psum_tensor("pT2", [128, 256], F32))
        pZ = e(nc.psum_tensor("pZ", [1, 256], F32))
        pZg = e(nc.psum_tensor("pZg", [1, 256], F32))
        pAV = e(nc.psum_tensor("pAV", [128, 256], F32))
        pAVg = e(nc.psum_tensor("pAVg", [128, 256], F32))

        B = {}

        def buf(name):
            if name not in B:
                B[name] = Buf(cx)
            return B[name]

        block = e(nc.Block())

        for n in ("cos_h", "sin_h", "ones", "I128", "maskT3", "sel8", "lng_cols",
                  "lnb_cols", "hs_row", "hd3f"):
            src_name = "hd3" if n == "hd3f" else n
            cx.dma("sync", sb[n][:, :], cx.ap[src_name], writes=[buf(n)], sem=cx.s["sync"])
        cx.dma("gpsimd", onesb[:, :], cx.ap["onesb"], writes=[buf("onesb")], sem=cx.s["gpsimd"])
        cx.dma("gpsimd", hd3b[:, :], cx.ap["hd3"], writes=[buf("hd3b")], sem=cx.s["gpsimd"])

        bx1, bxn2g, bxab = buf("x1g"), buf("xn2g"), buf("xn2ab")
        boa = buf("out_all")
        bqr, bt1, bqh, bkh, bvtm = buf("qr"), buf("t1"), buf("q_hd"), buf("k_hd"), buf("v_tm")
        bPT, bPgT, bom, bog = buf("PT"), buf("PgT"), buf("om"), buf("og")
        bwq, bwk, bwv, bwo = buf("wqh"), buf("wkh"), buf("wvh"), buf("wo_b")
        bx1t = buf("x1t")
        bsq = buf("sq_sb")
        bra, brb, brc = buf("row_a"), buf("row_b"), buf("row_c")
        bmb, brbs = buf("mb_sb"), buf("rb_sb")
        bnm, bnmg = buf("nm_all"), buf("nm_grp")
        bmT, bmTr, bnmc = buf("mT_sb"), buf("mTr"), buf("negm_cols")
        bqp, bkp = buf("qp_row"), buf("kp_row")
        bgq_mx, bgq_mn = buf("gq_mx"), buf("gq_mn")
        bgk_mx, bgk_mn = buf("gk_mx"), buf("gk_mn")
        bmgr, bmg4, bmgc, bnmgc = buf("mg_row"), buf("mg4"), buf("mg_col"), buf("negmg_cols")
        bZ, brZ, bZg, brZg = buf("Zrow"), buf("rZrow"), buf("Zgrow"), buf("rZgrow")
        bZb, bZgb = buf("rZb"), buf("rZgb")
        bhsc = buf("hs_col")
        bpS1, bpS2 = buf("pS1"), buf("pS2")
        bpT, bpT2 = buf("pT"), buf("pT2")
        bpZ, bpZg = buf("pZ"), buf("pZg")
        bpAV, bpAVg = buf("pAV"), buf("pAVg")

        def ln_stats(src_fn, src_buf, p1, p2, bp1, bp2):
            cx._hazard_waits("tensor", [src_buf], [bp1, bp2])
            for c in range(DC):
                src = src_fn(c)
                cx.step("scalar", lambda e_, src=src: e_.activation(sq_sb[:, :], src, AF.Square),
                        reads=[src_buf], writes=[bsq])
                cx.step("tensor", lambda e_, src=src, c=c: e_.matmul(
                    p1[0:1, 0:G], ones[:, 0:1], src, start=(c == 0), stop=(c == DC - 1)))
                cx.step("tensor", lambda e_, c=c: e_.matmul(
                    p2[0:1, 0:G], ones[:, 0:1], sq_sb[:, :], start=(c == 0), stop=(c == DC - 1)),
                    reads=[bsq])
            cx._hazard_update(cx.s["tensor"], cx.s["tensor"].n, [src_buf], [bp1, bp2])
            cx.step("vector", lambda e_: e_.tensor_scalar_mul(row_a[0:1, :], p1[0:1, 0:G], 1.0 / D),
                    reads=[bp1], writes=[bra])
            cx.step("vector", lambda e_: e_.tensor_scalar_mul(row_b[0:1, :], p2[0:1, 0:G], 1.0 / D),
                    reads=[bp2], writes=[brb])
            cx.step("vector", lambda e_: e_.tensor_mul(row_c[0:1, :], row_a[0:1, :], row_a[0:1, :]),
                    reads=[bra], writes=[brc])
            cx.step("vector", lambda e_: e_.tensor_tensor(row_b[0:1, :], row_b[0:1, :], row_c[0:1, :], ALU.subtract),
                    reads=[brc], writes=[brb])
            cx.step("vector", lambda e_: e_.tensor_scalar_add(row_b[0:1, :], row_b[0:1, :], 1e-5),
                    writes=[brb])
            cx.step("scalar", lambda e_: e_.activation(row_c[0:1, :], row_b[0:1, :], AF.Ln),
                    reads=[brb], writes=[brc])
            cx.step("scalar", lambda e_: e_.activation(row_b[0:1, :], row_c[0:1, :], AF.Exp, scale=-0.5),
                    reads=[brc], writes=[brb])
            cx.step("tensor", lambda e_, p1=p1: e_.matmul(p1[:, 0:G], ones[0:1, :], row_a[0:1, :], start=True, stop=True),
                    reads=[bra], writes=[bp1])
            cx.step("vector", lambda e_, p1=p1: e_.tensor_copy(mb_sb[:, :], p1[:, 0:G]),
                    reads=[bp1], writes=[bmb])
            cx.step("tensor", lambda e_, p1=p1: e_.matmul(p1[:, 0:G], ones[0:1, :], row_b[0:1, :], start=True, stop=True),
                    reads=[brb], writes=[bp1])
            cx.step("vector", lambda e_, p1=p1: e_.tensor_copy(rb_sb[:, :], p1[:, 0:G]),
                    reads=[bp1], writes=[brbs])

        for g in range(NT):
            cx.dma("sync", fm3(x1g, G), fm(cx.ap["x1_T"][:, g * G:(g + 1) * G]),
                   writes=[bx1], sem=cx.s["sync"])
            ln_stats(lambda c: x1g[:, c * G:(c + 1) * G], bx1, pS1, pS2, bpS1, bpS2)
            for c in range(DC):
                xsl = xn2g[:, c * G:(c + 1) * G]
                cx.step("vector", lambda e_, c=c, xsl=xsl: e_.tensor_tensor(
                    xsl, x1g[:, c * G:(c + 1) * G], mb_sb[:, :], ALU.subtract),
                    reads=[bx1, bmb], writes=[bxn2g])
                cx.step("vector", lambda e_, xsl=xsl: e_.tensor_tensor(xsl, xsl, rb_sb[:, :], ALU.mult),
                        reads=[brbs], writes=[bxn2g])
                cx.dma("sync", cx.ap["xn2_T"][c * 128:(c + 1) * 128, g * G:(g + 1) * G], xsl,
                       reads=[bxn2g], sem=cx.s["out"])
                cx.step("vector", lambda e_, c=c, xsl=xsl: e_.tensor_add(
                    x1g[:, c * G:(c + 1) * G], x1g[:, c * G:(c + 1) * G], xsl),
                    reads=[bxn2g], writes=[bx1])
                cx.dma("sync", cx.ap["acc1_T"][c * 128:(c + 1) * 128, g * G:(g + 1) * G],
                       x1g[:, c * G:(c + 1) * G], reads=[bx1], sem=cx.s["out"])
            ln_stats(lambda c: xn2g[:, c * G:(c + 1) * G], bxn2g, pS1, pS2, bpS1, bpS2)
            for c in range(DC):
                xsl = xn2g[:, c * G:(c + 1) * G]
                cx.step("vector", lambda e_, xsl=xsl: e_.tensor_tensor(xsl, xsl, mb_sb[:, :], ALU.subtract),
                        reads=[bmb], writes=[bxn2g])
                cx.step("vector", lambda e_, xsl=xsl: e_.tensor_tensor(xsl, xsl, rb_sb[:, :], ALU.mult),
                        reads=[brbs], writes=[bxn2g])
                cx.step("vector", lambda e_, c=c, g=g, xsl=xsl: e_.tensor_scalar(
                    xn2ab[:, c * T + g * G: c * T + (g + 1) * G], xsl,
                    lng_cols[:, c:c + 1], lnb_cols[:, c:c + 1], ALU.mult, ALU.add),
                    reads=[bxn2g, buf("lng_cols"), buf("lnb_cols")], writes=[bxab])


        SC = 0.08838834764831845
        for h in range(8):
            cx.dma("gpsimd", fm3(wqh, 128), fm(cx.ap["WqT"][:, h * 128:(h + 1) * 128]),
                   writes=[bwq], sem=cx.s["gpsimd"])
            cx.dma("gpsimd", fm3(wkh, 128), fm(cx.ap["WkT"][:, h * 128:(h + 1) * 128]),
                   writes=[bwk], sem=cx.s["gpsimd"])
            cx.dma("gpsimd", fm3(wvh, 128), fm(cx.ap["WvT"][:, h * 128:(h + 1) * 128]),
                   writes=[bwv], sem=cx.s["gpsimd"])
            for which, wb, bwb, dst, bdst, scl in (
                    ("q", wqh, bwq, q_hd, bqh, SC), ("k", wkh, bwk, k_hd, bkh, 1.0)):
                for g in range(NT):
                    ps, bps = (pS1, bpS1) if g % 2 == 0 else (pS2, bpS2)
                    cx._hazard_waits("tensor", [bwb, bxab], [bps])
                    for c in range(DC):
                        cx.step("tensor", lambda e_, c=c, g=g, wb=wb, ps=ps: e_.matmul(
                            ps[:, :], wb[:, c * 128:(c + 1) * 128],
                            xn2ab[:, c * T + g * G: c * T + (g + 1) * G],
                            start=(c == 0), stop=(c == DC - 1)))
                    cx._hazard_update(cx.s["tensor"], cx.s["tensor"].n, [bwb, bxab], [bps])
                    if scl == 1.0:
                        cx.step("scalar", lambda e_, g=g, ps=ps: e_.activation(
                            qr[:, :], ps[:, :], AF.Identity),
                            reads=[bps], writes=[bqr])
                    else:
                        cx.step("scalar", lambda e_, g=g, ps=ps, scl=scl: e_.activation(
                            qr[:, :], ps[:, :], AF.Identity, scale=scl),
                            reads=[bps], writes=[bqr])
                    ch = cos_h[:, g * G:(g + 1) * G]
                    sh_lo = sin_h[0:64, g * G:(g + 1) * G]
                    sh_hi = sin_h[64:128, g * G:(g + 1) * G]
                    cx.step("vector", lambda e_, sh_hi=sh_hi: e_.tensor_mul(
                        t1[0:64, :], qr[64:128, :], sh_hi),
                        reads=[bqr, buf("sin_h")], writes=[bt1])
                    cx.step("vector", lambda e_, sh_lo=sh_lo: e_.tensor_mul(
                        t1[64:128, :], qr[0:64, :], sh_lo),
                        reads=[bqr, buf("sin_h")], writes=[bt1])
                    cx.step("vector", lambda e_, ch=ch: e_.tensor_mul(qr[:, :], qr[:, :], ch),
                            reads=[buf("cos_h")], writes=[bqr])
                    cx.step("vector", lambda e_, dst=dst, g=g: e_.tensor_tensor(
                        dst[0:64, g * G:(g + 1) * G], qr[0:64, :], t1[0:64, :], ALU.subtract),
                        reads=[bqr, bt1], writes=[bdst])
                    cx.step("vector", lambda e_, dst=dst, g=g: e_.tensor_add(
                        dst[64:128, g * G:(g + 1) * G], qr[64:128, :], t1[64:128, :]),
                        reads=[bqr, bt1], writes=[bdst])
            for tt in range(16):
                ps, bps = (pS1, bpS1) if tt % 2 == 0 else (pS2, bpS2)
                cx._hazard_waits("tensor", [bwv, bxab], [bps])
                for c in range(DC):
                    cx.step("tensor", lambda e_, c=c, tt=tt, ps=ps: e_.matmul(
                        ps[:, 0:128], xn2ab[:, c * T + tt * 128: c * T + (tt + 1) * 128],
                        wvh[:, c * 128:(c + 1) * 128], start=(c == 0), stop=(c == DC - 1)))
                cx._hazard_update(cx.s["tensor"], cx.s["tensor"].n, [bwv, bxab], [bps])
                cx.step("vector", lambda e_, tt=tt, ps=ps: e_.tensor_copy(
                    v_tm[:, tt * 128:(tt + 1) * 128], ps[:, 0:128]),
                    reads=[bps], writes=[bvtm])
            for src_hd, bsrc, dstrow, bdstrow, scl in (
                    (q_hd, bqh, qp_row, bqp, 11.313708498984761),
                    (k_hd, bkh, kp_row, bkp, None)):
                for g in range(NT):
                    ps, bps = (pS1, bpS1) if g % 2 == 0 else (pS2, bpS2)
                    cx.step("tensor", lambda e_, h=h, g=g, src_hd=src_hd, ps=ps: e_.matmul(
                        ps[0:1, 0:G], hd3b[0:3, h:h + 1], src_hd[0:3, g * G:(g + 1) * G],
                        start=True, stop=True),
                        reads=[buf("hd3b"), bsrc], writes=[bps])
                    if scl is not None:
                        cx.step("scalar", lambda e_, g=g, ps=ps, dstrow=dstrow, scl=scl: e_.activation(
                            dstrow[0:1, g * G:(g + 1) * G], ps[0:1, 0:G], AF.Identity, scale=scl),
                            reads=[bps], writes=[bdstrow])
                    else:
                        cx.step("scalar", lambda e_, g=g, ps=ps, dstrow=dstrow: e_.activation(
                            dstrow[0:1, g * G:(g + 1) * G], ps[0:1, 0:G], AF.Identity),
                            reads=[bps], writes=[bdstrow])
            for it in range(16):
                njb = it // 4 + 1
                for jb in range(njb):
                    ps, bps = (pS1, bpS1) if jb % 2 == 0 else (pS2, bpS2)
                    cx.step("tensor", lambda e_, it=it, jb=jb, ps=ps: e_.matmul(
                        ps[:, :], q_hd[:, it * 128:(it + 1) * 128],
                        k_hd[:, jb * G:(jb + 1) * G], start=True, stop=True),
                        reads=[bqh, bkh], writes=[bps])
                    if jb == 0:
                        cx.step("vector", lambda e_, it=it, ps=ps: e_.reduce_max(
                            nm_all[:, it:it + 1], ps[:, :], axis=AX.X, negate=True),
                            reads=[bps], writes=[bnm])
                    else:
                        cx.step("vector", lambda e_, ps=ps: e_.reduce_max(
                            row_a[0:1, 0:1] if False else sq_sb[:, 0:1], ps[:, :], axis=AX.X, negate=True),
                            reads=[bps], writes=[bsq])
                        cx.step("vector", lambda e_, it=it: e_.tensor_tensor(
                            nm_all[:, it:it + 1], nm_all[:, it:it + 1], sq_sb[:, 0:1], ALU.min),
                            reads=[bsq], writes=[bnm])
            cx.step("vector", lambda e_: e_.tensor_tensor(
                nm_grp[:, :], nm_all[:, 0:16:2], nm_all[:, 1:16:2], ALU.min),
                reads=[bnm], writes=[bnmg])
            cx.step("tensor", lambda e_: e_.matmul(pS1[0:8, 0:128], nm_grp[:, :], I128[:, :],
                                                   start=True, stop=True),
                    reads=[bnmg, buf("I128")], writes=[bpS1])
            cx.step("vector", lambda e_: e_.tensor_copy(mT_sb[:, :], pS1[0:8, 0:128]),
                    reads=[bpS1], writes=[bmT])
            cx.step("vector", lambda e_: e_.tensor_reduce(mTr[:, 0:1], mT_sb[:, :], AX.X, ALU.min),
                    reads=[bmT], writes=[bmTr])
            for gp in range(8):
                cx.step("tensor", lambda e_, gp=gp: e_.matmul(
                    pS2[:, 0:1], sel8[:, gp * 128:(gp + 1) * 128], mTr[:, 0:1],
                    start=True, stop=True),
                    reads=[buf("sel8"), bmTr], writes=[bpS2])
                cx.step("vector", lambda e_, gp=gp: e_.tensor_copy(
                    negm_cols[:, gp:gp + 1], pS2[:, 0:1]),
                    reads=[bpS2], writes=[bnmc])
            for gp in range(8):
                cx.step("vector", lambda e_, gp=gp: e_.reduce_max(
                    gq_mx[0:1, gp:gp + 1], qp_row[0:1, gp * 256:(gp + 1) * 256], axis=AX.X),
                    reads=[bqp], writes=[bgq_mx])
                cx.step("vector", lambda e_, gp=gp: e_.tensor_reduce(
                    gq_mn[0:1, gp:gp + 1], qp_row[0:1, gp * 256:(gp + 1) * 256], AX.X, ALU.min),
                    reads=[bqp], writes=[bgq_mn])
                cx.step("vector", lambda e_, gp=gp: e_.reduce_max(
                    gk_mx[0:1, gp:gp + 1], kp_row[0:1, gp * 256:(gp + 1) * 256], axis=AX.X),
                    reads=[bkp], writes=[bgk_mx])
                cx.step("vector", lambda e_, gp=gp: e_.tensor_reduce(
                    gk_mn[0:1, gp:gp + 1], kp_row[0:1, gp * 256:(gp + 1) * 256], AX.X, ALU.min),
                    reads=[bkp], writes=[bgk_mn])
            for shift in (1, 2, 4):
                cx.step("vector", lambda e_, shift=shift: e_.tensor_tensor(
                    gk_mx[0:1, shift:8], gk_mx[0:1, shift:8], gk_mx[0:1, 0:8 - shift], ALU.max),
                    writes=[bgk_mx])
                cx.step("vector", lambda e_, shift=shift: e_.tensor_tensor(
                    gk_mn[0:1, shift:8], gk_mn[0:1, shift:8], gk_mn[0:1, 0:8 - shift], ALU.min),
                    writes=[bgk_mn])
            cx.step("vector", lambda e_: e_.tensor_mul(mg_row[0:1, :], gq_mx[0:1, :], gk_mx[0:1, :]),
                    reads=[bgq_mx, bgk_mx], writes=[bmgr])
            for a_t, a_b, b_t, b_b in ((gq_mx, bgq_mx, gk_mn, bgk_mn),
                                       (gq_mn, bgq_mn, gk_mx, bgk_mx),
                                       (gq_mn, bgq_mn, gk_mn, bgk_mn)):
                cx.step("vector", lambda e_, a_t=a_t, b_t=b_t: e_.tensor_mul(
                    mg4[0:1, :], a_t[0:1, :], b_t[0:1, :]),
                    reads=[a_b, b_b], writes=[bmg4])
                cx.step("vector", lambda e_: e_.tensor_tensor(
                    mg_row[0:1, :], mg_row[0:1, :], mg4[0:1, :], ALU.max),
                    reads=[bmg4], writes=[bmgr])
            cx.step("vector", lambda e_: e_.tensor_scalar_mul(mg_row[0:1, :], mg_row[0:1, :], -1.0),
                    writes=[bmgr])
            cx.step("tensor", lambda e_: e_.matmul(pS1[0:8, 0:1], mg_row[0:1, :], ones[0:1, 0:1],
                                                   start=True, stop=True),
                    reads=[bmgr, buf("ones")], writes=[bpS1])
            cx.step("vector", lambda e_: e_.tensor_copy(mg_col[:, 0:1], pS1[0:8, 0:1]),
                    reads=[bpS1], writes=[bmgc])
            for gp in range(8):
                cx.step("tensor", lambda e_, gp=gp: e_.matmul(
                    pS2[:, 0:1], sel8[:, gp * 128:(gp + 1) * 128], mg_col[:, 0:1],
                    start=True, stop=True),
                    reads=[buf("sel8"), bmgc], writes=[bpS2])
                cx.step("vector", lambda e_, gp=gp: e_.tensor_copy(
                    negmg_cols[:, gp:gp + 1], pS2[:, 0:1]),
                    reads=[bpS2], writes=[bnmgc])
            cx.step("tensor", lambda e_, h=h: e_.matmul(pS1[:, 0:1], ones[0:1, :], hs_row[0:1, h:h + 1],
                                                        start=True, stop=True),
                    reads=[buf("hs_row"), buf("ones")], writes=[bpS1])
            cx.step("vector", lambda e_: e_.tensor_copy(hs_col[:, 0:1], pS1[:, 0:1]),
                    reads=[bpS1], writes=[bhsc])
            for gp in range(8):
                njt = gp * 2 + 2
                for jt in range(njt):
                    cx.step("tensor", lambda e_, jt=jt, gp=gp: e_.matmul(
                        pT[:, :], k_hd[:, jt * 128:(jt + 1) * 128],
                        q_hd[:, gp * 256:(gp + 1) * 256], start=True, stop=True),
                        reads=[bkh, bqh], writes=[bpT])
                    if jt >= njt - 2:
                        off = 128 if jt == njt - 2 else 0
                        cx.step("vector", lambda e_, off=off: e_.tensor_add(
                            pT[:, :], pT[:, :], maskT3[:, off:off + 256]),
                            reads=[buf("maskT3")], writes=[bpT])
                    cx.step("scalar", lambda e_, jt=jt, gp=gp: e_.activation(
                        PT[:, jt * 256:(jt + 1) * 256], pT[:, :], AF.Exp,
                        bias=negm_cols[:, gp:gp + 1]),
                        reads=[bpT, bnmc], writes=[bPT])
                    cx.step("tensor", lambda e_, jt=jt, gp=gp: e_.matmul(
                        pT2[:, :], kp_row[0:1, jt * 128:(jt + 1) * 128],
                        qp_row[0:1, gp * 256:(gp + 1) * 256], start=True, stop=True),
                        reads=[bqp, bkp], writes=[bpT2])
                    if jt >= njt - 2:
                        off = 128 if jt == njt - 2 else 0
                        cx.step("vector", lambda e_, off=off: e_.tensor_add(
                            pT2[:, :], pT2[:, :], maskT3[:, off:off + 256]),
                            reads=[buf("maskT3")], writes=[bpT2])
                    cx.step("scalar", lambda e_, jt=jt, gp=gp: e_.activation(
                        PgT[:, jt * 256:(jt + 1) * 256], pT2[:, :], AF.Exp,
                        bias=negmg_cols[:, gp:gp + 1]),
                        reads=[bpT2, bnmgc], writes=[bPgT])
                cx._hazard_waits("tensor", [bPT, buf("onesb")], [bpZ])
                for jt in range(njt):
                    cx.step("tensor", lambda e_, jt=jt: e_.matmul(
                        pZ[0:1, :], onesb[:, 0:1], PT[:, jt * 256:(jt + 1) * 256],
                        start=(jt == 0), stop=(jt == njt - 1)))
                cx._hazard_update(cx.s["tensor"], cx.s["tensor"].n, [bPT, buf("onesb")], [bpZ])
                cx._hazard_waits("tensor", [bPgT, buf("onesb")], [bpZg])
                for jt in range(njt):
                    cx.step("tensor", lambda e_, jt=jt: e_.matmul(
                        pZg[0:1, :], onesb[:, 0:1], PgT[:, jt * 256:(jt + 1) * 256],
                        start=(jt == 0), stop=(jt == njt - 1)))
                cx._hazard_update(cx.s["tensor"], cx.s["tensor"].n, [bPgT, buf("onesb")], [bpZg])
                cx._hazard_waits("tensor", [bPT, bvtm], [bpAV])
                for jt in range(njt):
                    cx.step("tensor", lambda e_, jt=jt: e_.matmul(
                        pAV[:, :], v_tm[:, jt * 128:(jt + 1) * 128], PT[:, jt * 256:(jt + 1) * 256],
                        start=(jt == 0), stop=(jt == njt - 1)))
                cx._hazard_update(cx.s["tensor"], cx.s["tensor"].n, [bPT, bvtm], [bpAV])
                cx._hazard_waits("tensor", [bPgT, bvtm], [bpAVg])
                for jt in range(njt):
                    cx.step("tensor", lambda e_, jt=jt: e_.matmul(
                        pAVg[:, :], v_tm[:, jt * 128:(jt + 1) * 128], PgT[:, jt * 256:(jt + 1) * 256],
                        start=(jt == 0), stop=(jt == njt - 1)))
                cx._hazard_update(cx.s["tensor"], cx.s["tensor"].n, [bPgT, bvtm], [bpAVg])
                cx.step("vector", lambda e_: e_.tensor_copy(Zrow[0:1, :], pZ[0:1, :]),
                        reads=[bpZ], writes=[bZ])
                cx.step("vector", lambda e_: e_.reciprocal(rZrow[0:1, :], Zrow[0:1, :]),
                        reads=[bZ], writes=[brZ])
                cx.step("vector", lambda e_: e_.tensor_copy(Zgrow[0:1, :], pZg[0:1, :]),
                        reads=[bpZg], writes=[bZg])
                cx.step("vector", lambda e_: e_.reciprocal(rZgrow[0:1, :], Zgrow[0:1, :]),
                        reads=[bZg], writes=[brZg])
                cx.step("tensor", lambda e_: e_.matmul(pT[:, :], ones[0:1, :], rZrow[0:1, :],
                                                       start=True, stop=True),
                        reads=[brZ, buf("ones")], writes=[bpT])
                cx.step("vector", lambda e_: e_.tensor_copy(rZb[:, :], pT[:, :]),
                        reads=[bpT], writes=[bZb])
                cx.step("tensor", lambda e_: e_.matmul(pT2[:, :], ones[0:1, :], rZgrow[0:1, :],
                                                       start=True, stop=True),
                        reads=[brZg, buf("ones")], writes=[bpT2])
                cx.step("vector", lambda e_: e_.tensor_copy(rZgb[:, :], pT2[:, :]),
                        reads=[bpT2], writes=[bZgb])
                cx.step("vector", lambda e_: e_.tensor_mul(om[:, :], pAV[:, :], rZb[:, :]),
                        reads=[bpAV, bZb], writes=[bom])
                cx.step("vector", lambda e_: e_.tensor_mul(og[:, :], pAVg[:, :], rZgb[:, :]),
                        reads=[bpAVg, bZgb], writes=[bog])
                cx.step("vector", lambda e_: e_.tensor_tensor(og[:, :], og[:, :], om[:, :], ALU.subtract),
                        reads=[bom], writes=[bog])
                cx.step("vector", lambda e_: e_.tensor_scalar_mul(og[:, :], og[:, :], hs_col[:, 0:1]),
                        reads=[bhsc], writes=[bog])
                cx.step("vector", lambda e_, h=h, gp=gp: e_.tensor_tensor(
                    out_all[:, h * T + gp * 256: h * T + (gp + 1) * 256], om[:, :], og[:, :], ALU.add),
                    reads=[bom, bog], writes=[boa])

        for jc in range(DC):
            cx.dma("gpsimd", fm3(wo_b, 128), fm(cx.ap["WoT"][:, jc * 128:(jc + 1) * 128]),
                   writes=[bwo], sem=cx.s["gpsimd"])
            for g in range(NT):
                ps, bps = (pS1, bpS1) if g % 2 == 0 else (pS2, bpS2)
                cx._hazard_waits("tensor", [boa, bwo], [bps])
                for c in range(DC):
                    cx.step("tensor", lambda e_, jc=jc, c=c, g=g, ps=ps: e_.matmul(
                        ps[:, :], wo_b[:, c * 128:(c + 1) * 128],
                        out_all[:, c * T + g * G: c * T + (g + 1) * G],
                        start=(c == 0), stop=(c == DC - 1)))
                cx._hazard_update(cx.s["tensor"], cx.s["tensor"].n, [boa, bwo], [bps])
                cx.dma("sync", x1t[:, :], cx.ap["acc1_T"][jc * 128:(jc + 1) * 128, g * G:(g + 1) * G],
                       writes=[bx1t], sem=cx.s["sync"])
                cx.step("vector", lambda e_, ps=ps: e_.tensor_tensor(x1t[:, :], x1t[:, :], ps[:, :], ALU.add),
                        reads=[bps], writes=[bx1t])
                cx.dma("sync", cx.ap["acc1_T"][jc * 128:(jc + 1) * 128, g * G:(g + 1) * G], x1t[:, :],
                       reads=[bx1t], sem=cx.s["out"])
        cx.finish(block)
    return nc


def prep_l1a_inputs(x1_T, W, C):
    m = {"x1_T": x1_T}
    m["WqT"] = np.ascontiguousarray(W["L1_attn_Wq"].T)
    m["WkT"] = np.ascontiguousarray(W["L1_attn_Wk"].T)
    m["WvT"] = np.ascontiguousarray(W["L1_attn_Wv"].T)
    m["WoT"] = np.ascontiguousarray(W["L1_attn_Wo"].T)
    m["lng_cols"] = colmat(W["L1_attn_ln_g"], 8)
    m["lnb_cols"] = colmat(W["L1_attn_ln_b"], 8)
    m["hs_row"] = np.ascontiguousarray(np.asarray(W["L1_attn_head_scales"], np.float32)[None, :])
    m["hd3"] = np.ascontiguousarray(np.asarray(W["L1_attn_head_dirs"], np.float32).T)
    m["cos_h"] = C["cosf"]
    m["sin_h"] = np.ascontiguousarray(np.abs(C["sinm"]))
    m["ones"] = C["ones"]; m["onesb"] = C["onesb"]; m["I128"] = C["I128"]
    m["sel8"] = C["sel8"]
    jj, ii = np.meshgrid(np.arange(128), np.arange(256), indexing="ij")
    tri = np.where(jj <= ii - 128, 0.0, BIGNEG).astype(np.float32)
    m["maskT3"] = np.concatenate([tri, np.zeros((128, 128), np.float32)], axis=1)
    return m





def build_l1g():
    nc = bass.Bass("TRN2", target_bir_lowering=False)
    with ExitStack() as st:
        cx = Ctx(nc, st)
        e = st.enter_context
        inp, outp = cx.inp, cx.outp
        inp("xn2_T", (D, T))
        inp("gwT", (D, 8)); inp("gb_row", (1, 8))
        inp("prefmask", (8, 1))
        inp("ones", (128, 128)); inp("I128", (128, 128)); inp("U128", (128, 128))
        outp("cwT1", (8, T))

        cc_in = nc.dram_tensor("cc_in", [1, 8], F32)
        cc_out = nc.dram_tensor("cc_out", [8, 8], F32)

        sb = {}

        def tile(name, shape, dt=F32):
            sb[name] = e(nc.sbuf_tensor("sb_" + name, list(shape), dt))
            return sb[name]

        xn2f = tile("xn2f", [128, DC * T])
        gw_f = tile("gw_f", [128, DC * 8])
        gb_row = tile("gb_row", [1, 8])
        prefmask = tile("prefmask", [8, 1])
        ones = tile("ones", [128, 128]); I128 = tile("I128", [128, 128])
        U128 = tile("U128", [128, 128])
        lg = tile("lg", [128, 8])
        pr = {n: tile("g_" + n, [128, w]) for n, w in
              [("m", 1), ("e", 8), ("s", 1), ("rs", 1), ("p", 8), ("p1", 1), ("oh", 8)]}
        cs_sb = tile("cs_sb", [128, 8])
        sel_sb = tile("sel_sb", [128, 8])
        oh_all = tile("oh_all", [128, 16 * 8])
        pos_all = tile("pos_all", [128, 16])
        tp_all = tile("tp_all", [128, 16])
        carry = tile("carry", [1, 8])
        ag_sb = tile("ag_sb", [8, 8])
        offs_row = tile("offs_row", [1, 8])
        offs_b = tile("offs_b", [128, 8])
        po = tile("po", [128, 1])
        posg = tile("posg", [128, 1])
        keep = tile("keep", [128, 1])
        cw1 = tile("cw1", [128, 1])
        cwf = tile("cwf", [128, 8])
        cwT_st = tile("cwT_st", [8, 128])

        pA = e(nc.psum_tensor("pA", [128, G], F32))
        pB = e(nc.psum_tensor("pB", [128, G], F32))

        B = {}

        def buf(name):
            if name not in B:
                B[name] = Buf(cx)
            return B[name]

        block = e(nc.Block())

        for n in ("gb_row", "prefmask", "ones", "I128", "U128"):
            cx.dma("sync", sb[n][:, :], cx.ap[n if n != "gb_row" else "gb_row"],
                   writes=[buf(n)], sem=cx.s["sync"])
        cx.dma("sync", fm3(xn2f, T), fm(cx.ap["xn2_T"]), writes=[buf("xn2f")], sem=cx.s["sync"])
        cx.dma("sync", fm3(gw_f, 8), fm(cx.ap["gwT"]), writes=[buf("gw_f")], sem=cx.s["sync"])

        bxn, bgw = buf("xn2f"), buf("gw_f")
        bpA, bpB = buf("pA"), buf("pB")
        blg = buf("lg")
        bpr = {n: buf("g_" + n) for n in pr}
        bcs, bsel = buf("cs_sb"), buf("sel_sb")
        boh, bpos, btp = buf("oh_all"), buf("pos_all"), buf("tp_all")
        bcarry, bag = buf("carry"), buf("ag_sb")
        boffr, boffb = buf("offs_row"), buf("offs_b")
        bpo, bposg, bkeep, bcw1, bcwf = buf("po"), buf("posg"), buf("keep"), buf("cw1"), buf("cwf")
        bcwst = buf("cwT_st")

        cx.step("vector", lambda e_: e_.memset(carry[0:1, :], 0.0), writes=[bcarry])

        for blk in range(16):
            cx._hazard_waits("tensor", [bxn, bgw, buf("gb_row"), buf("ones")], [bpA])
            for c in range(DC):
                cx.step("tensor", lambda e_, c=c, blk=blk: e_.matmul(
                    pA[:, 0:8], xn2f[:, c * T + blk * 128: c * T + (blk + 1) * 128],
                    gw_f[:, c * 8:(c + 1) * 8], start=(c == 0), stop=False))
            cx.step("tensor", lambda e_: e_.matmul(
                pA[:, 0:8], ones[0:1, :], gb_row[0:1, :], start=False, stop=True))
            cx._hazard_update(cx.s["tensor"], cx.s["tensor"].n,
                              [bxn, bgw, buf("gb_row"), buf("ones")], [bpA])
            cx.step("vector", lambda e_: e_.tensor_copy(lg[:, :], pA[:, 0:8]),
                    reads=[bpA], writes=[blg])
            cx.step("vector", lambda e_: e_.reduce_max(pr["m"][:, 0:1], lg[:, :], axis=AX.X, negate=True),
                    reads=[blg], writes=[bpr["m"]])
            cx.step("scalar", lambda e_: e_.activation(
                pr["e"][:, :], lg[:, :], AF.Exp, bias=pr["m"][:, 0:1], accum_out=pr["s"][:, 0:1]),
                reads=[blg, bpr["m"]], writes=[bpr["e"], bpr["s"]])
            cx.step("vector", lambda e_: e_.reciprocal(pr["rs"][:, 0:1], pr["s"][:, 0:1]),
                    reads=[bpr["s"]], writes=[bpr["rs"]])
            cx.step("vector", lambda e_: e_.tensor_scalar_mul(pr["p"][:, :], pr["e"][:, :], pr["rs"][:, 0:1]),
                    reads=[bpr["e"], bpr["rs"]], writes=[bpr["p"]])
            cx.step("vector", lambda e_, blk=blk: e_.reduce_max(
                tp_all[:, blk:blk + 1], pr["p"][:, :], axis=AX.X),
                reads=[bpr["p"]], writes=[btp])
            cx.step("vector", lambda e_, blk=blk: e_.tensor_scalar(
                oh_all[:, blk * 8:(blk + 1) * 8], pr["p"][:, :], tp_all[:, blk:blk + 1], None, ALU.is_ge),
                reads=[bpr["p"], btp], writes=[boh])
            cx._hazard_waits("tensor", [boh, buf("U128"), bcarry, buf("ones")], [bpB])
            cx.step("tensor", lambda e_, blk=blk: e_.matmul(
                pB[:, 0:8], U128[:, :], oh_all[:, blk * 8:(blk + 1) * 8], start=True, stop=False))
            cx.step("tensor", lambda e_: e_.matmul(
                pB[:, 0:8], ones[0:1, :], carry[0:1, :], start=False, stop=True))
            cx._hazard_update(cx.s["tensor"], cx.s["tensor"].n,
                              [boh, buf("U128"), bcarry, buf("ones")], [bpB])
            cx.step("vector", lambda e_: e_.tensor_copy(cs_sb[:, :], pB[:, 0:8]),
                    reads=[bpB], writes=[bcs])
            cx.step("vector", lambda e_, blk=blk: e_.tensor_mul(
                sel_sb[:, :], oh_all[:, blk * 8:(blk + 1) * 8], cs_sb[:, :]),
                reads=[boh, bcs], writes=[bsel])
            cx.step("vector", lambda e_, blk=blk: e_.reduce_sum(
                pos_all[:, blk:blk + 1], sel_sb[:, :], axis=AX.X),
                reads=[bsel], writes=[bpos])
            cx.step("tensor", lambda e_: e_.matmul(pA[0:1, 0:8], I128[:, 127:128], cs_sb[:, :],
                                                   start=True, stop=True),
                    reads=[buf("I128"), bcs], writes=[bpA])
            cx.step("vector", lambda e_: e_.tensor_copy(carry[0:1, :], pA[0:1, 0:8]),
                    reads=[bpA], writes=[bcarry])

        cx.dma("gpsimd", cc_in.ap(), carry[0:1, :], reads=[bcarry], sem=cx.s["gpsimd"])
        gp_sem = cx.s["gpsimd"]
        cc_sem = Sem(e(nc.semaphore("s_cc")), "gpsimd")
        cx.q["gpsimd"].append(lambda e_, t=gp_sem.n: e_.wait_ge(gp_sem.h, t))
        cc_sem.n += 1

        def do_cc(e_):
            return e_.collective_compute(
                "AllGather", ALU.bypass, replica_groups=[list(range(8))],
                ins=[cc_in.ap().opt()], outs=[cc_out.ap().opt()])
        cx.q["gpsimd"].append(lambda e_: do_cc(e_).then_inc(cc_sem.h))
        cx.q["gpsimd"].append(lambda e_: e_.wait_ge(cc_sem.h, 1))
        cx.dma("gpsimd", ag_sb[:, :], cc_out.ap(), writes=[bag], sem=cx.s["gpsimd"])

        cx.step("tensor", lambda e_: e_.matmul(pA[0:1, 0:8], prefmask[:, 0:1], ag_sb[:, :],
                                               start=True, stop=True),
                reads=[buf("prefmask"), bag], writes=[bpA])
        cx.step("vector", lambda e_: e_.tensor_copy(offs_row[0:1, :], pA[0:1, 0:8]),
                reads=[bpA], writes=[boffr])
        cx.step("tensor", lambda e_: e_.matmul(pA[:, 0:8], ones[0:1, :], offs_row[0:1, :],
                                               start=True, stop=True),
                reads=[boffr, buf("ones")], writes=[bpA])
        cx.step("vector", lambda e_: e_.tensor_copy(offs_b[:, :], pA[:, 0:8]),
                reads=[bpA], writes=[boffb])

        for blk in range(16):
            cx.step("vector", lambda e_, blk=blk: e_.tensor_mul(
                sel_sb[:, :], oh_all[:, blk * 8:(blk + 1) * 8], offs_b[:, :]),
                reads=[boh, boffb], writes=[bsel])
            cx.step("vector", lambda e_: e_.reduce_sum(po[:, 0:1], sel_sb[:, :], axis=AX.X),
                    reads=[bsel], writes=[bpo])
            cx.step("vector", lambda e_, blk=blk: e_.tensor_add(
                posg[:, 0:1], pos_all[:, blk:blk + 1], po[:, 0:1]),
                reads=[bpos, bpo], writes=[bposg])
            cx.step("vector", lambda e_: e_.tensor_single_scalar(
                keep[:, 0:1], posg[:, 0:1], 2560.0, ALU.is_le),
                reads=[bposg], writes=[bkeep])
            cx.step("vector", lambda e_, blk=blk: e_.tensor_mul(
                cw1[:, 0:1], keep[:, 0:1], tp_all[:, blk:blk + 1]),
                reads=[bkeep, btp], writes=[bcw1])
            cx.step("vector", lambda e_, blk=blk: e_.tensor_scalar_mul(
                cwf[:, :], oh_all[:, blk * 8:(blk + 1) * 8], cw1[:, 0:1]),
                reads=[boh, bcw1], writes=[bcwf])
            cx.step("tensor", lambda e_: e_.transpose(pB[0:8, 0:128], cwf[:, :], I128[:, :]),
                    reads=[bcwf, buf("I128")], writes=[bpB])
            cx.step("vector", lambda e_: e_.tensor_copy(cwT_st[:, :], pB[0:8, 0:128]),
                    reads=[bpB], writes=[bcwst])
            cx.dma("sync", cx.ap["cwT1"][0:8, blk * 128:(blk + 1) * 128], cwT_st[:, :],
                   reads=[bcwst], sem=cx.s["out"])
        cx.finish(block)
    return nc


def prep_l1g_inputs(xn2_T, W, C, core):
    pm = np.zeros((8, 1), np.float32)
    pm[:core, 0] = 1.0
    return {"xn2_T": xn2_T,
            "gwT": np.ascontiguousarray(W["L1_ffn_gate_w"].T),
            "gb_row": np.ascontiguousarray(np.asarray(W["L1_ffn_gate_b"], np.float32)[None, :]),
            "prefmask": pm, "ones": C["ones"], "I128": C["I128"], "U128": C["U128"]}



def build_l1f():
    nc = bass.Bass("TRN2", target_bir_lowering=False)
    with ExitStack() as st:
        cx = Ctx(nc, st)
        e = st.enter_context
        inp, outp = cx.inp, cx.outp
        inp("xn2_T", (D, T)); inp("acc1_T", (D, T)); inp("cwT1", (8, T))
        inp("w1", (8, D, 4 * D)); inp("w2", (8, 4 * D, D))
        inp("b1_cols", (128, 8 * 32)); inp("b2_cols", (128, 8 * 8))
        inp("sel8", (8, 8 * 128))
        outp("out_T", (D, T))

        sb = {}

        def tile(name, shape, dt=F32):
            sb[name] = e(nc.sbuf_tensor("sb_" + name, list(shape), dt))
            return sb[name]

        xnb = tile("xnb", [128, DC * T], BF16)
        acc = tile("acc", [128, DC * T])
        h_sb = tile("h_sb", [128, 32 * 1024], BF16)
        wbuf = tile("wbuf", [128, 8 * 1024], BF16)
        cw_sb = tile("cw_sb", [8, T])
        sel8 = tile("sel8", [8, 8 * 128])
        b1_cols = tile("b1_cols", [128, 8 * 32])
        b2_cols = tile("b2_cols", [128, 8 * 8])
        cwb_sb = tile("cwb_sb", [128, 2 * G])
        ytmp = tile("ytmp", [128, G])
        yts = tile("yts", [128, G])

        hp = [e(nc.psum_tensor("hp0", [128, G], F32)), e(nc.psum_tensor("hp1", [128, G], F32))]
        yp = [e(nc.psum_tensor("yp0", [128, G], F32)), e(nc.psum_tensor("yp1", [128, G], F32))]
        cp = e(nc.psum_tensor("cp", [128, G], F32))

        B = {}

        def buf(name):
            if name not in B:
                B[name] = Buf(cx)
            return B[name]

        block = e(nc.Block())

        bxnb, bacc, bh, bw = buf("xnb"), buf("acc"), buf("h_sb"), buf("wbuf")
        bcws, bsel, bcwb, byt = buf("cw_sb"), buf("sel8"), buf("cwb_sb"), buf("ytmp")
        byts = buf("yts")
        bhp = [buf("hp0"), buf("hp1")]
        byp = [buf("yp0"), buf("yp1")]
        bcp = buf("cp")

        cx.dma("gpsimd", fm3(xnb, T), fm(cx.ap["xn2_T"]), writes=[bxnb], sem=cx.s["gpsimd"])
        cx.dma("sync", fm3(acc, T), fm(cx.ap["acc1_T"]), writes=[bacc], sem=cx.s["sync"])
        cx.dma("sync", cw_sb[:, :], cx.ap["cwT1"], writes=[bcws], sem=cx.s["sync"])
        cx.dma("sync", sel8[:, :], cx.ap["sel8"], writes=[bsel], sem=cx.s["sync"])
        cx.dma("sync", b1_cols[:, :], cx.ap["b1_cols"], writes=[buf("b1_cols")], sem=cx.s["sync"])
        cx.dma("sync", b2_cols[:, :], cx.ap["b2_cols"], writes=[buf("b2_cols")], sem=cx.s["sync"])

        NTOK = 1024

        for ex in range(8):
            for nt in range(2):
                for fb in range(4):
                    cx.dma("gpsimd",
                           wbuf[:, :].rearrange("p (c f) -> p c f", f=1024),
                           cx.ap["w1"][ex, :, fb * 1024:(fb + 1) * 1024]
                           .rearrange("(c p) f -> p c f", p=128),
                           writes=[bw], sem=cx.s["gpsimd"])
                    for fc in range(8):
                        fcg = fb * 8 + fc
                        for ns in range(2):
                            ps, bps = hp[(fcg * 2 + ns) % 2], bhp[(fcg * 2 + ns) % 2]
                            cx._hazard_waits("tensor", [bw, bxnb], [bps])
                            for c in range(DC):
                                cx.step("tensor", lambda e_, c=c, fc=fc, nt=nt, ns=ns, ps=ps: e_.matmul(
                                    ps[:, :],
                                    wbuf[:, c * 1024 + fc * 128: c * 1024 + (fc + 1) * 128],
                                    xnb[:, c * T + nt * NTOK + ns * G: c * T + nt * NTOK + (ns + 1) * G],
                                    start=(c == 0), stop=(c == DC - 1)))
                            cx._hazard_update(cx.s["tensor"], cx.s["tensor"].n, [bw, bxnb], [bps])
                            cx.step("scalar", lambda e_, ex=ex, fcg=fcg, ns=ns, ps=ps: e_.activation(
                                h_sb[:, fcg * 1024 + ns * G: fcg * 1024 + (ns + 1) * G],
                                ps[:, :], AF.Relu, bias=b1_cols[:, ex * 32 + fcg: ex * 32 + fcg + 1]),
                                reads=[bps, buf("b1_cols")], writes=[bh])
                for ns in range(2):
                    cx.step("tensor", lambda e_, ex=ex, nt=nt, ns=ns: e_.matmul(
                        cp[:, :], sel8[:, ex * 128:(ex + 1) * 128],
                        cw_sb[0:8, nt * NTOK + ns * G: nt * NTOK + (ns + 1) * G],
                        start=True, stop=True),
                        reads=[bsel, bcws], writes=[bcp])
                    cx.step("vector", lambda e_, ns=ns: e_.tensor_copy(
                        cwb_sb[:, ns * G:(ns + 1) * G], cp[:, :]),
                        reads=[bcp], writes=[bcwb])
                for jb in range(4):
                    cx.dma("gpsimd",
                           wbuf[:, :].rearrange("p (c f) -> p c f", f=256),
                           cx.ap["w2"][ex, :, jb * 256:(jb + 1) * 256]
                           .rearrange("(c p) j -> p c j", p=128),
                           writes=[bw], sem=cx.s["gpsimd"])
                    for ns in range(2):
                        for jcr in range(2):
                            jc = jb * 2 + jcr
                            ps, bps = yp[jcr], byp[jcr]
                            cx._hazard_waits("tensor", [bw, bh], [bps])
                            for fc in range(32):
                                cx.step("tensor", lambda e_, fc=fc, jcr=jcr, ns=ns, ps=ps: e_.matmul(
                                    ps[:, :],
                                    wbuf[:, fc * 256 + jcr * 128: fc * 256 + (jcr + 1) * 128],
                                    h_sb[:, fc * 1024 + ns * G: fc * 1024 + (ns + 1) * G],
                                    start=(fc == 0), stop=(fc == 31)))
                            cx._hazard_update(cx.s["tensor"], cx.s["tensor"].n, [bw, bh], [bps])
                            cx.step("scalar", lambda e_, ex=ex, jc=jc, ps=ps: e_.activation(
                                yts[:, :], ps[:, :], AF.Identity,
                                bias=b2_cols[:, ex * 8 + jc: ex * 8 + jc + 1]),
                                reads=[bps, buf("b2_cols")], writes=[byts])
                            cx.step("vector", lambda e_, ns=ns: e_.tensor_tensor(
                                ytmp[:, :], yts[:, :], cwb_sb[:, ns * G:(ns + 1) * G], ALU.mult),
                                reads=[byts, bcwb], writes=[byt])
                            asl = acc[:, jc * T + nt * NTOK + ns * G: jc * T + nt * NTOK + (ns + 1) * G]
                            cx.step("vector", lambda e_, asl=asl: e_.tensor_add(asl, asl, ytmp[:, :]),
                                    reads=[byt], writes=[bacc])

        for c in range(DC):
            cx.dma("sync", cx.ap["out_T"][c * 128:(c + 1) * 128, :], acc[:, c * T:(c + 1) * T],
                   reads=[bacc], sem=cx.s["out"])
        cx.finish(block)
    return nc


def prep_l1f_inputs(xn2_T, acc1_T, cwT1, W, C):
    b1 = np.asarray(W["L1_ffn_b1"], np.float32)
    b2 = np.asarray(W["L1_ffn_b2"], np.float32)
    b1c = b1.reshape(8, 32, 128).transpose(2, 0, 1).reshape(128, 8 * 32)
    b2c = b2.reshape(8, 8, 128).transpose(2, 0, 1).reshape(128, 8 * 8)
    return {"xn2_T": xn2_T, "acc1_T": acc1_T, "cwT1": cwT1,
            "w1": W["L1_ffn_w1"], "w2": W["L1_ffn_w2"],
            "b1_cols": np.ascontiguousarray(b1c), "b2_cols": np.ascontiguousarray(b2c),
            "sel8": C["sel8"]}





def emit_moe(cx, *, ex, nt, T_, G_, xnb, bxnb, h_sb, bh, wbufs, bwbufs,
             accsl_fn, bacc, cw_tile, bcw, sel8, bsel8, cwb_sb, bcwb,
             ytmp, byt, yts, byts, b1_cols, bb1, b2_cols, bb2,
             hp, bhp, yp4, byp4, cp, bcp, w1_ap, w2_ap, act):
    NTOK = 1024
    AF_ = mybir.ActivationFunctionType

    def h_dma(s):
        fb, half = divmod(s, 2)
        wb, bw = wbufs[s % 2], bwbufs[s % 2]
        cx.dma("gpsimd", wb[:, :].rearrange("p (c f) -> p c f", f=512),
               w1_ap[ex, :, fb * 1024 + half * 512: fb * 1024 + (half + 1) * 512]
               .rearrange("(c p) f -> p c f", p=128),
               writes=[bw], sem=cx.s["ws%d" % (s % 2)])

    def h_mms(s):
        fb, half = divmod(s, 2)
        wb, bw = wbufs[s % 2], bwbufs[s % 2]
        for fcrel in range(4):
            fcg = fb * 8 + half * 4 + fcrel
            for ns in range(2):
                ps, bps = hp[(fcg * 2 + ns) % 2], bhp[(fcg * 2 + ns) % 2]
                cx._hazard_waits("tensor", [bw, bxnb], [bps])
                for c in range(DC):
                    cx.step("tensor", lambda e_, c=c, fcrel=fcrel, ns=ns, ps=ps, wb=wb: e_.matmul(
                        ps[:, :], wb[:, c * 512 + fcrel * 128: c * 512 + (fcrel + 1) * 128],
                        xnb[:, c * T_ + nt * NTOK + ns * G_: c * T_ + nt * NTOK + (ns + 1) * G_],
                        start=(c == 0), stop=(c == DC - 1)))
                cx._hazard_update(cx.s["tensor"], cx.s["tensor"].n, [bw, bxnb], [bps])
                if act == "gelu":
                    cx.step("scalar", lambda e_, fcg=fcg, ns=ns, ps=ps: e_.activation(
                        h_sb[:, fcg * 1024 + ns * G_: fcg * 1024 + (ns + 1) * G_],
                        ps[:, :], AF_.Gelu),
                        reads=[bps], writes=[bh])
                else:
                    cx.step("scalar", lambda e_, fcg=fcg, ns=ns, ps=ps: e_.activation(
                        h_sb[:, fcg * 1024 + ns * G_: fcg * 1024 + (ns + 1) * G_],
                        ps[:, :], AF_.Relu, bias=b1_cols[:, ex * 32 + fcg: ex * 32 + fcg + 1]),
                        reads=[bps, bb1], writes=[bh])

    def y_dma(s, bufsel):
        jb, half = divmod(s, 2)
        wb, bw = wbufs[bufsel], bwbufs[bufsel]
        cx.dma("gpsimd", wb[:, :].rearrange("p (c f) -> p c f", f=256),
               w2_ap[ex, half * 2048:(half + 1) * 2048, jb * 256:(jb + 1) * 256]
               .rearrange("(c p) j -> p c j", p=128),
               writes=[bw], sem=cx.s["ws%d" % bufsel])

    def y_mms(s, bufsel):
        jb, half = divmod(s, 2)
        wb, bw = wbufs[bufsel], bwbufs[bufsel]
        for ns in range(2):
            for jcr in range(2):
                ps, bps = yp4[ns * 2 + jcr], byp4[ns * 2 + jcr]
                if half == 0:
                    cx._hazard_waits("tensor", [bw, bh], [bps])
                else:
                    cx._hazard_waits("tensor", [bw, bh], [])
                for fcrel in range(16):
                    fc = half * 16 + fcrel
                    cx.step("tensor", lambda e_, fcrel=fcrel, fc=fc, jcr=jcr, ns=ns, ps=ps, wb=wb: e_.matmul(
                        ps[:, :], wb[:, fcrel * 256 + jcr * 128: fcrel * 256 + (jcr + 1) * 128],
                        h_sb[:, fc * 1024 + ns * G_: fc * 1024 + (ns + 1) * G_],
                        start=(half == 0 and fcrel == 0), stop=(half == 1 and fcrel == 15)))
                if half == 1:
                    cx._hazard_update(cx.s["tensor"], cx.s["tensor"].n, [bw, bh], [bps])
                else:
                    bw.readers.append((cx.s["tensor"], cx.s["tensor"].n))
                    bh.readers.append((cx.s["tensor"], cx.s["tensor"].n))
                    bps.w = (cx.s["tensor"], cx.s["tensor"].n)
        if half == 1:
            for ns in range(2):
                for jcr in range(2):
                    jc = jb * 2 + jcr
                    ps, bps = yp4[ns * 2 + jcr], byp4[ns * 2 + jcr]
                    if b2_cols is not None:
                        cx.step("scalar", lambda e_, jc=jc, ps=ps: e_.activation(
                            yts[:, :], ps[:, :], AF_.Identity,
                            bias=b2_cols[:, ex * 8 + jc: ex * 8 + jc + 1]),
                            reads=[bps, bb2], writes=[byts])
                        ysrc, bysrc = yts, byts
                    else:
                        ysrc, bysrc = ps, bps
                    cx.step("vector", lambda e_, ns=ns, ysrc=ysrc: e_.tensor_tensor(
                        ytmp[:, :], ysrc[:, :], cwb_sb[:, ns * G_:(ns + 1) * G_], ALU.mult),
                        reads=[bysrc, bcwb], writes=[byt])
                    asl = accsl_fn(jc, ns)
                    cx.step("vector", lambda e_, asl=asl: e_.tensor_add(asl, asl, ytmp[:, :]),
                            reads=[byt], writes=[bacc])

    for s in range(8):
        h_dma(s)
        if s >= 1:
            h_mms(s - 1)
    h_mms(7)
    for ns in range(2):
        cx.step("tensor", lambda e_, ns=ns: e_.matmul(
            cp[:, :], sel8[:, ex * 128:(ex + 1) * 128],
            cw_tile[0:8, nt * NTOK + ns * G_: nt * NTOK + (ns + 1) * G_],
            start=True, stop=True),
            reads=[bsel8, bcw], writes=[bcp])
        cx.step("vector", lambda e_, ns=ns: e_.tensor_copy(
            cwb_sb[:, ns * G_:(ns + 1) * G_], cp[:, :]),
            reads=[bcp], writes=[bcwb])
    for s in range(8):
        y_dma(s, s % 2)
        if s >= 1:
            y_mms(s - 1, (s - 1) % 2)
    y_mms(7, 1)



def build_fused(upto=5, dbg=False, p3_alloc_only=False):
    nc = bass.Bass("TRN2", target_bir_lowering=False)
    with ExitStack() as st:
        cx = Ctx(nc, st)
        e = st.enter_context
        inp, outp = cx.inp, cx.outp

        inp("xT", (D, T))
        inp("WqkvT", (D, 3 * D)); inp("bq_cols", (128, 24))
        inp("WoutT", (D, D)); inp("bout_cols", (128, 8))
        inp("gamma_cols", (128, 8)); inp("beta_cols", (128, 8))
        inp("rwT", (D, 8))
        inp("w1_0", (8, D, 4 * D)); inp("w2_0", (8, 4 * D, D))
        inp("WqT", (D, D)); inp("WkT", (D, D)); inp("WvT", (D, D)); inp("WoT", (D, D))
        inp("lng_cols", (128, 8)); inp("lnb_cols", (128, 8))
        inp("hs_row", (1, 8)); inp("hd3", (3, 8))
        inp("gwT", (D, 8)); inp("gb_row", (1, 8)); inp("prefmask", (8, 1))
        inp("w1_1", (8, D, 4 * D)); inp("w2_1", (8, 4 * D, D))
        inp("b1_cols", (128, 8 * 32)); inp("b2_cols", (128, 8 * 8))
        inp("ones", (128, 128)); inp("onesb", (128, 128), BF16)
        inp("I128", (128, 128)); inp("I128b", (128, 128), BF16)
        inp("U128", (128, 128)); inp("maskT3", (128, 384)); inp("sel8", (8, 8 * 128))
        inp("U512", (128, 512)); inp("maskTW", (128, 7 * 128))
        inp("cos_h", (128, T)); inp("sin_h", (128, T))
        outp("OUT_T", (D, T))

        if dbg:
            acc_d = outp("dbg_acc", (D, T))
            cw1_dbg = outp("dbg_cw1", (8, T))
            dbg_p2 = outp("dbg_p2", (D, T))
            dbg_xnb2 = outp("dbg_xnb2", (D, T), BF16)
        else:
            acc_d = nc.dram_tensor("acc_d", [D, T], F32).ap()
        block = st.enter_context(nc.Block())
        cc_in = nc.dram_tensor("cc_in", [1, 8], F32)
        cc_out = nc.dram_tensor("cc_out", [8, 8], F32)

        B = {}

        def buf(name):
            if name not in B:
                B[name] = Buf(cx)
            return B[name]

        sb = {}

        _nseq = [0]

        def mktile(scope, name, shape, dt=F32):
            _nseq[0] += 1
            sb[name] = scope.enter_context(nc.sbuf_tensor(f"sb_{name}_{_nseq[0]}", list(shape), dt))
            return sb[name]

        tile = lambda n, s, d=F32: mktile(st, n, s, d)
        ones = tile("ones", [128, 128]); onesb = tile("onesb", [128, 128], BF16)
        I128 = tile("I128", [128, 128]); I128b = tile("I128b", [128, 128], BF16)
        U128 = tile("U128", [128, 128])
        sel8 = tile("sel8", [8, 8 * 128])
        bq_cols = tile("bq_cols", [128, 24]); nbq_cols = tile("nbq_cols", [128, 24])
        bout_cols = tile("bout_cols", [128, 8])
        gamma_cols = tile("gamma_cols", [128, 8]); beta_cols = tile("beta_cols", [128, 8])
        lng_cols = tile("lng_cols", [128, 8]); lnb_cols = tile("lnb_cols", [128, 8])
        hs_row = tile("hs_row", [1, 8]); hd3f = tile("hd3f", [3, 8]); hd3b = tile("hd3b", [3, 8], BF16)
        gb_row = tile("gb_row", [1, 8]); prefmask = tile("prefmask", [8, 1])
        rw_f = tile("rw_f", [128, DC * 8]); gw_f = tile("gw_f", [128, DC * 8])
        cwT_sb = tile("cwT_sb", [8, T])
        oh_all = tile("oh_all", [128, 16 * 8]); pos_all = tile("pos_all", [128, 16])
        tp_all = tile("tp_all", [128, 16]); carry = tile("carry", [1, 8])
        ag_sb = tile("ag_sb", [8, 8])

        for n in ("ones", "I128", "U128", "sel8", "bq_cols", "bout_cols",
                  "gamma_cols", "beta_cols", "lng_cols", "lnb_cols", "hs_row",
                  "gb_row", "prefmask"):
            cx.dma("sync", sb[n][:, :], cx.ap[n], writes=[buf(n)], sem=cx.s["sync"])
        cx.dma("sync", sb["hd3f"][:, :], cx.ap["hd3"], writes=[buf("hd3f")], sem=cx.s["sync"])
        U512 = tile("U512", [128, 512])
        cx.dma("sync", U512[:, :], cx.ap["U512"], writes=[buf("U512")], sem=cx.s["sync"])
        cx.dma("sync", fm3(rw_f, 8), fm(cx.ap["rwT"]), writes=[buf("rw_f")], sem=cx.s["sync"])
        cx.dma("sync", fm3(gw_f, 8), fm(cx.ap["gwT"]), writes=[buf("gw_f")], sem=cx.s["sync"])
        cx.dma("gpsimd", onesb[:, :], cx.ap["onesb"], writes=[buf("onesb")], sem=cx.s["gpsimd"])
        cx.dma("gpsimd", I128b[:, :], cx.ap["I128b"], writes=[buf("I128b")], sem=cx.s["gpsimd"])
        cx.dma("gpsimd", hd3b[:, :], cx.ap["hd3"], writes=[buf("hd3b")], sem=cx.s["gpsimd"])

        cx.step("vector", lambda e_: e_.tensor_scalar_mul(nbq_cols[:, :], bq_cols[:, :], -1.0),
                reads=[buf("bq_cols")], writes=[buf("nbq_cols")])

        bacc_d = buf("acc_d")
        bcw = buf("cwT_sb")
        bsq = buf("sq_sb")
        bra, brb, brc, brd = buf("row_a"), buf("row_b"), buf("row_c"), buf("row_d")
        bmb, brbs, bs2 = buf("mb_sb"), buf("rb_sb"), buf("s2_sb")
        bxn = buf("xn_sc")
        boh, bpos, btp = buf("oh_all"), buf("pos_all"), buf("tp_all")
        bcarry, bag = buf("carry"), buf("ag_sb")

        def ln_stats(src_fn, src_buf, p1, p2, bp1, bp2, want_s2=False):
            sq_sb = sb["sq_sb"]; row_a = sb["row_a"]; row_b = sb["row_b"]
            row_c = sb["row_c"]; row_d = sb["row_d"]
            mb_sb = sb["mb_sb"]; rb_sb = sb["rb_sb"]; s2_sb = sb["s2_sb"]
            cx._hazard_waits("tensor", [src_buf], [bp1, bp2])
            for c in range(DC):
                srcv = src_fn(c)
                cx.step("scalar", lambda e_, srcv=srcv: e_.activation(sq_sb[:, :], srcv, AF.Square),
                        reads=[src_buf], writes=[bsq])
                cx.step("tensor", lambda e_, srcv=srcv, c=c: e_.matmul(
                    p1[0:1, 0:G], ones[:, 0:1], srcv, start=(c == 0), stop=(c == DC - 1)))
                cx.step("tensor", lambda e_, c=c: e_.matmul(
                    p2[0:1, 0:G], ones[:, 0:1], sq_sb[:, :], start=(c == 0), stop=(c == DC - 1)),
                    reads=[bsq])
            cx._hazard_update(cx.s["tensor"], cx.s["tensor"].n, [src_buf], [bp1, bp2])
            cx.step("vector", lambda e_: e_.tensor_scalar_mul(row_a[0:1, :], p1[0:1, 0:G], 1.0 / D),
                    reads=[bp1], writes=[bra])
            cx.step("vector", lambda e_: e_.tensor_scalar_mul(row_b[0:1, :], p2[0:1, 0:G], 1.0 / D),
                    reads=[bp2], writes=[brb])
            cx.step("vector", lambda e_: e_.tensor_mul(row_c[0:1, :], row_a[0:1, :], row_a[0:1, :]),
                    reads=[bra], writes=[brc])
            cx.step("vector", lambda e_: e_.tensor_tensor(row_b[0:1, :], row_b[0:1, :], row_c[0:1, :], ALU.subtract),
                    reads=[brc], writes=[brb])
            if want_s2:
                cx.step("vector", lambda e_: e_.tensor_copy(row_d[0:1, :], row_b[0:1, :]),
                        reads=[brb], writes=[brd])
            cx.step("vector", lambda e_: e_.tensor_scalar_add(row_b[0:1, :], row_b[0:1, :], 1e-5),
                    writes=[brb])
            cx.step("scalar", lambda e_: e_.activation(row_c[0:1, :], row_b[0:1, :], AF.Ln),
                    reads=[brb], writes=[brc])
            cx.step("scalar", lambda e_: e_.activation(row_b[0:1, :], row_c[0:1, :], AF.Exp, scale=-0.5),
                    reads=[brc], writes=[brb])
            cx.step("tensor", lambda e_, p1=p1: e_.matmul(p1[:, 0:G], ones[0:1, :], row_a[0:1, :], start=True, stop=True),
                    reads=[bra], writes=[bp1])
            cx.step("vector", lambda e_, p1=p1: e_.tensor_copy(mb_sb[:, :], p1[:, 0:G]),
                    reads=[bp1], writes=[bmb])
            cx.step("tensor", lambda e_, p1=p1: e_.matmul(p1[:, 0:G], ones[0:1, :], row_b[0:1, :], start=True, stop=True),
                    reads=[brb], writes=[bp1])
            cx.step("vector", lambda e_, p1=p1: e_.tensor_copy(rb_sb[:, :], p1[:, 0:G]),
                    reads=[bp1], writes=[brbs])
            if want_s2:
                cx.step("vector", lambda e_: e_.tensor_mul(row_c[0:1, :], row_b[0:1, :], row_b[0:1, :]),
                        reads=[brb], writes=[brc])
                cx.step("vector", lambda e_: e_.tensor_mul(row_d[0:1, :], row_d[0:1, :], row_c[0:1, :]),
                        reads=[brc], writes=[brd])
                cx.step("vector", lambda e_: e_.tensor_scalar_add(row_d[0:1, :], row_d[0:1, :], 1e-5),
                        writes=[brd])
                cx.step("scalar", lambda e_: e_.activation(row_c[0:1, :], row_d[0:1, :], AF.Ln),
                        reads=[brd], writes=[brc])
                cx.step("scalar", lambda e_: e_.activation(row_d[0:1, :], row_c[0:1, :], AF.Exp, scale=-0.5),
                        reads=[brc], writes=[brd])
                cx.step("tensor", lambda e_, p2=p2: e_.matmul(p2[:, 0:G], ones[0:1, :], row_d[0:1, :], start=True, stop=True),
                        reads=[brd], writes=[bp2])
                cx.step("vector", lambda e_, p2=p2: e_.tensor_copy(s2_sb[:, :], p2[:, 0:G]),
                        reads=[bp2], writes=[bs2])

        stA = ExitStack()
        xnb = mktile(stA, "xnb", [128, DC * T], BF16)
        stA1 = ExitStack()
        t1_ = lambda n, s, d=F32: mktile(stA1, n, s, d)
        sq_sb = t1_("sq_sb", [128, G])
        row_a = t1_("row_a", [1, G]); row_b = t1_("row_b", [1, G])
        row_c = t1_("row_c", [1, G]); row_d = t1_("row_d", [1, G])
        mb_sb = t1_("mb_sb", [128, G]); rb_sb = t1_("rb_sb", [128, G])
        s2_sb = t1_("s2_sb", [128, G]); xn_sc = t1_("xn_sc", [128, G])
        x_g = t1_("x_g", [128, DC * G])
        wjc = [t1_("wjc0", [128, DC * 128], BF16), t1_("wjc1", [128, DC * 128], BF16)]
        wojc = t1_("wojc", [128, DC * 128], BF16)
        phiq = t1_("phiq", [64, 16 * G], BF16)
        phik = t1_("phik", [64, 16 * G], BF16)
        v_fm = t1_("v_fm", [128, 8 * G], BF16)
        t_r1 = t1_("t_r1", [128, G]); t_r2 = t1_("t_r2", [128, G])
        t_e2 = t1_("t_e2", [128, G]); t_s = t1_("t_s", [128, G])
        v_tm = t1_("v_tm", [128, 4 * 1024], BF16)
        k_tm = t1_("k_tm", [128, 4 * 1024], BF16)
        A4_sb = t1_("A4_sb", [128, 512], BF16)
        A4_sb2 = t1_("A4_sb2", [128, 512], BF16)
        S_all = t1_("S_all", [64, 16 * 128])
        S_allb = t1_("S_allb", [64, 16 * 128], BF16)
        dn4 = t1_("dn4", [1, 512]); rdn4 = t1_("rdn4", [1, 512])
        rdnb4 = t1_("rdnb4", [64, 512])
        ol_fm = t1_("ol_fm", [128, 8 * G], BF16)
        a0_fm = t1_("a0_fm", [128, 8 * G])
        lg_all = t1_("lg_all", [128, 32])
        pr4 = [{n: t1_("pr%d_" % blk + n, [128, w]) for n, w in
                [("m", 1), ("e", 8), ("s", 1), ("rs", 1), ("p", 8), ("p1", 1),
                 ("oh", 8), ("pm", 8), ("p2", 1), ("mk", 8), ("dn", 1), ("rd", 1), ("cw", 8)]}
               for blk in range(4)]
        cwst = t1_("cwst", [8, 512])

        stAp = ExitStack()
        p_mm1 = stAp.enter_context(nc.psum_tensor("p_mm1", [128, G], F32))
        p_mm2 = stAp.enter_context(nc.psum_tensor("p_mm2", [128, G], F32))
        p_bc = stAp.enter_context(nc.psum_tensor("p_bc", [128, G], F32))
        p_tp = stAp.enter_context(nc.psum_tensor("p_tp", [128, 1024], BF16))
        pA = stAp.enter_context(nc.psum_tensor("pA", [128, G], F32))
        pN = stAp.enter_context(nc.psum_tensor("pN", [128, G], F32))
        pB = stAp.enter_context(nc.psum_tensor("pB", [128, G], F32))
        pS = stAp.enter_context(nc.psum_tensor("pS", [128, G], F32))

        bxnb, bx = buf("xnb"), buf("x_g")
        bwjc = [buf("wjc0"), buf("wjc1")]
        bwo = buf("wojc")
        bphiq, bphik, bvfm = buf("phiq"), buf("phik"), buf("v_fm")
        br1, br2, be2, bts = buf("t_r1"), buf("t_r2"), buf("t_e2"), buf("t_s")
        bvtm, bktm = buf("v_tm"), buf("k_tm")
        bA = buf("A4_sb"); bA2 = buf("A4_sb2"); bS, bSb = buf("S_all"), buf("S_allb")
        bdn, brdn, brdnb = buf("dn4"), buf("rdn4"), buf("rdnb4")
        bol = buf("ol_fm"); ba0 = buf("a0_fm")
        blga = buf("lg_all")
        bpr4 = [{n: buf("pr%d_" % blk + n) for n in pr4[blk]} for blk in range(4)]
        bcwst = buf("cwst")
        bp1, bp2, bpbc = buf("p_mm1"), buf("p_mm2"), buf("p_bc")
        bptp = buf("p_tp")
        bpA, bpN, bpB, bpS = buf("pA"), buf("pN"), buf("pB"), buf("pS")

        cx.step("vector", lambda e_: e_.memset(S_all[:, :], 0.0), writes=[bS])
        cx.step("vector", lambda e_: e_.memset(S_allb[:, :], 0.0), writes=[bSb])

        def router_tails(g):
            for blk in range(4):
                cx.step("vector", lambda e_, blk=blk: e_.reduce_max(
                    pr4[blk]["m"][:, 0:1], lg_all[:, blk * 8:(blk + 1) * 8], axis=AX.X, negate=True),
                    reads=[blga], writes=[bpr4[blk]["m"]])
            for blk in range(4):
                cx.step("scalar", lambda e_, blk=blk: e_.activation(
                    pr4[blk]["e"][:, :], lg_all[:, blk * 8:(blk + 1) * 8], AF.Exp,
                    bias=pr4[blk]["m"][:, 0:1], accum_out=pr4[blk]["s"][:, 0:1]),
                    reads=[blga, bpr4[blk]["m"]], writes=[bpr4[blk]["e"], bpr4[blk]["s"]])
            for blk in range(4):
                cx.step("vector", lambda e_, blk=blk: e_.reciprocal(
                    pr4[blk]["rs"][:, 0:1], pr4[blk]["s"][:, 0:1]),
                    reads=[bpr4[blk]["s"]], writes=[bpr4[blk]["rs"]])
            for blk in range(4):
                cx.step("vector", lambda e_, blk=blk: e_.tensor_scalar_mul(
                    pr4[blk]["p"][:, :], pr4[blk]["e"][:, :], pr4[blk]["rs"][:, 0:1]),
                    reads=[bpr4[blk]["e"], bpr4[blk]["rs"]], writes=[bpr4[blk]["p"]])
            for blk in range(4):
                cx.step("vector", lambda e_, blk=blk: e_.reduce_max(
                    pr4[blk]["p1"][:, 0:1], pr4[blk]["p"][:, :], axis=AX.X),
                    reads=[bpr4[blk]["p"]], writes=[bpr4[blk]["p1"]])
            for blk in range(4):
                cx.step("vector", lambda e_, blk=blk: e_.tensor_scalar(
                    pr4[blk]["oh"][:, :], pr4[blk]["p"][:, :], pr4[blk]["p1"][:, 0:1], None, ALU.is_ge),
                    reads=[bpr4[blk]["p"], bpr4[blk]["p1"]], writes=[bpr4[blk]["oh"]])
            for blk in range(4):
                cx.step("vector", lambda e_, blk=blk: e_.scalar_tensor_tensor(
                    pr4[blk]["pm"][:, :], pr4[blk]["oh"][:, :], BIGNEG, pr4[blk]["p"][:, :],
                    ALU.mult, ALU.add),
                    reads=[bpr4[blk]["oh"], bpr4[blk]["p"]], writes=[bpr4[blk]["pm"]])
            for blk in range(4):
                cx.step("vector", lambda e_, blk=blk: e_.reduce_max(
                    pr4[blk]["p2"][:, 0:1], pr4[blk]["pm"][:, :], axis=AX.X),
                    reads=[bpr4[blk]["pm"]], writes=[bpr4[blk]["p2"]])
            for blk in range(4):
                cx.step("vector", lambda e_, blk=blk: e_.tensor_scalar(
                    pr4[blk]["mk"][:, :], pr4[blk]["p"][:, :], pr4[blk]["p2"][:, 0:1], None, ALU.is_ge),
                    reads=[bpr4[blk]["p"], bpr4[blk]["p2"]], writes=[bpr4[blk]["mk"]])
            for blk in range(4):
                cx.step("vector", lambda e_, blk=blk: e_.tensor_add(
                    pr4[blk]["dn"][:, 0:1], pr4[blk]["p1"][:, 0:1], pr4[blk]["p2"][:, 0:1]),
                    reads=[bpr4[blk]["p1"], bpr4[blk]["p2"]], writes=[bpr4[blk]["dn"]])
            for blk in range(4):
                cx.step("vector", lambda e_, blk=blk: e_.tensor_scalar_add(
                    pr4[blk]["dn"][:, 0:1], pr4[blk]["dn"][:, 0:1], 1e-9),
                    writes=[bpr4[blk]["dn"]])
            for blk in range(4):
                cx.step("vector", lambda e_, blk=blk: e_.reciprocal(
                    pr4[blk]["rd"][:, 0:1], pr4[blk]["dn"][:, 0:1]),
                    reads=[bpr4[blk]["dn"]], writes=[bpr4[blk]["rd"]])
            for blk in range(4):
                cx.step("vector", lambda e_, blk=blk: e_.tensor_mul(
                    pr4[blk]["cw"][:, :], pr4[blk]["p"][:, :], pr4[blk]["mk"][:, :]),
                    reads=[bpr4[blk]["p"], bpr4[blk]["mk"]], writes=[bpr4[blk]["cw"]])
            for blk in range(4):
                cx.step("vector", lambda e_, blk=blk: e_.tensor_scalar_mul(
                    pr4[blk]["cw"][:, :], pr4[blk]["cw"][:, :], pr4[blk]["rd"][:, 0:1]),
                    reads=[bpr4[blk]["rd"]], writes=[bpr4[blk]["cw"]])
            for blk in range(4):
                cx.step("tensor", lambda e_, blk=blk: e_.transpose(
                    p_mm1[0:8, blk * 128:(blk + 1) * 128], pr4[blk]["cw"][:, :], I128[:, :]),
                    reads=[bpr4[blk]["cw"], buf("I128")], writes=[bp1])
            for blk in range(4):
                cx.step("vector", lambda e_, blk=blk: e_.tensor_copy(
                    cwst[:, blk * 128:(blk + 1) * 128], p_mm1[0:8, blk * 128:(blk + 1) * 128]),
                    reads=[bp1], writes=[bcwst])
            for blk in range(4):
                tl = g * 4 + blk
                cx.step("vector", lambda e_, blk=blk, tl=tl: e_.tensor_copy(
                    cwT_sb[0:8, tl * 128:(tl + 1) * 128], cwst[:, blk * 128:(blk + 1) * 128]),
                    reads=[bcwst], writes=[bcw])

        for g in range(NT):
            cx.dma("sync", fm3(x_g, G), fm(cx.ap["xT"][:, g * G:(g + 1) * G]),
                   writes=[bx], sem=cx.s["sync"])
            ln_stats(lambda c: x_g[:, c * G:(c + 1) * G], bx, p_mm1, p_mm2, bp1, bp2)
            p_rt = [pA, pN, pB, pS]
            b_rt = [bpA, bpN, bpB, bpS]
            for c in range(DC):
                cx.step("vector", lambda e_, c=c, xn_sc=xn_sc, mb_sb=mb_sb: e_.tensor_tensor(
                    xn_sc[:, :], x_g[:, c * G:(c + 1) * G], mb_sb[:, :], ALU.subtract),
                    reads=[bx, bmb], writes=[bxn])
                cx.step("vector", lambda e_, xn_sc=xn_sc, rb_sb=rb_sb: e_.tensor_tensor(
                    xn_sc[:, :], xn_sc[:, :], rb_sb[:, :], ALU.mult),
                    reads=[brbs], writes=[bxn])
                cx.step("vector", lambda e_, c=c, g=g, xn_sc=xn_sc: e_.tensor_copy(
                    xnb[:, c * T + g * G: c * T + (g + 1) * G], xn_sc[:, :]),
                    reads=[bxn], writes=[bxnb])
                cx._hazard_waits("tensor", [bxn, buf("rw_f")], [])
                for blk in range(4):
                    if c == 0:
                        cx._hazard_waits("tensor", [], [b_rt[blk]])
                    cx.step("tensor", lambda e_, c=c, blk=blk, xn_sc=xn_sc: e_.matmul(
                        p_rt[blk][:, 0:8], xn_sc[:, blk * 128:(blk + 1) * 128],
                        rw_f[:, c * 8:(c + 1) * 8], start=(c == 0), stop=(c == DC - 1)))
                    if c == DC - 1:
                        cx._hazard_update(cx.s["tensor"], cx.s["tensor"].n, [bxn, buf("rw_f")], [b_rt[blk]])
                    else:
                        b_rt[blk].readers.append((cx.s["tensor"], cx.s["tensor"].n))
                        bxn.readers.append((cx.s["tensor"], cx.s["tensor"].n))
            for blk in range(4):
                cx.step("vector", lambda e_, blk=blk: e_.tensor_copy(
                    lg_all[:, blk * 8:(blk + 1) * 8], p_rt[blk][:, 0:8]),
                    reads=[b_rt[blk]], writes=[blga])

            prot = [(p_mm1, bp1), (p_mm2, bp2), (p_bc, bpbc), (pA, bpA)]
            for jc in range(24):
                wb, bwb = wjc[jc % 2], bwjc[jc % 2]
                cx.dma("gpsimd", fm3(wb, 128), fm(cx.ap["WqkvT"][:, jc * 128:(jc + 1) * 128]),
                       writes=[bwb], sem=cx.s["ws%d" % (jc % 2)])
                ps, bps = prot[jc % 4]
                cx._hazard_waits("tensor", [bxnb, bwb], [bps])
                for c in range(DC):
                    cx.step("tensor", lambda e_, c=c, jc=jc, g=g, wb=wb, ps=ps: e_.matmul(
                        ps[:, :], wb[:, c * 128:(c + 1) * 128],
                        xnb[:, c * T + g * G: c * T + (g + 1) * G],
                        start=(c == 0), stop=(c == DC - 1)))
                cx._hazard_update(cx.s["tensor"], cx.s["tensor"].n, [bxnb, bwb], [bps])
                if jc < 16:
                    is_q = jc < 8
                    dst, bdst, dj = (phiq, bphiq, jc) if is_q else (phik, bphik, jc - 8)
                    cx.step("scalar", lambda e_, jc=jc, ps=ps: e_.activation(
                        t_r1[:, :], ps[:, :], AF.Relu, bias=bq_cols[:, jc:jc + 1]),
                        reads=[bps, buf("bq_cols")], writes=[br1])
                    cx.step("scalar", lambda e_, jc=jc, ps=ps: e_.activation(
                        t_r2[:, :], ps[:, :], AF.Relu, bias=nbq_cols[:, jc:jc + 1], scale=-1.0),
                        reads=[bps, buf("nbq_cols")], writes=[br2])
                    cx.step("scalar", lambda e_: e_.activation(t_e2[:, :], t_r2[:, :], AF.Exp, scale=-1.0),
                            reads=[br2], writes=[be2])
                    cx.step("vector", lambda e_: e_.tensor_add(t_s[:, :], t_r1[:, :], t_e2[:, :]),
                            reads=[br1, be2], writes=[bts])
                    for half in range(2):
                        h_ = dj * 2 + half
                        dsl = dst[0:64, h_ * G:(h_ + 1) * G]
                        ssl = t_s[half * 64:(half + 1) * 64, :]
                        if is_q:
                            cx.step("vector", lambda e_, dsl=dsl, ssl=ssl: e_.tensor_scalar(
                                dsl, ssl, 0.125, 1.2501250e-05, ALU.mult, ALU.add),
                                reads=[bts], writes=[bdst])
                        else:
                            cx.step("vector", lambda e_, dsl=dsl, ssl=ssl: e_.tensor_scalar_add(
                                dsl, ssl, 1e-4),
                                reads=[bts], writes=[bdst])
                else:
                    dj = jc - 16
                    cx.step("scalar", lambda e_, jc=jc, dj=dj, ps=ps: e_.activation(
                        v_fm[:, dj * G:(dj + 1) * G], ps[:, :], AF.Identity, bias=bq_cols[:, jc:jc + 1]),
                        reads=[bps, buf("bq_cols")], writes=[bvfm])

            jobs = []
            for h in range(16):
                for tc in range(4):
                    jobs.append((phik[0:64, h * G + tc * 128: h * G + (tc + 1) * 128],
                                 I128b[0:64, 0:64], 64,
                                 k_tm[:, tc * 1024 + h * 64: tc * 1024 + (h + 1) * 64],
                                 bphik, bktm))
            for fc in range(8):
                for tc in range(4):
                    jobs.append((v_fm[:, fc * G + tc * 128: fc * G + (tc + 1) * 128],
                                 I128b[:, :], 128,
                                 v_tm[:, tc * 1024 + fc * 128: tc * 1024 + (fc + 1) * 128],
                                 bvfm, bvtm))
            bi = 0
            while bi < len(jobs):
                batch, used = [], 0
                while bi < len(jobs) and used + jobs[bi][2] <= 1024:
                    batch.append((used, jobs[bi]))
                    used += jobs[bi][2]
                    bi += 1
                cx._hazard_waits("tensor", [j[4] for (_, j) in batch] + [buf("I128b")], [bptp])
                for off, (in_ap, id_ap, w, dst_ap, bsrc, bdst) in batch:
                    cx.step("tensor", lambda e_, in_ap=in_ap, id_ap=id_ap, off=off, w=w: e_.transpose(
                        p_tp[:, off:off + w], in_ap, id_ap))
                cx._hazard_update(cx.s["tensor"], cx.s["tensor"].n,
                                  [j[4] for (_, j) in batch], [bptp])
                for off, (in_ap, id_ap, w, dst_ap, bsrc, bdst) in batch:
                    cx.step("vector", lambda e_, dst_ap=dst_ap, off=off, w=w: e_.tensor_copy(
                        dst_ap, p_tp[:, off:off + w]),
                        reads=[bptp], writes=[bdst])

            WAVES = [[0, 2, 4, 6], [8, 10, 12, 14], [1, 3, 5, 7], [9, 11, 13, 15]]
            wsets = [(pA, bpA, pN, bpN, pB, bpB, A4_sb, bA),
                     (p_mm1, bp1, p_mm2, bp2, p_bc, bpbc, A4_sb2, bA2)]
            widx = 0
            for tc in range(4):
                for wv in WAVES:
                    pAx, bpAx, pNx, bpNx, pBx, bpBx, Ax, bAx = wsets[widx % 2]
                    widx += 1
                    rb0 = (wv[0] % 2) * 64
                    cx._hazard_waits("tensor", [bphik, bphiq], [bpAx])
                    for i, h in enumerate(wv):
                        cx.step("tensor", lambda e_, h=h, i=i, tc=tc, pAx=pAx: e_.matmul(
                            pAx[:, i * 128:(i + 1) * 128],
                            phik[0:64, h * G + tc * 128: h * G + (tc + 1) * 128],
                            phiq[0:64, h * G + tc * 128: h * G + (tc + 1) * 128],
                            start=True, stop=True))
                    cx._hazard_update(cx.s["tensor"], cx.s["tensor"].n, [bphik, bphiq], [bpAx])
                    cx.step("vector", lambda e_, Ax=Ax, pAx=pAx: e_.tensor_mul(
                        Ax[:, :], pAx[:, 0:512], U512[:, :]),
                        reads=[bpAx, buf("U512")], writes=[bAx])
                    cx._hazard_waits("tensor", [bSb, bAx, bvtm, bphiq, buf("onesb")], [bpNx])
                    for i, h in enumerate(wv):
                        cx.step("tensor", lambda e_, h=h, i=i, tc=tc, pNx=pNx: e_.matmul(
                            pNx[0:65, i * 128:(i + 1) * 128],
                            S_allb[0:64, h * 128: h * 128 + 65],
                            phiq[0:64, h * G + tc * 128: h * G + (tc + 1) * 128],
                            start=True, stop=False))
                        cx.step("tensor", lambda e_, h=h, i=i, tc=tc, pNx=pNx, Ax=Ax: e_.matmul(
                            pNx[0:64, i * 128:(i + 1) * 128],
                            v_tm[:, tc * 1024 + h * 64: tc * 1024 + (h + 1) * 64],
                            Ax[:, i * 128:(i + 1) * 128], start=False, stop=False))
                        cx.step("tensor", lambda e_, i=i, pNx=pNx, Ax=Ax: e_.matmul(
                            pNx[64:65, i * 128:(i + 1) * 128], onesb[:, 0:1],
                            Ax[:, i * 128:(i + 1) * 128], start=False, stop=True))
                    cx._hazard_update(cx.s["tensor"], cx.s["tensor"].n,
                                      [bSb, bAx, bvtm, bphiq, buf("onesb")], [bpNx])
                    cx.step("vector", lambda e_, pNx=pNx: e_.tensor_scalar_add(
                        dn4[0:1, :], pNx[64:65, 0:512], 1e-4),
                        reads=[bpNx], writes=[bdn])
                    cx.step("vector", lambda e_: e_.reciprocal(rdn4[0:1, :], dn4[0:1, :]),
                            reads=[bdn], writes=[brdn])
                    cx.step("tensor", lambda e_, pBx=pBx: e_.matmul(
                        pBx[0:64, 0:512], ones[0:1, 0:64], rdn4[0:1, :], start=True, stop=True),
                        reads=[brdn], writes=[bpBx])
                    cx.step("vector", lambda e_, pBx=pBx: e_.tensor_copy(rdnb4[:, :], pBx[0:64, 0:512]),
                            reads=[bpBx], writes=[brdnb])
                    hh0 = wv[0] // 2
                    ol_v = ol_fm[rb0:rb0 + 64, :].rearrange("p (hh t) -> p hh t", t=G)[
                        :, hh0:hh0 + 4, tc * 128:(tc + 1) * 128]
                    cx.step("vector", lambda e_, ol_v=ol_v, pNx=pNx: e_.tensor_mul(
                        ol_v, pNx[0:64, 0:512].rearrange("p (i t) -> p i t", t=128),
                        rdnb4[:, :].rearrange("p (i t) -> p i t", t=128)),
                        reads=[bpNx, brdnb], writes=[bol])
                    cx._hazard_waits("tensor", [bktm, bvtm, buf("onesb")], [bpS])
                    for i, h in enumerate(wv):
                        cx.step("tensor", lambda e_, h=h, i=i, tc=tc: e_.matmul(
                            pS[0:64, i * 128: i * 128 + 64],
                            k_tm[:, tc * 1024 + h * 64: tc * 1024 + (h + 1) * 64],
                            v_tm[:, tc * 1024 + h * 64: tc * 1024 + (h + 1) * 64],
                            start=True, stop=False))
                        cx.step("tensor", lambda e_, h=h, i=i, tc=tc: e_.matmul(
                            pS[0:64, i * 128 + 64: i * 128 + 65],
                            k_tm[:, tc * 1024 + h * 64: tc * 1024 + (h + 1) * 64],
                            onesb[:, 0:1], start=False, stop=True))
                    cx._hazard_update(cx.s["tensor"], cx.s["tensor"].n,
                                      [bktm, bvtm, buf("onesb")], [bpS])
                    S_v = S_all[0:64, :].rearrange("p (h c) -> p h c", c=128)[:, wv[0]:min(wv[0] + 8, 16):2, :]
                    Sb_v = S_allb[0:64, :].rearrange("p (h c) -> p h c", c=128)[:, wv[0]:min(wv[0] + 8, 16):2, :]
                    ps_v = pS[0:64, 0:512].rearrange("p (i c) -> p i c", c=128)
                    cx.step("vector", lambda e_, S_v=S_v: e_.tensor_add(S_v, S_v, ps_v),
                            reads=[bpS], writes=[bS])
                    cx.step("vector", lambda e_, S_v=S_v, Sb_v=Sb_v: e_.tensor_copy(Sb_v, S_v),
                            reads=[bS], writes=[bSb])

            router_tails(g)

            for jc in range(DC):
                cx.dma("gpsimd", fm3(wojc, 128), fm(cx.ap["WoutT"][:, jc * 128:(jc + 1) * 128]),
                       writes=[bwo], sem=cx.s["gpsimd"])
                ps, bps = prot[jc % 4]
                cx._hazard_waits("tensor", [bol, bwo], [bps])
                for c in range(DC):
                    cx.step("tensor", lambda e_, jc=jc, c=c, ps=ps: e_.matmul(
                        ps[:, :], wojc[:, c * 128:(c + 1) * 128],
                        ol_fm[:, c * G:(c + 1) * G], start=(c == 0), stop=(c == DC - 1)))
                cx._hazard_update(cx.s["tensor"], cx.s["tensor"].n, [bol, bwo], [bps])
                cx.step("scalar", lambda e_, jc=jc, ps=ps: e_.activation(
                    a0_fm[:, jc * G:(jc + 1) * G], ps[:, :], AF.Identity, bias=bout_cols[:, jc:jc + 1]),
                    reads=[bps, buf("bout_cols")], writes=[ba0])

            ln_stats(lambda c: a0_fm[:, c * G:(c + 1) * G], ba0, p_mm1, p_mm2, bp1, bp2)
            for c in range(DC):
                sl = a0_fm[:, c * G:(c + 1) * G]
                cx.step("vector", lambda e_, sl=sl, mb_sb=mb_sb: e_.tensor_tensor(
                    sl, sl, mb_sb[:, :], ALU.subtract),
                    reads=[bmb], writes=[ba0])
                cx.step("vector", lambda e_, sl=sl, rb_sb=rb_sb: e_.tensor_tensor(
                    sl, sl, rb_sb[:, :], ALU.mult),
                    reads=[brbs], writes=[ba0])
                cx.step("vector", lambda e_, sl=sl, c=c: e_.tensor_scalar(
                    sl, sl, gamma_cols[:, c:c + 1], beta_cols[:, c:c + 1], ALU.mult, ALU.add),
                    reads=[buf("gamma_cols"), buf("beta_cols")], writes=[ba0])
                cx.step("vector", lambda e_, sl=sl, c=c: e_.tensor_add(
                    x_g[:, c * G:(c + 1) * G], x_g[:, c * G:(c + 1) * G], sl),
                    reads=[ba0], writes=[bx])
                cx.dma("sync", acc_d[c * 128:(c + 1) * 128, g * G:(g + 1) * G],
                       x_g[:, c * G:(c + 1) * G], reads=[bx, bacc_d], writes=[bacc_d], sem=cx.s["out"])

        stAp.close()
        stA1.close()
        cx.barrier(B)

        if upto >= 2:
            stB = ExitStack()
            t2_ = lambda n, s, d=F32: mktile(stB, n, s, d)
            acc_nt = t2_("acc_nt", [128, DC * 1024])
            h_sb = t2_("h_sb", [128, 32 * 1024], BF16)
            wbufs = [t2_("wbufA", [128, 8 * 512], BF16), t2_("wbufB", [128, 8 * 512], BF16)]
            cwb_sb = t2_("cwb_sb", [128, 2 * G])
            ytmp = t2_("ytmp", [128, G])
            stBp = ExitStack()
            hp = [stBp.enter_context(nc.psum_tensor("hp0", [128, G], F32)),
                  stBp.enter_context(nc.psum_tensor("hp1", [128, G], F32))]
            yp = [stBp.enter_context(nc.psum_tensor("yp0", [128, G], F32)),
                  stBp.enter_context(nc.psum_tensor("yp1", [128, G], F32))]
            cp = stBp.enter_context(nc.psum_tensor("cp", [128, G], F32))
            bacc = buf("acc_nt"); bh = buf("h_sb")
            bwbufs = [buf("wbufA"), buf("wbufB")]
            bcwb, byt = buf("cwb_sb"), buf("ytmp")
            bhp = [buf("hp0"), buf("hp1")]
            byp = [buf("yp0"), buf("yp1")]
            bcp = buf("cp")
            yp4 = [hp[0], hp[1], yp[0], yp[1]]
            byp4 = [bhp[0], bhp[1], byp[0], byp[1]]
            NTOK = 1024
            for nt in range(2):
                for c in range(DC):
                    cx.dma("sync", acc_nt[:, c * 1024:(c + 1) * 1024],
                           acc_d[c * 128:(c + 1) * 128, nt * NTOK:(nt + 1) * NTOK],
                           reads=[bacc_d], writes=[bacc], sem=cx.s["sync"])
                for ex in range(8):
                    emit_moe(cx, ex=ex, nt=nt, T_=T, G_=G, xnb=xnb, bxnb=bxnb,
                             h_sb=h_sb, bh=bh, wbufs=wbufs, bwbufs=bwbufs,
                             accsl_fn=lambda jc, ns, nt=nt: acc_nt[:, jc * 1024 + ns * G: jc * 1024 + (ns + 1) * G],
                             bacc=bacc, cw_tile=cwT_sb, bcw=bcw, sel8=sel8, bsel8=buf("sel8"),
                             cwb_sb=cwb_sb, bcwb=bcwb, ytmp=ytmp, byt=byt,
                             yts=None, byts=None, b1_cols=None, bb1=None,
                             b2_cols=None, bb2=None, hp=hp, bhp=bhp, yp4=yp4, byp4=byp4,
                             cp=cp, bcp=bcp, w1_ap=cx.ap["w1_0"], w2_ap=cx.ap["w2_0"], act="gelu")
                for c in range(DC):
                    cx.dma("sync", acc_d[c * 128:(c + 1) * 128, nt * NTOK:(nt + 1) * NTOK],
                           acc_nt[:, c * 1024:(c + 1) * 1024],
                           reads=[bacc, bacc_d], writes=[bacc_d], sem=cx.s["out"])
            stBp.close()
            stB.close()
        stA.close()
        cx.barrier(B)
        if dbg and upto >= 2:
            cx.dma("sync", cx.ap["dbg_p2"], acc_d, reads=[bacc_d], sem=cx.s["out"])
            cx.barrier(B)

        ones = sb["ones"]; onesb = sb["onesb"]; I128 = sb["I128"]; I128b = sb["I128b"]
        U128 = sb["U128"]; sel8 = sb["sel8"]
        cc_sem = Sem(e(nc.semaphore("s_cc")), "gpsimd")
        xnb2 = tile("xnb2", [128, DC * T], BF16)
        bxnb2 = buf("xnb2")

        if upto >= 3 or p3_alloc_only:
            _alloc_sbuf = p3_alloc_only in (True, "sbuf", "scratch", "xn2ab")
            _alloc_big = p3_alloc_only in (True, "sbuf", "xn2ab")
            _alloc_scr = p3_alloc_only in (True, "sbuf", "scratch")
            _alloc_psum = p3_alloc_only in (True, "psum")
            _do_memset = p3_alloc_only in (True, "sbuf", "memset")
            stC = ExitStack()
            stC1 = ExitStack()
            stCp = ExitStack()
            if (not p3_alloc_only) or _alloc_big:
                xn2ab = mktile(stC, "xn2ab", [128, DC * T], BF16)
            t3_ = lambda n, s, d=F32: mktile(stC1, n, s, d)
            if p3_alloc_only and not _alloc_scr:
                t3_ = lambda n, s, d=F32: None
            sq_sb = t3_("sq_sb", [128, G])
            row_a = t3_("row_a", [1, G]); row_b = t3_("row_b", [1, G])
            row_c = t3_("row_c", [1, G]); row_d = t3_("row_d", [1, G])
            mb_sb = t3_("mb_sb", [128, G]); rb_sb = t3_("rb_sb", [128, G])
            s2_sb = t3_("s2_sb", [128, G]); xn_sc = t3_("xn_sc", [128, G])
            x1g = t3_("x1g", [128, DC * G])
            lg = t3_("lg", [128, 8])
            _ = None
            g_m = t3_("g_m", [128, 1]); g_e = t3_("g_e", [128, 8]); g_s = t3_("g_s", [128, 1])
            g_rs = t3_("g_rs", [128, 1]); g_p = t3_("g_p", [128, 8])
            cs_sb = t3_("cs_sb", [128, 8]); sel_sb = t3_("sel_sb", [128, 8])
            if not p3_alloc_only or _alloc_psum:
                pS1 = stCp.enter_context(nc.psum_tensor("c_pS1", [128, G], F32))
                pS2 = stCp.enter_context(nc.psum_tensor("c_pS2", [128, G], F32))
                g_ps = [stCp.enter_context(nc.psum_tensor(f"c_g{i}", [128, 8], F32)) for i in range(4)]
            bx1, bxab = buf("x1g"), buf("xn2ab")
            blg = buf("lg")
            bgm, bge, bgs, bgrs, bgp = buf("g_m"), buf("g_e"), buf("g_s"), buf("g_rs"), buf("g_p")
            bcs, bsel = buf("cs_sb"), buf("sel_sb")
            bpS1, bpS2 = buf("c_pS1"), buf("c_pS2")
            bg_ps = [buf(f"c_g{i}") for i in range(4)]

            if (not p3_alloc_only) or _do_memset:
                cx.step("vector", lambda e_: e_.memset(carry[0:1, :], 0.0), writes=[bcarry])

            for g in range(NT if not p3_alloc_only else 0):
                for c in range(DC):
                    cx.dma("sync", x1g[:, c * G:(c + 1) * G],
                           acc_d[c * 128:(c + 1) * 128, g * G:(g + 1) * G],
                           reads=[bacc_d], writes=[bx1], sem=cx.s["sync"])
                ln_stats(lambda c: x1g[:, c * G:(c + 1) * G], bx1, pS1, pS2, bpS1, bpS2,
                         want_s2=True)
                for c in range(DC):
                    cx.step("vector", lambda e_, c=c: e_.tensor_tensor(
                        xn_sc[:, :], x1g[:, c * G:(c + 1) * G], mb_sb[:, :], ALU.subtract),
                        reads=[bx1, bmb], writes=[bxn])
                    cx.step("vector", lambda e_: e_.tensor_tensor(xn_sc[:, :], xn_sc[:, :], rb_sb[:, :], ALU.mult),
                            reads=[brbs], writes=[bxn])
                    cx.step("vector", lambda e_, c=c, g=g: e_.tensor_copy(
                        xnb2[:, c * T + g * G: c * T + (g + 1) * G], xn_sc[:, :]),
                        reads=[bxn], writes=[bxnb2])
                    cx.step("vector", lambda e_, c=c: e_.tensor_add(
                        x1g[:, c * G:(c + 1) * G], x1g[:, c * G:(c + 1) * G], xn_sc[:, :]),
                        reads=[bxn], writes=[bx1])
                    cx._hazard_waits("tensor", [bxn, buf("gw_f")], [])
                    for blk in range(4):
                        if c == 0:
                            cx._hazard_waits("tensor", [], [bg_ps[blk]])
                        cx.step("tensor", lambda e_, c=c, blk=blk: e_.matmul(
                            g_ps[blk][:, 0:8], xn_sc[:, blk * 128:(blk + 1) * 128],
                            gw_f[:, c * 8:(c + 1) * 8], start=(c == 0), stop=False))
                        if c == DC - 1:
                            cx.step("tensor", lambda e_, blk=blk: e_.matmul(
                                g_ps[blk][:, 0:8], ones[0:1, :], gb_row[0:1, :],
                                start=False, stop=True),
                                reads=[buf("ones"), buf("gb_row")])
                            cx._hazard_update(cx.s["tensor"], cx.s["tensor"].n,
                                              [bxn, buf("gw_f")], [bg_ps[blk]])
                        else:
                            bg_ps[blk].readers.append((cx.s["tensor"], cx.s["tensor"].n))
                            bxn.readers.append((cx.s["tensor"], cx.s["tensor"].n))
                    cx.step("vector", lambda e_: e_.tensor_tensor(xn_sc[:, :], xn_sc[:, :], s2_sb[:, :], ALU.mult),
                            reads=[bs2], writes=[bxn])
                    cx.step("vector", lambda e_, c=c, g=g: e_.tensor_scalar(
                        xn2ab[:, c * T + g * G: c * T + (g + 1) * G], xn_sc[:, :],
                        lng_cols[:, c:c + 1], lnb_cols[:, c:c + 1], ALU.mult, ALU.add),
                        reads=[bxn, buf("lng_cols"), buf("lnb_cols")], writes=[bxab])
                for c in range(DC):
                    cx.dma("sync", acc_d[c * 128:(c + 1) * 128, g * G:(g + 1) * G],
                           x1g[:, c * G:(c + 1) * G], reads=[bx1, bacc_d], writes=[bacc_d],
                           sem=cx.s["out"])
                for blk in range(4):
                    tl = g * 4 + blk
                    cx.step("vector", lambda e_, blk=blk: e_.tensor_copy(lg[:, :], g_ps[blk][:, 0:8]),
                            reads=[bg_ps[blk]], writes=[blg])
                    cx.step("vector", lambda e_: e_.reduce_max(g_m[:, 0:1], lg[:, :], axis=AX.X, negate=True),
                            reads=[blg], writes=[bgm])
                    cx.step("scalar", lambda e_: e_.activation(
                        g_e[:, :], lg[:, :], AF.Exp, bias=g_m[:, 0:1], accum_out=g_s[:, 0:1]),
                        reads=[blg, bgm], writes=[bge, bgs])
                    cx.step("vector", lambda e_: e_.reciprocal(g_rs[:, 0:1], g_s[:, 0:1]),
                            reads=[bgs], writes=[bgrs])
                    cx.step("vector", lambda e_: e_.tensor_scalar_mul(g_p[:, :], g_e[:, :], g_rs[:, 0:1]),
                            reads=[bge, bgrs], writes=[bgp])
                    cx.step("vector", lambda e_, tl=tl: e_.reduce_max(
                        tp_all[:, tl:tl + 1], g_p[:, :], axis=AX.X),
                        reads=[bgp], writes=[btp])
                    cx.step("vector", lambda e_, tl=tl: e_.tensor_scalar(
                        oh_all[:, tl * 8:(tl + 1) * 8], g_p[:, :], tp_all[:, tl:tl + 1], None, ALU.is_ge),
                        reads=[bgp, btp], writes=[boh])
                    cx._hazard_waits("tensor", [boh, buf("U128"), bcarry, buf("ones")], [bpS1])
                    cx.step("tensor", lambda e_, tl=tl: e_.matmul(
                        pS1[:, 0:8], U128[:, :], oh_all[:, tl * 8:(tl + 1) * 8], start=True, stop=False))
                    cx.step("tensor", lambda e_: e_.matmul(
                        pS1[:, 0:8], ones[0:1, :], carry[0:1, :], start=False, stop=True))
                    cx._hazard_update(cx.s["tensor"], cx.s["tensor"].n,
                                      [boh, buf("U128"), bcarry, buf("ones")], [bpS1])
                    cx.step("vector", lambda e_: e_.tensor_copy(cs_sb[:, :], pS1[:, 0:8]),
                            reads=[bpS1], writes=[bcs])
                    cx.step("vector", lambda e_, tl=tl: e_.tensor_mul(
                        sel_sb[:, :], oh_all[:, tl * 8:(tl + 1) * 8], cs_sb[:, :]),
                        reads=[boh, bcs], writes=[bsel])
                    cx.step("vector", lambda e_, tl=tl: e_.reduce_sum(
                        pos_all[:, tl:tl + 1], sel_sb[:, :], axis=AX.X),
                        reads=[bsel], writes=[bpos])
                    cx.step("tensor", lambda e_: e_.matmul(pS2[0:1, 0:8], I128[:, 127:128], cs_sb[:, :],
                                                           start=True, stop=True),
                            reads=[buf("I128"), bcs], writes=[bpS2])
                    cx.step("vector", lambda e_: e_.tensor_copy(carry[0:1, :], pS2[0:1, 0:8]),
                            reads=[bpS2], writes=[bcarry])
            if dbg:
                for c in range(DC):
                    cx.dma("sync", cx.ap["dbg_xnb2"][c * 128:(c + 1) * 128, :],
                           xnb2[:, c * T:(c + 1) * T], reads=[bxnb2], sem=cx.s["out"])
            if p3_alloc_only:
                stCp.close(); stC1.close(); stC.close()
                cx.finish(block)
                return nc
            cx.dma("gpsimd", cc_in.ap(), carry[0:1, :], reads=[bcarry], sem=cx.s["gpsimd"])
            gn = cx.s["gpsimd"].n
            cx.q["gpsimd"].append(lambda e_, t=gn: e_.wait_ge(cx.s["gpsimd"].h, t))

            def do_cc(e_):
                return e_.collective_compute(
                    "AllGather", ALU.bypass, replica_groups=[list(range(8))],
                    ins=[cc_in.ap().opt()], outs=[cc_out.ap().opt()])
            cx.q["gpsimd"].append(lambda e_: do_cc(e_).then_inc(cc_sem.h))
            cc_sem.n += 1
            stCp.close()
            stC1.close()
            cx.barrier(B, exclude=())

        if upto >= 4:
            stD = ExitStack()
            t4_ = lambda n, s, d=F32: mktile(stD, n, s, d)
            out_all = t4_("out_all", [128, 4 * T], BF16)
            maskTW = t4_("maskTW", [128, 7 * 128])
            cx.dma("sync", maskTW[:, :], cx.ap["maskTW"], writes=[buf("maskTW")], sem=cx.s["sync"])
            cos_h = t4_("cos_h", [128, T]); sin_h = t4_("sin_h", [128, T])
            qr = t4_("qr", [128, G]); t1 = t4_("t1", [128, G])
            q_hd = t4_("q_hd", [128, T], BF16); k_hd = t4_("k_hd", [128, T], BF16)
            v_tm4 = t4_("v_tm4", [128, 16 * 128], BF16)
            PT = t4_("PT", [128, 16 * G], BF16)
            PgT = t4_("PgT", [128, 16 * G], BF16)
            om = t4_("om", [128, G]); og = t4_("og", [128, G])
            wqh = t4_("wqh", [128, DC * 128], BF16); wkh = t4_("wkh", [128, DC * 128], BF16)
            wvh = t4_("wvh", [128, DC * 128], BF16)
            wo4 = t4_("wo4", [128, DC * 128], BF16)
            nm_all = t4_("nm_all", [128, 16]); nm_grp = t4_("nm_grp", [128, 8])
            nm_g4 = t4_("nm_g4", [128, 4])
            mT_sb = t4_("mT_sb", [8, 128]); mTr = t4_("mTr", [8, 1])
            negm_cols = t4_("negm_cols", [128, 4])
            qp_row = t4_("qp_row", [1, T]); kp_row = t4_("kp_row", [1, T])
            gq_mx = t4_("gq_mx", [1, 8]); gq_mn = t4_("gq_mn", [1, 8])
            gk_mx = t4_("gk_mx", [1, 8]); gk_mn = t4_("gk_mn", [1, 8])
            mg_row = t4_("mg_row", [1, 8]); mg4 = t4_("mg4", [1, 8])
            mg_col = t4_("mg_col", [8, 1]); negmg_cols = t4_("negmg_cols", [128, 4])
            Zrow = t4_("Zrow", [1, G]); Zgrow = t4_("Zgrow", [1, G])
            rZrow = Zrow; rZgrow = Zgrow
            rZb = t4_("rZb", [128, G]); rZgb = t4_("rZgb", [128, G])
            hs_col = t4_("hs_col", [128, 1])
            stDp = ExitStack()
            pS1 = stDp.enter_context(nc.psum_tensor("d_pS1", [128, G], F32))
            pS2 = stDp.enter_context(nc.psum_tensor("d_pS2", [128, G], F32))
            pT = stDp.enter_context(nc.psum_tensor("d_pT", [128, G], F32))
            pT2 = stDp.enter_context(nc.psum_tensor("d_pT2", [128, G], F32))
            pZ = stDp.enter_context(nc.psum_tensor("d_pZ", [1, G], F32))
            pZg = stDp.enter_context(nc.psum_tensor("d_pZg", [1, G], F32))
            pAV = stDp.enter_context(nc.psum_tensor("d_pAV", [128, G], F32))
            pAVg = stDp.enter_context(nc.psum_tensor("d_pAVg", [128, G], F32))

            cx.dma("sync", cos_h[:, :], cx.ap["cos_h"], writes=[buf("cos_h")], sem=cx.s["sync"])
            cx.dma("sync", sin_h[:, :], cx.ap["sin_h"], writes=[buf("sin_h")], sem=cx.s["sync"])
            sq_sb = t4_("sq4", [128, 1])
            bsq = buf("sq4")

            boa = buf("out_all")
            bqr, bt1, bqh, bkh, bvtm = buf("qr"), buf("t1"), buf("q_hd"), buf("k_hd"), buf("v_tm4")
            bPT, bPgT, bom, bog = buf("PT"), buf("PgT"), buf("om"), buf("og")
            bwq, bwk, bwv, bwo4 = buf("wqh"), buf("wkh"), buf("wvh"), buf("wo4")

            bnm, bnmg = buf("nm_all"), buf("nm_grp")
            bnm4 = buf("nm_g4")
            bmT, bmTr, bnmc = buf("mT_sb"), buf("mTr"), buf("negm_cols")
            bqp, bkp = buf("qp_row"), buf("kp_row")
            bgq_mx, bgq_mn = buf("gq_mx"), buf("gq_mn")
            bgk_mx, bgk_mn = buf("gk_mx"), buf("gk_mn")
            bmgr, bmg4, bmgc, bnmgc = buf("mg_row"), buf("mg4"), buf("mg_col"), buf("negmg_cols")
            bZ, bZg = buf("Zrow"), buf("Zgrow")
            brZ, brZg = bZ, bZg
            bZb, bZgb = buf("rZb"), buf("rZgb")
            bhsc = buf("hs_col")
            bpS1, bpS2 = buf("d_pS1"), buf("d_pS2")
            bpT, bpT2 = buf("d_pT"), buf("d_pT2")
            bpZ, bpZg = buf("d_pZ"), buf("d_pZg")
            bpAV, bpAVg = buf("d_pAV"), buf("d_pAVg")
            bxab = buf("xn2ab")

            SC = 0.08838834764831845
            for hh in range(8):
                half4, h = divmod(hh, 4)
                h = hh
                cx.dma("gpsimd", fm3(wqh, 128), fm(cx.ap["WqT"][:, h * 128:(h + 1) * 128]),
                       writes=[bwq], sem=cx.s["gpsimd"])
                cx.dma("gpsimd", fm3(wkh, 128), fm(cx.ap["WkT"][:, h * 128:(h + 1) * 128]),
                       writes=[bwk], sem=cx.s["gpsimd"])
                cx.dma("gpsimd", fm3(wvh, 128), fm(cx.ap["WvT"][:, h * 128:(h + 1) * 128]),
                       writes=[bwv], sem=cx.s["gpsimd"])
                prot4 = [(pS1, bpS1), (pS2, bpS2), (pT, bpT), (pT2, bpT2)]
                for which, wb, bwb, dst, bdst, scl in (
                        ("q", wqh, bwq, q_hd, bqh, SC), ("k", wkh, bwk, k_hd, bkh, 1.0)):
                    for g in range(NT):
                        ps, bps = prot4[g % 4]
                        cx._hazard_waits("tensor", [bwb, bxab], [bps])
                        for c in range(DC):
                            cx.step("tensor", lambda e_, c=c, g=g, wb=wb, ps=ps: e_.matmul(
                                ps[:, :], wb[:, c * 128:(c + 1) * 128],
                                xn2ab[:, c * T + g * G: c * T + (g + 1) * G],
                                start=(c == 0), stop=(c == DC - 1)))
                        cx._hazard_update(cx.s["tensor"], cx.s["tensor"].n, [bwb, bxab], [bps])
                        if scl == 1.0:
                            cx.step("scalar", lambda e_, ps=ps: e_.activation(
                                qr[:, :], ps[:, :], AF.Identity),
                                reads=[bps], writes=[bqr])
                        else:
                            cx.step("scalar", lambda e_, ps=ps, scl=scl: e_.activation(
                                qr[:, :], ps[:, :], AF.Identity, scale=scl),
                                reads=[bps], writes=[bqr])
                        ch = cos_h[:, g * G:(g + 1) * G]
                        sh_lo = sin_h[0:64, g * G:(g + 1) * G]
                        sh_hi = sin_h[64:128, g * G:(g + 1) * G]
                        cx.step("vector", lambda e_, sh_hi=sh_hi: e_.tensor_mul(
                            t1[0:64, :], qr[64:128, :], sh_hi),
                            reads=[bqr, buf("sin_h")], writes=[bt1])
                        cx.step("vector", lambda e_, sh_lo=sh_lo: e_.tensor_mul(
                            t1[64:128, :], qr[0:64, :], sh_lo),
                            reads=[bqr, buf("sin_h")], writes=[bt1])
                        cx.step("vector", lambda e_, ch=ch: e_.tensor_mul(qr[:, :], qr[:, :], ch),
                                reads=[buf("cos_h")], writes=[bqr])
                        cx.step("vector", lambda e_, dst=dst, g=g: e_.tensor_tensor(
                            dst[0:64, g * G:(g + 1) * G], qr[0:64, :], t1[0:64, :], ALU.subtract),
                            reads=[bqr, bt1], writes=[bdst])
                        cx.step("vector", lambda e_, dst=dst, g=g: e_.tensor_add(
                            dst[64:128, g * G:(g + 1) * G], qr[64:128, :], t1[64:128, :]),
                            reads=[bqr, bt1], writes=[bdst])
                for tt in range(16):
                    ps, bps = prot4[tt % 4]
                    cx._hazard_waits("tensor", [bwv, bxab], [bps])
                    for c in range(DC):
                        cx.step("tensor", lambda e_, c=c, tt=tt, ps=ps: e_.matmul(
                            ps[:, 0:128], xn2ab[:, c * T + tt * 128: c * T + (tt + 1) * 128],
                            wvh[:, c * 128:(c + 1) * 128], start=(c == 0), stop=(c == DC - 1)))
                    cx._hazard_update(cx.s["tensor"], cx.s["tensor"].n, [bwv, bxab], [bps])
                    cx.step("vector", lambda e_, tt=tt, ps=ps: e_.tensor_copy(
                        v_tm4[:, tt * 128:(tt + 1) * 128], ps[:, 0:128]),
                        reads=[bps], writes=[bvtm])
                for src_hd, bsrc, dstrow, bdstrow, scl in (
                        (q_hd, bqh, qp_row, bqp, 11.313708498984761),
                        (k_hd, bkh, kp_row, bkp, None)):
                    for g in range(NT):
                        ps, bps = (pS1, bpS1) if g % 2 == 0 else (pS2, bpS2)
                        cx.step("tensor", lambda e_, h=h, g=g, src_hd=src_hd, ps=ps: e_.matmul(
                            ps[0:1, 0:G], hd3b[0:3, h:h + 1], src_hd[0:3, g * G:(g + 1) * G],
                            start=True, stop=True),
                            reads=[buf("hd3b"), bsrc], writes=[bps])
                        if scl is not None:
                            cx.step("scalar", lambda e_, g=g, ps=ps, dstrow=dstrow, scl=scl: e_.activation(
                                dstrow[0:1, g * G:(g + 1) * G], ps[0:1, 0:G], AF.Identity, scale=scl),
                                reads=[bps], writes=[bdstrow])
                        else:
                            cx.step("scalar", lambda e_, g=g, ps=ps, dstrow=dstrow: e_.activation(
                                dstrow[0:1, g * G:(g + 1) * G], ps[0:1, 0:G], AF.Identity),
                                reads=[bps], writes=[bdstrow])
                for it in range(16):
                    njb = it // 4 + 1
                    for jb in range(njb):
                        ps, bps = (pS1, bpS1) if jb % 2 == 0 else (pS2, bpS2)
                        cx.step("tensor", lambda e_, it=it, jb=jb, ps=ps: e_.matmul(
                            ps[:, :], q_hd[:, it * 128:(it + 1) * 128],
                            k_hd[:, jb * G:(jb + 1) * G], start=True, stop=True),
                            reads=[bqh, bkh], writes=[bps])
                        if jb == 0:
                            cx.step("vector", lambda e_, it=it, ps=ps: e_.reduce_max(
                                nm_all[:, it:it + 1], ps[:, :], axis=AX.X, negate=True),
                                reads=[bps], writes=[bnm])
                        else:
                            cx.step("vector", lambda e_, ps=ps: e_.reduce_max(
                                sq_sb[:, 0:1], ps[:, :], axis=AX.X, negate=True),
                                reads=[bps], writes=[bsq])
                            cx.step("vector", lambda e_, it=it: e_.tensor_tensor(
                                nm_all[:, it:it + 1], nm_all[:, it:it + 1], sq_sb[:, 0:1], ALU.min),
                                reads=[bsq], writes=[bnm])
                cx.step("vector", lambda e_: e_.tensor_tensor(
                    nm_grp[:, :], nm_all[:, 0:16:2], nm_all[:, 1:16:2], ALU.min),
                    reads=[bnm], writes=[bnmg])
                cx.step("vector", lambda e_: e_.tensor_tensor(
                    nm_g4[:, :], nm_grp[:, 0:8:2], nm_grp[:, 1:8:2], ALU.min),
                    reads=[bnmg], writes=[bnm4])
                cx.step("tensor", lambda e_: e_.matmul(pS1[0:4, 0:128], nm_g4[:, :], I128[:, :],
                                                       start=True, stop=True),
                        reads=[bnm4, buf("I128")], writes=[bpS1])
                cx.step("vector", lambda e_: e_.tensor_copy(mT_sb[0:4, :], pS1[0:4, 0:128]),
                        reads=[bpS1], writes=[bmT])
                cx.step("vector", lambda e_: e_.tensor_reduce(mTr[0:4, 0:1], mT_sb[0:4, :], AX.X, ALU.min),
                        reads=[bmT], writes=[bmTr])
                for gp in range(4):
                    cx.step("tensor", lambda e_, gp=gp: e_.matmul(
                        pS2[:, 0:1], sel8[0:4, gp * 128:(gp + 1) * 128], mTr[0:4, 0:1],
                        start=True, stop=True),
                        reads=[buf("sel8"), bmTr], writes=[bpS2])
                    cx.step("vector", lambda e_, gp=gp: e_.tensor_copy(
                        negm_cols[:, gp:gp + 1], pS2[:, 0:1]),
                        reads=[bpS2], writes=[bnmc])
                for gp in range(4):
                    cx.step("vector", lambda e_, gp=gp: e_.reduce_max(
                        gq_mx[0:1, gp:gp + 1], qp_row[0:1, gp * G:(gp + 1) * G], axis=AX.X),
                        reads=[bqp], writes=[bgq_mx])
                    cx.step("vector", lambda e_, gp=gp: e_.tensor_reduce(
                        gq_mn[0:1, gp:gp + 1], qp_row[0:1, gp * G:(gp + 1) * G], AX.X, ALU.min),
                        reads=[bqp], writes=[bgq_mn])
                    cx.step("vector", lambda e_, gp=gp: e_.reduce_max(
                        gk_mx[0:1, gp:gp + 1], kp_row[0:1, gp * G:(gp + 1) * G], axis=AX.X),
                        reads=[bkp], writes=[bgk_mx])
                    cx.step("vector", lambda e_, gp=gp: e_.tensor_reduce(
                        gk_mn[0:1, gp:gp + 1], kp_row[0:1, gp * G:(gp + 1) * G], AX.X, ALU.min),
                        reads=[bkp], writes=[bgk_mn])
                for shift in (1, 2):
                    cx.step("vector", lambda e_, shift=shift: e_.tensor_tensor(
                        gk_mx[0:1, shift:4], gk_mx[0:1, shift:4], gk_mx[0:1, 0:4 - shift], ALU.max),
                        writes=[bgk_mx])
                    cx.step("vector", lambda e_, shift=shift: e_.tensor_tensor(
                        gk_mn[0:1, shift:4], gk_mn[0:1, shift:4], gk_mn[0:1, 0:4 - shift], ALU.min),
                        writes=[bgk_mn])
                cx.step("vector", lambda e_: e_.tensor_mul(mg_row[0:1, 0:4], gq_mx[0:1, 0:4], gk_mx[0:1, 0:4]),
                        reads=[bgq_mx, bgk_mx], writes=[bmgr])
                for a_t, a_b, b_t, b_b in ((gq_mx, bgq_mx, gk_mn, bgk_mn),
                                           (gq_mn, bgq_mn, gk_mx, bgk_mx),
                                           (gq_mn, bgq_mn, gk_mn, bgk_mn)):
                    cx.step("vector", lambda e_, a_t=a_t, b_t=b_t: e_.tensor_mul(
                        mg4[0:1, 0:4], a_t[0:1, 0:4], b_t[0:1, 0:4]),
                        reads=[a_b, b_b], writes=[bmg4])
                    cx.step("vector", lambda e_: e_.tensor_tensor(
                        mg_row[0:1, 0:4], mg_row[0:1, 0:4], mg4[0:1, 0:4], ALU.max),
                        reads=[bmg4], writes=[bmgr])
                cx.step("vector", lambda e_: e_.tensor_scalar_mul(mg_row[0:1, 0:4], mg_row[0:1, 0:4], -1.0),
                        writes=[bmgr])
                cx.step("tensor", lambda e_: e_.matmul(pS1[0:4, 0:1], mg_row[0:1, 0:4], ones[0:1, 0:1],
                                                       start=True, stop=True),
                        reads=[bmgr, buf("ones")], writes=[bpS1])
                cx.step("vector", lambda e_: e_.tensor_copy(mg_col[0:4, 0:1], pS1[0:4, 0:1]),
                        reads=[bpS1], writes=[bmgc])
                for gp in range(4):
                    cx.step("tensor", lambda e_, gp=gp: e_.matmul(
                        pS2[:, 0:1], sel8[0:4, gp * 128:(gp + 1) * 128], mg_col[0:4, 0:1],
                        start=True, stop=True),
                        reads=[buf("sel8"), bmgc], writes=[bpS2])
                    cx.step("vector", lambda e_, gp=gp: e_.tensor_copy(
                        negmg_cols[:, gp:gp + 1], pS2[:, 0:1]),
                        reads=[bpS2], writes=[bnmgc])
                cx.step("tensor", lambda e_, h=h: e_.matmul(pS1[:, 0:1], ones[0:1, :], hs_row[0:1, h:h + 1],
                                                            start=True, stop=True),
                        reads=[buf("hs_row"), buf("ones")], writes=[bpS1])
                cx.step("vector", lambda e_: e_.tensor_copy(hs_col[:, 0:1], pS1[:, 0:1]),
                        reads=[bpS1], writes=[bhsc])
                for gp in range(4):
                    njt = gp * 4 + 4
                    mrot = [(pT, bpT), (pS1, bpS1)]
                    grot = [(pT2, bpT2), (pS2, bpS2)]
                    for jt in range(njt):
                        mp, bmp = mrot[jt % 2]
                        gpp, bgpp = grot[jt % 2]
                        cx.step("tensor", lambda e_, jt=jt, gp=gp, mp=mp: e_.matmul(
                            mp[:, :], k_hd[:, jt * 128:(jt + 1) * 128],
                            q_hd[:, gp * G:(gp + 1) * G], start=True, stop=True),
                            reads=[bkh, bqh], writes=[bmp])
                        if jt >= njt - 4:
                            off = (3 - (jt - (njt - 4))) * 128
                            cx.step("vector", lambda e_, off=off, mp=mp: e_.tensor_add(
                                mp[:, :], mp[:, :], maskTW[:, off:off + G]),
                                reads=[buf("maskTW")], writes=[bmp])
                        cx.step("scalar", lambda e_, jt=jt, gp=gp, mp=mp: e_.activation(
                            PT[:, jt * G:(jt + 1) * G], mp[:, :], AF.Exp,
                            bias=negm_cols[:, gp:gp + 1]),
                            reads=[bmp, bnmc], writes=[bPT])
                        cx.step("tensor", lambda e_, jt=jt, gp=gp, gpp=gpp: e_.matmul(
                            gpp[:, :], kp_row[0:1, jt * 128:(jt + 1) * 128],
                            qp_row[0:1, gp * G:(gp + 1) * G], start=True, stop=True),
                            reads=[bqp, bkp], writes=[bgpp])
                        if jt >= njt - 4:
                            off = (3 - (jt - (njt - 4))) * 128
                            cx.step("vector", lambda e_, off=off, gpp=gpp: e_.tensor_add(
                                gpp[:, :], gpp[:, :], maskTW[:, off:off + G]),
                                reads=[buf("maskTW")], writes=[bgpp])
                        cx.step("scalar", lambda e_, jt=jt, gp=gp, gpp=gpp: e_.activation(
                            PgT[:, jt * G:(jt + 1) * G], gpp[:, :], AF.Exp,
                            bias=negmg_cols[:, gp:gp + 1]),
                            reads=[bgpp, bnmgc], writes=[bPgT])
                    cx._hazard_waits("tensor", [bPT, buf("onesb")], [bpZ])
                    for jt in range(njt):
                        cx.step("tensor", lambda e_, jt=jt: e_.matmul(
                            pZ[0:1, :], onesb[:, 0:1], PT[:, jt * G:(jt + 1) * G],
                            start=(jt == 0), stop=(jt == njt - 1)))
                    cx._hazard_update(cx.s["tensor"], cx.s["tensor"].n, [bPT, buf("onesb")], [bpZ])
                    cx._hazard_waits("tensor", [bPgT, buf("onesb")], [bpZg])
                    for jt in range(njt):
                        cx.step("tensor", lambda e_, jt=jt: e_.matmul(
                            pZg[0:1, :], onesb[:, 0:1], PgT[:, jt * G:(jt + 1) * G],
                            start=(jt == 0), stop=(jt == njt - 1)))
                    cx._hazard_update(cx.s["tensor"], cx.s["tensor"].n, [bPgT, buf("onesb")], [bpZg])
                    cx._hazard_waits("tensor", [bPT, bvtm], [bpAV])
                    for jt in range(njt):
                        cx.step("tensor", lambda e_, jt=jt: e_.matmul(
                            pAV[:, :], v_tm4[:, jt * 128:(jt + 1) * 128], PT[:, jt * G:(jt + 1) * G],
                            start=(jt == 0), stop=(jt == njt - 1)))
                    cx._hazard_update(cx.s["tensor"], cx.s["tensor"].n, [bPT, bvtm], [bpAV])
                    cx._hazard_waits("tensor", [bPgT, bvtm], [bpAVg])
                    for jt in range(njt):
                        cx.step("tensor", lambda e_, jt=jt: e_.matmul(
                            pAVg[:, :], v_tm4[:, jt * 128:(jt + 1) * 128], PgT[:, jt * G:(jt + 1) * G],
                            start=(jt == 0), stop=(jt == njt - 1)))
                    cx._hazard_update(cx.s["tensor"], cx.s["tensor"].n, [bPgT, bvtm], [bpAVg])
                    cx.step("vector", lambda e_: e_.tensor_copy(Zrow[0:1, :], pZ[0:1, :]),
                            reads=[bpZ], writes=[bZ])
                    cx.step("vector", lambda e_: e_.reciprocal(rZrow[0:1, :], Zrow[0:1, :]),
                            reads=[bZ], writes=[brZ])
                    cx.step("vector", lambda e_: e_.tensor_copy(Zgrow[0:1, :], pZg[0:1, :]),
                            reads=[bpZg], writes=[bZg])
                    cx.step("vector", lambda e_: e_.reciprocal(rZgrow[0:1, :], Zgrow[0:1, :]),
                            reads=[bZg], writes=[brZg])
                    cx.step("tensor", lambda e_: e_.matmul(pT[:, :], ones[0:1, :], rZrow[0:1, :],
                                                           start=True, stop=True),
                            reads=[brZ, buf("ones")], writes=[bpT])
                    cx.step("vector", lambda e_: e_.tensor_copy(rZb[:, :], pT[:, :]),
                            reads=[bpT], writes=[bZb])
                    cx.step("tensor", lambda e_: e_.matmul(pT2[:, :], ones[0:1, :], rZgrow[0:1, :],
                                                           start=True, stop=True),
                            reads=[brZg, buf("ones")], writes=[bpT2])
                    cx.step("vector", lambda e_: e_.tensor_copy(rZgb[:, :], pT2[:, :]),
                            reads=[bpT2], writes=[bZgb])
                    cx.step("vector", lambda e_: e_.tensor_mul(om[:, :], pAV[:, :], rZb[:, :]),
                            reads=[bpAV, bZb], writes=[bom])
                    cx.step("vector", lambda e_: e_.tensor_mul(og[:, :], pAVg[:, :], rZgb[:, :]),
                            reads=[bpAVg, bZgb], writes=[bog])
                    cx.step("vector", lambda e_: e_.tensor_tensor(og[:, :], og[:, :], om[:, :], ALU.subtract),
                            reads=[bom], writes=[bog])
                    cx.step("vector", lambda e_: e_.tensor_scalar_mul(og[:, :], og[:, :], hs_col[:, 0:1]),
                            reads=[bhsc], writes=[bog])
                    cx.step("vector", lambda e_, h=h, gp=gp: e_.tensor_tensor(
                        out_all[:, (h % 4) * T + gp * G: (h % 4) * T + (gp + 1) * G],
                        om[:, :], og[:, :], ALU.add),
                        reads=[bom, bog], writes=[boa])
                if h % 4 == 3:
                    half = h // 4
                    for jc in range(DC):
                        cx.dma("gpsimd",
                               wo4[:, 0:512].rearrange("p (c t) -> p c t", t=128),
                               fm(cx.ap["WoT"][half * 512:(half + 1) * 512, jc * 128:(jc + 1) * 128]),
                               writes=[bwo4], sem=cx.s["gpsimd"])
                        for g in range(NT):
                            ps, bps = (pS1, bpS1) if g % 2 == 0 else (pS2, bpS2)
                            cx._hazard_waits("tensor", [boa, bwo4], [bps])
                            for c in range(4):
                                cx.step("tensor", lambda e_, jc=jc, c=c, g=g, ps=ps: e_.matmul(
                                    ps[:, :], wo4[:, c * 128:(c + 1) * 128],
                                    out_all[:, c * T + g * G: c * T + (g + 1) * G],
                                    start=(c == 0), stop=(c == 3)))
                            cx._hazard_update(cx.s["tensor"], cx.s["tensor"].n, [boa, bwo4], [bps])
                            cx.dma("sync", qr[:, :], acc_d[jc * 128:(jc + 1) * 128, g * G:(g + 1) * G],
                                   reads=[bacc_d], writes=[bqr], sem=cx.s["sync"])
                            cx.step("vector", lambda e_, ps=ps: e_.tensor_tensor(
                                qr[:, :], qr[:, :], ps[:, :], ALU.add),
                                reads=[bps], writes=[bqr])
                            cx.dma("sync", acc_d[jc * 128:(jc + 1) * 128, g * G:(g + 1) * G], qr[:, :],
                                   reads=[bqr, bacc_d], writes=[bacc_d], sem=cx.s["out"])
            stDp.close()
            stD.close()
            if upto >= 3:
                stC.close()
            cx.barrier(B)

        if upto >= 5:
            stE = ExitStack()
            t5_ = lambda n, s, d=F32: mktile(stE, n, s, d)
            acc_f = t5_("acc_f", [128, DC * T])
            h_sb5 = t5_("h_sb5", [128, 32 * 1024], BF16)
            wbufs5 = [t5_("wbuf5A", [128, 8 * 512], BF16), t5_("wbuf5B", [128, 8 * 512], BF16)]
            b1_cols = t5_("b1_cols", [128, 8 * 32]); b2_cols = t5_("b2_cols", [128, 8 * 8])
            cwb5 = t5_("cwb5", [128, 2 * G])
            ytmp5 = t5_("ytmp5", [128, G]); yts5 = t5_("yts5", [128, G])
            offs_row = t5_("offs_row", [1, 8]); offs_b = t5_("offs_b", [128, 8])
            po = t5_("po", [128, 1]); posg = t5_("posg", [128, 1])
            keep = t5_("keep", [128, 1]); cw1 = t5_("cw1", [128, 1])
            cwf = t5_("cwf", [128, 8]); sel_sb5 = t5_("sel_sb5", [128, 8])
            cwst5 = t5_("cwst5", [8, 128])
            stEp = ExitStack()
            hp = [stEp.enter_context(nc.psum_tensor("e_hp0", [128, G], F32)),
                  stEp.enter_context(nc.psum_tensor("e_hp1", [128, G], F32))]
            yp = [stEp.enter_context(nc.psum_tensor("e_yp0", [128, G], F32)),
                  stEp.enter_context(nc.psum_tensor("e_yp1", [128, G], F32))]
            cp = stEp.enter_context(nc.psum_tensor("e_cp", [128, G], F32))
            yp4 = [hp[0], hp[1], yp[0], yp[1]]
            bacc = buf("acc_f"); bh = buf("h_sb5")
            bwbufs5 = [buf("wbuf5A"), buf("wbuf5B")]
            bcwb, byt, byts = buf("cwb5"), buf("ytmp5"), buf("yts5")
            boffr, boffb = buf("offs_row"), buf("offs_b")
            bpo, bposg, bkeep, bcw1, bcwf = buf("po"), buf("posg"), buf("keep"), buf("cw1"), buf("cwf")
            bsel5, bcwst5 = buf("sel_sb5"), buf("cwst5")
            bhp = [buf("e_hp0"), buf("e_hp1")]
            byp = [buf("e_yp0"), buf("e_yp1")]
            bcp = buf("e_cp")
            byp4 = [bhp[0], bhp[1], byp[0], byp[1]]

            cx.dma("sync", b1_cols[:, :], cx.ap["b1_cols"], writes=[buf("b1_cols")], sem=cx.s["sync"])
            cx.dma("sync", b2_cols[:, :], cx.ap["b2_cols"], writes=[buf("b2_cols")], sem=cx.s["sync"])

            cx.q["gpsimd"].append(lambda e_: e_.wait_ge(cc_sem.h, 1))
            cx.dma("gpsimd", ag_sb[:, :], cc_out.ap(), writes=[bag], sem=cx.s["gpsimd"])
            cx.step("tensor", lambda e_: e_.matmul(cp[0:1, 0:8], prefmask[:, 0:1], ag_sb[:, :],
                                                   start=True, stop=True),
                    reads=[buf("prefmask"), bag], writes=[bcp])
            cx.step("vector", lambda e_: e_.tensor_copy(offs_row[0:1, :], cp[0:1, 0:8]),
                    reads=[bcp], writes=[boffr])
            cx.step("tensor", lambda e_: e_.matmul(cp[:, 0:8], ones[0:1, :], offs_row[0:1, :],
                                                   start=True, stop=True),
                    reads=[boffr, buf("ones")], writes=[bcp])
            cx.step("vector", lambda e_: e_.tensor_copy(offs_b[:, :], cp[:, 0:8]),
                    reads=[bcp], writes=[boffb])
            for blk in range(16):
                cx.step("vector", lambda e_, blk=blk: e_.tensor_mul(
                    sel_sb5[:, :], oh_all[:, blk * 8:(blk + 1) * 8], offs_b[:, :]),
                    reads=[boh, boffb], writes=[bsel5])
                cx.step("vector", lambda e_: e_.reduce_sum(po[:, 0:1], sel_sb5[:, :], axis=AX.X),
                        reads=[bsel5], writes=[bpo])
                cx.step("vector", lambda e_, blk=blk: e_.tensor_add(
                    posg[:, 0:1], pos_all[:, blk:blk + 1], po[:, 0:1]),
                    reads=[bpos, bpo], writes=[bposg])
                cx.step("vector", lambda e_: e_.tensor_single_scalar(
                    keep[:, 0:1], posg[:, 0:1], 2560.0, ALU.is_le),
                    reads=[bposg], writes=[bkeep])
                cx.step("vector", lambda e_, blk=blk: e_.tensor_mul(
                    cw1[:, 0:1], keep[:, 0:1], tp_all[:, blk:blk + 1]),
                    reads=[bkeep, btp], writes=[bcw1])
                cx.step("vector", lambda e_, blk=blk: e_.tensor_scalar_mul(
                    cwf[:, :], oh_all[:, blk * 8:(blk + 1) * 8], cw1[:, 0:1]),
                    reads=[boh, bcw1], writes=[bcwf])
                cx.step("tensor", lambda e_: e_.transpose(cp[0:8, 0:128], cwf[:, :], I128[:, :]),
                        reads=[bcwf, buf("I128")], writes=[bcp])
                cx.step("vector", lambda e_: e_.tensor_copy(cwst5[:, :], cp[0:8, 0:128]),
                        reads=[bcp], writes=[bcwst5])
                cx.step("vector", lambda e_, blk=blk: e_.tensor_copy(
                    cwT_sb[0:8, blk * 128:(blk + 1) * 128], cwst5[:, :]),
                    reads=[bcwst5], writes=[bcw])
            if dbg:
                cx.dma("sync", cw1_dbg, cwT_sb[:, :], reads=[bcw], sem=cx.s["out"])
            for c in range(DC):
                cx.dma("sync", acc_f[:, c * T:(c + 1) * T], acc_d[c * 128:(c + 1) * 128, :],
                       reads=[bacc_d], writes=[bacc], sem=cx.s["sync"])
            for ex in range(8):
                for nt in range(2):
                    emit_moe(cx, ex=ex, nt=nt, T_=T, G_=G, xnb=xnb2, bxnb=bxnb2,
                             h_sb=h_sb5, bh=bh, wbufs=wbufs5, bwbufs=bwbufs5,
                             accsl_fn=lambda jc, ns, nt=nt: acc_f[:, jc * T + nt * 1024 + ns * G: jc * T + nt * 1024 + (ns + 1) * G],
                             bacc=bacc, cw_tile=cwT_sb, bcw=bcw, sel8=sel8, bsel8=buf("sel8"),
                             cwb_sb=cwb5, bcwb=bcwb, ytmp=ytmp5, byt=byt,
                             yts=yts5, byts=byts, b1_cols=b1_cols, bb1=buf("b1_cols"),
                             b2_cols=b2_cols, bb2=buf("b2_cols"), hp=hp, bhp=bhp,
                             yp4=yp4, byp4=byp4, cp=cp, bcp=bcp,
                             w1_ap=cx.ap["w1_1"], w2_ap=cx.ap["w2_1"], act="relu")
            for c in range(DC):
                cx.dma("sync", cx.ap["OUT_T"][c * 128:(c + 1) * 128, :], acc_f[:, c * T:(c + 1) * T],
                       reads=[bacc], sem=cx.s["out"])
            stEp.close()
            stE.close()
        elif upto == 3:
            stC.close()

        cx.finish(block)
    return nc


def prep_fused_inputs(x_core, W, C, core):
    m = {"xT": np.ascontiguousarray(x_core.T)}
    m["WqkvT"] = np.ascontiguousarray(W["L0_attn_Wqkv"].T)
    m["bq_cols"] = colmat(W["L0_attn_bqkv"], 24)
    m["WoutT"] = np.ascontiguousarray(W["L0_attn_Wout"].T)
    m["bout_cols"] = colmat(W["L0_attn_bout"], 8)
    m["gamma_cols"] = colmat(W["L0_attn_gamma"], 8)
    m["beta_cols"] = colmat(W["L0_attn_beta"], 8)
    m["rwT"] = np.ascontiguousarray(W["L0_ffn_router_w"].T)
    m["w1_0"] = W["L0_ffn_w1"]; m["w2_0"] = W["L0_ffn_w2"]
    m["WqT"] = np.ascontiguousarray(W["L1_attn_Wq"].T)
    m["WkT"] = np.ascontiguousarray(W["L1_attn_Wk"].T)
    m["WvT"] = np.ascontiguousarray(W["L1_attn_Wv"].T)
    m["WoT"] = np.ascontiguousarray(W["L1_attn_Wo"].T)
    m["lng_cols"] = colmat(W["L1_attn_ln_g"], 8)
    m["lnb_cols"] = colmat(W["L1_attn_ln_b"], 8)
    m["hs_row"] = np.ascontiguousarray(np.asarray(W["L1_attn_head_scales"], np.float32)[None, :])
    m["hd3"] = np.ascontiguousarray(np.asarray(W["L1_attn_head_dirs"], np.float32).T)
    m["gwT"] = np.ascontiguousarray(W["L1_ffn_gate_w"].T)
    m["gb_row"] = np.ascontiguousarray(np.asarray(W["L1_ffn_gate_b"], np.float32)[None, :])
    pm = np.zeros((8, 1), np.float32); pm[:core, 0] = 1.0
    m["prefmask"] = pm
    m["w1_1"] = W["L1_ffn_w1"]; m["w2_1"] = W["L1_ffn_w2"]
    b1 = np.asarray(W["L1_ffn_b1"], np.float32)
    b2 = np.asarray(W["L1_ffn_b2"], np.float32)
    m["b1_cols"] = np.ascontiguousarray(b1.reshape(8, 32, 128).transpose(2, 0, 1).reshape(128, 8 * 32))
    m["b2_cols"] = np.ascontiguousarray(b2.reshape(8, 8, 128).transpose(2, 0, 1).reshape(128, 8 * 8))
    m["ones"] = C["ones"]; m["onesb"] = C["onesb"]
    m["I128"] = C["I128"]; m["I128b"] = C["I128b"]; m["U128"] = C["U128"]
    m["U512"] = C["U512"]
    jj, ii = np.meshgrid(np.arange(128), np.arange(256), indexing="ij")
    tri = np.where(jj <= ii - 128, 0.0, BIGNEG).astype(np.float32)
    m["maskT3"] = np.concatenate([tri, np.zeros((128, 128), np.float32)], axis=1)
    jj2, ii2 = np.meshgrid(np.arange(128), np.arange(128), indexing="ij")
    triW = np.where(jj2 <= ii2, 0.0, BIGNEG).astype(np.float32)
    m["maskTW"] = np.ascontiguousarray(np.concatenate(
        [np.full((128, 384), BIGNEG, np.float32), triW, np.zeros((128, 384), np.float32)], axis=1))
    m["sel8"] = C["sel8"]
    m["cos_h"] = C["cosf"]
    m["sin_h"] = np.ascontiguousarray(np.abs(C["sinm"]))
    return m


_BUILD_CACHE = {}


def _get(name, builder):
    if name not in _BUILD_CACHE:
        _BUILD_CACHE[name] = builder()
    return _BUILD_CACHE[name]


def kernel(**inputs):
    W = inputs
    C = host_constants()
    x = np.asarray(W["L0_attn_x"])
    cores = list(range(NCORES))
    nc = _get("fused", build_fused)
    res = run_bass_kernel_spmd(nc, [prep_fused_inputs(x[c], W, C, c) for c in cores],
                               core_ids=cores)
    out = np.stack([np.ascontiguousarray(r["OUT_T"].T) for r in res.results], axis=0)
    return out.astype(np.float32)


def kernel_multilaunch(**inputs):
    W = inputs
    C = host_constants()
    x = np.asarray(W["L0_attn_x"])
    cores = list(range(NCORES))

    nc_a = _get("l0a", build_l0a)
    res_a = run_bass_kernel_spmd(nc_a, [prep_l0a_inputs(x[c], W, C) for c in cores], core_ids=cores)

    nc_f = _get("l0f", build_l0f)
    res_f = run_bass_kernel_spmd(
        nc_f, [prep_l0f_inputs(r["xn_T"], r["acc_T"], r["cwT"], W, C) for r in res_a.results],
        core_ids=cores)

    nc_1a = _get("l1a", build_l1a)
    res_1a = run_bass_kernel_spmd(
        nc_1a, [prep_l1a_inputs(r["x1_T"], W, C) for r in res_f.results], core_ids=cores)

    nc_g = _get("l1g", build_l1g)
    res_g = run_bass_kernel_spmd(
        nc_g, [prep_l1g_inputs(res_1a.results[c]["xn2_T"], W, C, c) for c in cores], core_ids=cores)

    nc_1f = _get("l1f", build_l1f)
    res_1f = run_bass_kernel_spmd(
        nc_1f, [prep_l1f_inputs(res_1a.results[c]["xn2_T"], res_1a.results[c]["acc1_T"],
                                res_g.results[c]["cwT1"], W, C) for c in cores],
        core_ids=cores)

    out = np.stack([np.ascontiguousarray(r["out_T"].T) for r in res_1f.results], axis=0)
    return out.astype(np.float32)


# version 12
# speedup vs baseline: 1.0011x; 1.0011x over previous
import numpy as np
import ml_dtypes
import concourse.bass as bass
import concourse.mybir as mybir
from concourse.bass_utils import run_bass_kernel_spmd
from contextlib import ExitStack

F32 = mybir.dt.float32
BF16 = mybir.dt.bfloat16
AF = mybir.ActivationFunctionType
ALU = mybir.AluOpType
AX = mybir.AxisListType

NCORES = 8
T = 2048
D = 1024
DC = 8
NT = 4
G = 512
BIGNEG = -1e30

ENGS = ("sync", "gpsimd", "vector", "scalar", "tensor")


class Sem:
    def __init__(self, handle, owner, is_dma=False):
        self.h = handle
        self.owner = owner
        self.is_dma = is_dma
        self.n = 0


class Buf:

    def __init__(self, ctx):
        self.ctx = ctx
        self.w = None
        self.readers = []


class Ctx:
    def __init__(self, nc, st):
        e = st.enter_context
        self.nc = nc
        self.q = {en: [] for en in ENGS}
        self.s = {
            "sync": Sem(e(nc.semaphore("s_sync")), "sync", True),
            "out": Sem(e(nc.semaphore("s_out")), "sync", True),
            "gpsimd": Sem(e(nc.semaphore("s_gp")), "gpsimd", True),
            "ws0": Sem(e(nc.semaphore("s_ws0")), "gpsimd", True),
            "ws1": Sem(e(nc.semaphore("s_ws1")), "gpsimd", True),
            "tensor": Sem(e(nc.semaphore("s_te")), "tensor"),
            "vector": Sem(e(nc.semaphore("s_ve")), "vector"),
            "scalar": Sem(e(nc.semaphore("s_sc")), "scalar"),
        }
        self.ap = {}

    def _wait(self, eng, sem, tgt):
        if tgt > 0:
            self.q[eng].append(lambda e, s=sem.h, t=tgt: e.wait_ge(s, t))

    def _hazard_waits(self, eng, reads, writes):
        seen = set()

        def w(sem, tgt):
            key = (id(sem), tgt)
            if key in seen:
                return
            seen.add(key)
            self._wait(eng, sem, tgt)

        def wv(sem, tgt):
            w(sem, sem.n if sem.is_dma else tgt)

        for b in reads:
            if b.w is not None:
                wv(*b.w)
        for b in writes:
            if b.w is not None:
                wv(*b.w)
            for sem, tgt in b.readers:
                wv(sem, tgt)

    def _hazard_update(self, sem, tgt, reads, writes):
        for b in reads:
            b.readers.append((sem, tgt))
        for b in writes:
            b.w = (sem, tgt)
            b.readers = []

    def step(self, eng, fn, reads=(), writes=(), extra_waits=()):
        sem = self.s[eng]
        self._hazard_waits(eng, reads, writes)
        for s2, t2 in extra_waits:
            self._wait(eng, s2, t2)
        sem.n += 1
        tgt = sem.n
        self.q[eng].append(lambda e, f=fn, s=sem.h: f(e).then_inc(s, 1))
        self._hazard_update(sem, tgt, reads, writes)
        return tgt

    def dma(self, eng, out_ap, in_ap, reads=(), writes=(), sem=None, **kw):
        sem = sem or (self.s["out"] if eng == "sync" and any(writes) is False and False else None)
        sem = self.s[eng] if sem is None else sem
        self._hazard_waits(eng, reads, writes)
        sem.n += 16
        tgt = sem.n
        self.q[eng].append(lambda e, o=out_ap, i=in_ap, s=sem.h, k=dict(kw):
                           e.dma_start(out=o, in_=i, **k).then_inc(s, 16))
        self._hazard_update(sem, tgt, reads, writes)
        return tgt

    def barrier(self, bufs=None, exclude=()):
        targets = [(s, s.n) for k, s in self.s.items() if s.n > 0 and k not in exclude]
        for eng in ENGS:
            for s, t in targets:
                if s.owner != eng:
                    self._wait(eng, s, t)
        if bufs:
            for b in bufs.values():
                b.w = None
                b.readers = []

    def selfwait(self, eng):
        sem = self.s[eng]
        self._wait(eng, sem, sem.n)

    def raw_wait(self, eng, sem, tgt):
        self._wait(eng, sem, tgt)

    def inp(self, name, shape, dt=F32):
        self.ap[name] = self.nc.dram_tensor(name, list(shape), dt, kind="ExternalInput").ap()
        return self.ap[name]

    def outp(self, name, shape, dt=F32):
        self.ap[name] = self.nc.dram_tensor(name, list(shape), dt, kind="ExternalOutput").ap()
        return self.ap[name]

    def finish(self, block):
        self._wait("sync", self.s["out"], self.s["out"].n)
        q = self.q

        @block.sync
        def _(sync):
            for fn in q["sync"]:
                fn(sync)

        @block.gpsimd
        def _(gpsimd):
            for fn in q["gpsimd"]:
                fn(gpsimd)

        @block.vector
        def _(vector):
            for fn in q["vector"]:
                fn(vector)

        @block.scalar
        def _(scalar):
            for fn in q["scalar"]:
                fn(scalar)

        @block.tensor
        def _(tensor):
            for fn in q["tensor"]:
                fn(tensor)


def host_constants():
    c = {}
    c["ones"] = np.ones((128, 128), np.float32)
    c["onesb"] = np.ones((128, 128), np.float32).astype(ml_dtypes.bfloat16)
    c["I128"] = np.eye(128, dtype=np.float32)
    c["I128b"] = np.eye(128, dtype=np.float32).astype(ml_dtypes.bfloat16)
    jj, ii = np.meshgrid(np.arange(128), np.arange(128), indexing="ij")
    c["U128"] = (jj <= ii).astype(np.float32)
    c["U512"] = np.ascontiguousarray(np.tile((jj <= ii).astype(np.float32), (1, 4)))
    c["maskTneg"] = np.where(jj <= ii, 0.0, BIGNEG).astype(np.float32)
    sel8 = np.zeros((8, 8 * 128), np.float32)
    for e in range(8):
        sel8[e, e * 128:(e + 1) * 128] = 1.0
    c["sel8"] = sel8
    sel16 = np.zeros((16, 16 * 128), np.float32)
    for t_ in range(16):
        sel16[t_, t_ * 128:(t_ + 1) * 128] = 1.0
    c["sel16"] = sel16
    inv = 1.0 / (10000.0 ** (np.arange(0, 128, 2, dtype=np.float64) / 128.0))
    fr = np.outer(inv, np.arange(T, dtype=np.float64))
    cos = np.cos(fr).astype(np.float32)
    sin = np.sin(fr).astype(np.float32)
    c["cosf"] = np.ascontiguousarray(np.concatenate([cos, cos], axis=0))
    c["sinm"] = np.ascontiguousarray(np.concatenate([-sin, sin], axis=0))
    return c


def colmat(v, chunks):
    return np.ascontiguousarray(np.asarray(v, np.float32).reshape(chunks, 128).T)


def fm(dram_ap):
    return dram_ap.rearrange("(c p) t -> p c t", p=128)


def fm3(sbuf_tile, width):
    return sbuf_tile[:, :].rearrange("p (c t) -> p c t", t=width)



def build_l0a():
    nc = bass.Bass("TRN2", target_bir_lowering=False)
    with ExitStack() as st:
        cx = Ctx(nc, st)
        e = st.enter_context
        inp, outp = cx.inp, cx.outp

        inp("xT", (D, T))
        inp("ones", (128, 128)); inp("I128", (128, 128)); inp("U128", (128, 128))
        inp("onesb", (128, 128), BF16); inp("I128b", (128, 128), BF16)
        inp("WqkvT", (D, 3 * D)); inp("bq_cols", (128, 24))
        inp("WoutT", (D, D)); inp("bout_cols", (128, 8))
        inp("gamma_cols", (128, 8)); inp("beta_cols", (128, 8))
        inp("rwT", (D, 8))
        outp("xn_T", (D, T)); outp("acc_T", (D, T)); outp("cwT", (8, T))

        sb = {}

        def tile(name, shape, dt=F32):
            sb[name] = e(nc.sbuf_tensor("sb_" + name, list(shape), dt))
            return sb[name]

        x_g = tile("x_g", [128, DC * G])
        xnb_g = tile("xnb_g", [128, DC * G], BF16)
        wq_b = tile("wq_b", [128, DC * 3 * D], BF16)
        wo_b = tile("wo_b", [128, DC * D], BF16)
        rw_f = tile("rw_f", [128, DC * 8])
        ones = tile("ones", [128, 128]); I128 = tile("I128", [128, 128])
        U128 = tile("U128", [128, 128])
        onesb = tile("onesb", [128, 128], BF16); I128b = tile("I128b", [128, 128], BF16)
        bq_cols = tile("bq_cols", [128, 24]); nbq_cols = tile("nbq_cols", [128, 24])
        bout_cols = tile("bout_cols", [128, 8])
        gamma_cols = tile("gamma_cols", [128, 8]); beta_cols = tile("beta_cols", [128, 8])
        sq_sb = tile("sq_sb", [128, G])
        row_a = tile("row_a", [1, G]); row_b = tile("row_b", [1, G]); row_c = tile("row_c", [1, G])
        mb_sb = tile("mb_sb", [128, G]); rb_sb = tile("rb_sb", [128, G])
        xn_sc = tile("xn_sc", [128, G])
        phiq = tile("phiq", [64, 16 * G], BF16)
        phik = tile("phik", [64, 16 * G], BF16)
        v_fm = tile("v_fm", [128, 8 * G], BF16)
        t_r1 = tile("t_r1", [128, G]); t_r2 = tile("t_r2", [128, G])
        t_e2 = tile("t_e2", [128, G]); t_s = tile("t_s", [128, G])
        v_tm = tile("v_tm", [128, 4 * 1024], BF16)
        k_tm = tile("k_tm", [128, 4 * 1024], BF16)
        A_sb = tile("A_sb", [128, 128], BF16)
        S_all = tile("S_all", [64, 16 * 72])
        S_allb = tile("S_allb", [64, 16 * 72], BF16)
        dn_row = tile("dn_row", [1, 128]); rdn_row = tile("rdn_row", [1, 128])
        rdn_b = tile("rdn_b", [64, 128])
        ol_fm = tile("ol_fm", [128, 8 * G], BF16)
        a0_fm = tile("a0_fm", [128, 8 * G])
        lg_fm = tile("lg_fm", [8, G])
        lg_tm = tile("lg_tm", [128, 8])
        pr = {n: tile("pr_" + n, [128, w]) for n, w in
              [("m", 1), ("e", 8), ("s", 1), ("rs", 1), ("p", 8), ("p1", 1),
               ("oh", 8), ("pm", 8), ("p2", 1), ("mk", 8), ("dn", 1), ("rd", 1), ("cw", 8)]}
        cwT_sb = tile("cwT_sb", [8, 128])

        p_mm1 = e(nc.psum_tensor("p_mm1", [128, G], F32))
        p_mm2 = e(nc.psum_tensor("p_mm2", [128, G], F32))
        p_bc = e(nc.psum_tensor("p_bc", [128, G], F32))
        p_tp = e(nc.psum_tensor("p_tp", [128, 1024], BF16))
        pA = e(nc.psum_tensor("pA", [128, 128], F32))
        pN = e(nc.psum_tensor("pN", [128, 128], F32))
        pB = e(nc.psum_tensor("pB", [128, 128], F32))
        pS = e(nc.psum_tensor("pS", [128, 128], F32))

        B = {}

        def buf(name):
            if name not in B:
                B[name] = Buf(cx)
            return B[name]

        block = e(nc.Block())

        for n in ("ones", "I128", "U128", "bq_cols", "bout_cols", "gamma_cols", "beta_cols"):
            cx.dma("sync", sb[n][:, :], cx.ap[n], writes=[buf(n)], sem=cx.s["sync"])
        cx.dma("sync", sb["onesb"][:, :], cx.ap["onesb"], writes=[buf("onesb")], sem=cx.s["sync"])
        cx.dma("sync", sb["I128b"][:, :], cx.ap["I128b"], writes=[buf("I128b")], sem=cx.s["sync"])
        cx.dma("gpsimd", fm3(wq_b, 3 * D), fm(cx.ap["WqkvT"]), writes=[buf("wq_b")], sem=cx.s["gpsimd"])
        cx.dma("gpsimd", fm3(wo_b, D), fm(cx.ap["WoutT"]), writes=[buf("wo_b")], sem=cx.s["gpsimd"])
        cx.dma("sync", fm3(rw_f, 8), fm(cx.ap["rwT"]), writes=[buf("rw_f")], sem=cx.s["sync"])

        cx.step("vector", lambda e_: e_.tensor_scalar_mul(nbq_cols[:, :], bq_cols[:, :], -1.0),
                reads=[buf("bq_cols")], writes=[buf("nbq_cols")])
        cx.step("vector", lambda e_: e_.memset(S_all[:, :], 0.0), writes=[buf("S_all")])
        cx.step("vector", lambda e_: e_.memset(S_allb[:, :], 0.0), writes=[buf("S_allb")])

        bx = buf("x_g"); bxnb = buf("xnb_g"); bsq = buf("sq_sb")
        bra, brb, brc = buf("row_a"), buf("row_b"), buf("row_c")
        bmb, brbs = buf("mb_sb"), buf("rb_sb")
        bxn = buf("xn_sc")
        bp1, bp2, bpbc = buf("p_mm1"), buf("p_mm2"), buf("p_bc")
        bptp = buf("p_tp")
        bpA, bpN, bpB, bpS = buf("pA"), buf("pN"), buf("pB"), buf("pS")
        bphiq, bphik, bvfm = buf("phiq"), buf("phik"), buf("v_fm")
        br1, br2, be2, bts = buf("t_r1"), buf("t_r2"), buf("t_e2"), buf("t_s")
        bvtm, bktm = buf("v_tm"), buf("k_tm")
        bA = buf("A4_sb"); bS, bSb = buf("S_all"), buf("S_allb")
        bdn, brdn, brdnb = buf("dn4"), buf("rdn4"), buf("rdnb4")
        bol = buf("ol_fm"); ba0 = buf("a0_fm")
        blg, blgt = buf("lg_fm"), buf("lg_tm")
        bpr = {n: buf("pr_" + n) for n in pr}
        bcw = buf("cwT_sb")

        def ln_stats(src_fn, src_buf, extra_eps=1e-5):
            cx._hazard_waits("tensor", [src_buf], [bp1, bp2])
            for c in range(DC):
                src = src_fn(c)
                t_sq = cx.step("scalar", lambda e_, src=src: e_.activation(sq_sb[:, :], src, AF.Square),
                               reads=[src_buf], writes=[bsq])
                cx.step("tensor", lambda e_, src=src, c=c: e_.matmul(
                    p_mm1[0:1, :], ones[:, 0:1], src, start=(c == 0), stop=(c == DC - 1)))
                cx.step("tensor", lambda e_, c=c: e_.matmul(
                    p_mm2[0:1, :], ones[:, 0:1], sq_sb[:, :], start=(c == 0), stop=(c == DC - 1)),
                    reads=[bsq])
            cx._hazard_update(cx.s["tensor"], cx.s["tensor"].n, [src_buf], [bp1, bp2])
            cx.step("vector", lambda e_: e_.tensor_scalar_mul(row_a[0:1, :], p_mm1[0:1, :], 1.0 / D),
                    reads=[bp1], writes=[bra])
            cx.step("vector", lambda e_: e_.tensor_scalar_mul(row_b[0:1, :], p_mm2[0:1, :], 1.0 / D),
                    reads=[bp2], writes=[brb])
            cx.step("vector", lambda e_: e_.tensor_mul(row_c[0:1, :], row_a[0:1, :], row_a[0:1, :]),
                    reads=[bra], writes=[brc])
            cx.step("vector", lambda e_: e_.tensor_tensor(row_b[0:1, :], row_b[0:1, :], row_c[0:1, :], ALU.subtract),
                    reads=[brc], writes=[brb])
            cx.step("vector", lambda e_: e_.tensor_scalar_add(row_b[0:1, :], row_b[0:1, :], extra_eps),
                    writes=[brb])
            cx.step("scalar", lambda e_: e_.activation(row_c[0:1, :], row_b[0:1, :], AF.Ln),
                    reads=[brb], writes=[brc])
            cx.selfwait("scalar")
            cx.step("scalar", lambda e_: e_.activation(row_b[0:1, :], row_c[0:1, :], AF.Exp, scale=-0.5),
                    reads=[brc], writes=[brb])
            cx.step("tensor", lambda e_: e_.matmul(p_bc[:, :], ones[0:1, :], row_a[0:1, :], start=True, stop=True),
                    reads=[bra], writes=[bpbc])
            cx.step("vector", lambda e_: e_.tensor_copy(mb_sb[:, :], p_bc[:, :]),
                    reads=[bpbc], writes=[bmb])
            cx.step("tensor", lambda e_: e_.matmul(p_bc[:, :], ones[0:1, :], row_b[0:1, :], start=True, stop=True),
                    reads=[brb], writes=[bpbc])
            cx.step("vector", lambda e_: e_.tensor_copy(rb_sb[:, :], p_bc[:, :]),
                    reads=[bpbc], writes=[brbs])

        def router_tail(tl):
                cx.step("vector", lambda e_: e_.reduce_max(pr["m"][:, 0:1], lg_tm[:, :], axis=AX.X, negate=True),
                        reads=[blgt], writes=[bpr["m"]])
                cx.step("scalar", lambda e_: e_.activation(
                    pr["e"][:, :], lg_tm[:, :], AF.Exp, bias=pr["m"][:, 0:1], accum_out=pr["s"][:, 0:1]),
                    reads=[blgt, bpr["m"]], writes=[bpr["e"], bpr["s"]])
                cx.step("vector", lambda e_: e_.reciprocal(pr["rs"][:, 0:1], pr["s"][:, 0:1]),
                        reads=[bpr["s"]], writes=[bpr["rs"]])
                cx.selfwait("vector")
                cx.step("vector", lambda e_: e_.tensor_scalar_mul(pr["p"][:, :], pr["e"][:, :], pr["rs"][:, 0:1]),
                        reads=[bpr["e"], bpr["rs"]], writes=[bpr["p"]])
                cx.selfwait("vector")
                cx.step("vector", lambda e_: e_.reduce_max(pr["p1"][:, 0:1], pr["p"][:, :], axis=AX.X),
                        reads=[bpr["p"]], writes=[bpr["p1"]])
                cx.selfwait("vector")
                cx.step("vector", lambda e_: e_.tensor_scalar(
                    pr["oh"][:, :], pr["p"][:, :], pr["p1"][:, 0:1], None, ALU.is_ge),
                    reads=[bpr["p"], bpr["p1"]], writes=[bpr["oh"]])
                cx.step("vector", lambda e_: e_.scalar_tensor_tensor(
                    pr["pm"][:, :], pr["oh"][:, :], BIGNEG, pr["p"][:, :], ALU.mult, ALU.add),
                    reads=[bpr["oh"], bpr["p"]], writes=[bpr["pm"]])
                cx.selfwait("vector")
                cx.step("vector", lambda e_: e_.reduce_max(pr["p2"][:, 0:1], pr["pm"][:, :], axis=AX.X),
                        reads=[bpr["pm"]], writes=[bpr["p2"]])
                cx.selfwait("vector")
                cx.step("vector", lambda e_: e_.tensor_scalar(
                    pr["mk"][:, :], pr["p"][:, :], pr["p2"][:, 0:1], None, ALU.is_ge),
                    reads=[bpr["p"], bpr["p2"]], writes=[bpr["mk"]])
                cx.step("vector", lambda e_: e_.tensor_add(pr["dn"][:, 0:1], pr["p1"][:, 0:1], pr["p2"][:, 0:1]),
                        reads=[bpr["p1"], bpr["p2"]], writes=[bpr["dn"]])
                cx.step("vector", lambda e_: e_.tensor_scalar_add(pr["dn"][:, 0:1], pr["dn"][:, 0:1], 1e-9),
                        writes=[bpr["dn"]])
                cx.selfwait("vector")
                cx.step("vector", lambda e_: e_.reciprocal(pr["rd"][:, 0:1], pr["dn"][:, 0:1]),
                        reads=[bpr["dn"]], writes=[bpr["rd"]])
                cx.step("vector", lambda e_: e_.tensor_mul(pr["cw"][:, :], pr["p"][:, :], pr["mk"][:, :]),
                        reads=[bpr["p"], bpr["mk"]], writes=[bpr["cw"]])
                cx.selfwait("vector")
                cx.step("vector", lambda e_: e_.tensor_scalar_mul(pr["cw"][:, :], pr["cw"][:, :], pr["rd"][:, 0:1]),
                        reads=[bpr["rd"]], writes=[bpr["cw"]])
                cx.step("tensor", lambda e_: e_.transpose(p_mm1[0:8, 0:128], pr["cw"][:, :], I128[:, :]),
                        reads=[bpr["cw"]], writes=[bp1])
                cx.step("vector", lambda e_: e_.tensor_copy(
                    cwT_sb[0:8, 0:128], p_mm1[0:8, 0:128]),
                    reads=[bp1], writes=[bcw])
                cx.dma("sync", cx.ap["cwT"][0:8, tl * 128:(tl + 1) * 128], cwT_sb[0:8, 0:128],
                       reads=[bcw], sem=cx.s["out"])


        for g in range(NT):
            cx.dma("sync", fm3(x_g, G), fm(cx.ap["xT"][:, g * G:(g + 1) * G]),
                   writes=[bx], sem=cx.s["sync"])

            ln_stats(lambda c: x_g[:, c * G:(c + 1) * G], bx)
            p_rt = [pA, pN, pB, pS]
            b_rt = [bpA, bpN, bpB, bpS]
            for c in range(DC):
                cx.step("vector", lambda e_, c=c: e_.tensor_tensor(
                    xn_sc[:, :], x_g[:, c * G:(c + 1) * G], mb_sb[:, :], ALU.subtract),
                    reads=[bx, bmb], writes=[bxn])
                cx.step("vector", lambda e_: e_.tensor_tensor(xn_sc[:, :], xn_sc[:, :], rb_sb[:, :], ALU.mult),
                        reads=[brbs], writes=[bxn])
                cx.step("vector", lambda e_, c=c: e_.tensor_copy(xnb_g[:, c * G:(c + 1) * G], xn_sc[:, :]),
                        reads=[bxn], writes=[bxnb])
                cx._hazard_waits("tensor", [bxn, buf("rw_f")], [])
                for blk in range(4):
                    if c == 0:
                        cx._hazard_waits("tensor", [], [b_rt[blk]])
                    cx.step("tensor", lambda e_, c=c, blk=blk: e_.matmul(
                        p_rt[blk][:, 0:8], xn_sc[:, blk * 128:(blk + 1) * 128],
                        rw_f[:, c * 8:(c + 1) * 8], start=(c == 0), stop=(c == DC - 1)))
                    if c == DC - 1:
                        cx._hazard_update(cx.s["tensor"], cx.s["tensor"].n, [bxn, buf("rw_f")], [b_rt[blk]])
                    else:
                        b_rt[blk].readers.append((cx.s["tensor"], cx.s["tensor"].n))
                        bxn.readers.append((cx.s["tensor"], cx.s["tensor"].n))
                cx.dma("sync", cx.ap["xn_T"][c * 128:(c + 1) * 128, g * G:(g + 1) * G], xn_sc[:, :],
                       reads=[bxn], sem=cx.s["out"])
            for blk in range(4):
                tl = g * 4 + blk
                cx.step("vector", lambda e_, blk=blk: e_.tensor_copy(lg_tm[:, :], p_rt[blk][:, 0:8]),
                        reads=[b_rt[blk]], writes=[blgt])
                router_tail(tl)

            for jc in range(24):
                ps, bps = (p_mm1, bp1) if jc % 2 == 0 else (p_mm2, bp2)
                cx._hazard_waits("tensor", [bxnb, buf("wq_b")], [bps])
                for c in range(DC):
                    cx.step("tensor", lambda e_, c=c, jc=jc, ps=ps: e_.matmul(
                        ps[:, :], wq_b[:, c * 3072 + jc * 128: c * 3072 + (jc + 1) * 128],
                        xnb_g[:, c * G:(c + 1) * G], start=(c == 0), stop=(c == DC - 1)))
                cx._hazard_update(cx.s["tensor"], cx.s["tensor"].n, [bxnb, buf("wq_b")], [bps])
                if jc < 16:
                    is_q = jc < 8
                    dst, bdst, dj = (phiq, bphiq, jc) if is_q else (phik, bphik, jc - 8)
                    cx.step("scalar", lambda e_, jc=jc, ps=ps: e_.activation(
                        t_r1[:, :], ps[:, :], AF.Relu, bias=bq_cols[:, jc:jc + 1]),
                        reads=[bps, buf("bq_cols")], writes=[br1])
                    cx.step("scalar", lambda e_, jc=jc, ps=ps: e_.activation(
                        t_r2[:, :], ps[:, :], AF.Relu, bias=nbq_cols[:, jc:jc + 1], scale=-1.0),
                        reads=[bps, buf("nbq_cols")], writes=[br2])
                    cx.selfwait("scalar")
                    cx.step("scalar", lambda e_: e_.activation(t_e2[:, :], t_r2[:, :], AF.Exp, scale=-1.0),
                            reads=[br2], writes=[be2])
                    cx.step("vector", lambda e_: e_.tensor_add(t_s[:, :], t_r1[:, :], t_e2[:, :]),
                            reads=[br1, be2], writes=[bts])
                    for half in range(2):
                        h_ = dj * 2 + half
                        dsl = dst[0:64, h_ * G:(h_ + 1) * G]
                        ssl = t_s[half * 64:(half + 1) * 64, :]
                        if is_q:
                            cx.step("vector", lambda e_, dsl=dsl, ssl=ssl: e_.tensor_scalar(
                                dsl, ssl, 0.125, 1.2501250e-05, ALU.mult, ALU.add),
                                reads=[bts], writes=[bdst])
                        else:
                            cx.step("vector", lambda e_, dsl=dsl, ssl=ssl: e_.tensor_scalar_add(
                                dsl, ssl, 1e-4),
                                reads=[bts], writes=[bdst])
                else:
                    dj = jc - 16
                    cx.step("scalar", lambda e_, jc=jc, dj=dj, ps=ps: e_.activation(
                        v_fm[:, dj * G:(dj + 1) * G], ps[:, :], AF.Identity, bias=bq_cols[:, jc:jc + 1]),
                        reads=[bps, buf("bq_cols")], writes=[bvfm])

            jobs = []
            for h in range(16):
                for tc in range(4):
                    jobs.append((phik[0:64, h * G + tc * 128: h * G + (tc + 1) * 128],
                                 I128b[0:64, 0:64], 64,
                                 k_tm[:, tc * 1024 + h * 64: tc * 1024 + (h + 1) * 64],
                                 bphik, bktm))
            for fc in range(8):
                for tc in range(4):
                    jobs.append((v_fm[:, fc * G + tc * 128: fc * G + (tc + 1) * 128],
                                 I128b[:, :], 128,
                                 v_tm[:, tc * 1024 + fc * 128: tc * 1024 + (fc + 1) * 128],
                                 bvfm, bvtm))
            bi = 0
            while bi < len(jobs):
                batch, used = [], 0
                while bi < len(jobs) and used + jobs[bi][2] <= 1024:
                    batch.append((used, jobs[bi]))
                    used += jobs[bi][2]
                    bi += 1
                cx._hazard_waits("tensor", [j[4] for (_, j) in batch], [bptp])
                for off, (in_ap, id_ap, w, dst_ap, bsrc, bdst) in batch:
                    cx.step("tensor", lambda e_, in_ap=in_ap, id_ap=id_ap, off=off, w=w: e_.transpose(
                        p_tp[:, off:off + w], in_ap, id_ap))
                cx._hazard_update(cx.s["tensor"], cx.s["tensor"].n,
                                  [j[4] for (_, j) in batch], [bptp])
                for off, (in_ap, id_ap, w, dst_ap, bsrc, bdst) in batch:
                    cx.step("vector", lambda e_, dst_ap=dst_ap, off=off, w=w: e_.tensor_copy(
                        dst_ap, p_tp[:, off:off + w]),
                        reads=[bptp], writes=[bdst])

            for tc in range(4):
                for h in range(16):
                    rb0 = (h % 2) * 64
                    phiq_sl = phiq[0:64, h * G + tc * 128: h * G + (tc + 1) * 128]
                    phik_sl = phik[0:64, h * G + tc * 128: h * G + (tc + 1) * 128]
                    ktm_sl = k_tm[:, tc * 1024 + h * 64: tc * 1024 + (h + 1) * 64]
                    vtm_sl = v_tm[:, tc * 1024 + h * 64: tc * 1024 + (h + 1) * 64]
                    S_sl = S_all[0:64, h * 72: h * 72 + 65]
                    Sb_sl = S_allb[0:64, h * 72: h * 72 + 65]
                    cx.step("tensor", lambda e_, ks=phik_sl, qs=phiq_sl: e_.matmul(
                        pA[:, :], ks, qs, start=True, stop=True),
                        reads=[bphik, bphiq], writes=[bpA])
                    cx.step("vector", lambda e_: e_.tensor_mul(A_sb[:, :], pA[:, :], U128[:, :]),
                            reads=[bpA, buf("U128")], writes=[bA])
                    cx._hazard_waits("tensor", [bSb, bA, bvtm, buf("onesb")], [bpN])
                    cx.step("tensor", lambda e_, qs=phiq_sl, Sb=Sb_sl: e_.matmul(
                        pN[0:65, 0:128], Sb, qs, start=True, stop=False))
                    cx.step("tensor", lambda e_, vs=vtm_sl: e_.matmul(
                        pN[0:64, 0:128], vs, A_sb[:, :], start=False, stop=False))
                    cx.step("tensor", lambda e_: e_.matmul(
                        pN[64:65, 0:128], onesb[:, 0:1], A_sb[:, :], start=False, stop=True))
                    cx._hazard_update(cx.s["tensor"], cx.s["tensor"].n,
                                      [bSb, bA, bvtm, buf("onesb")], [bpN])
                    cx.step("vector", lambda e_: e_.tensor_scalar_add(dn_row[0:1, :], pN[64:65, 0:128], 1e-4),
                            reads=[bpN], writes=[bdn])
                    cx.selfwait("vector")
                    cx.step("vector", lambda e_: e_.reciprocal(rdn_row[0:1, :], dn_row[0:1, :]),
                            reads=[bdn], writes=[brdn])
                    cx.step("tensor", lambda e_: e_.matmul(
                        pB[0:64, 0:128], ones[0:1, 0:64], rdn_row[0:1, :], start=True, stop=True),
                        reads=[brdn], writes=[bpB])
                    cx.step("vector", lambda e_: e_.tensor_copy(rdn_b[:, :], pB[0:64, 0:128]),
                            reads=[bpB], writes=[brdnb])
                    cx.selfwait("vector")
                    cx.step("vector", lambda e_, h=h, tc=tc: e_.tensor_mul(
                        ol_fm[rb0:rb0 + 64, (h // 2) * G + tc * 128:(h // 2) * G + (tc + 1) * 128]
                        if False else
                        ol_fm[(h % 2) * 64:(h % 2) * 64 + 64,
                              (h // 2) * G + tc * 128:(h // 2) * G + (tc + 1) * 128],
                        pN[0:64, 0:128], rdn_b[:, :]),
                        reads=[bpN, brdnb], writes=[bol])
                    cx._hazard_waits("tensor", [bktm, bvtm, buf("onesb")], [bpS])
                    cx.step("tensor", lambda e_, ks=ktm_sl, vs=vtm_sl: e_.matmul(
                        pS[0:64, 0:64], ks, vs, start=True, stop=False))
                    cx.step("tensor", lambda e_, ks=ktm_sl: e_.matmul(
                        pS[0:64, 64:65], ks, onesb[:, 0:1], start=False, stop=True))
                    cx._hazard_update(cx.s["tensor"], cx.s["tensor"].n,
                                      [bktm, bvtm, buf("onesb")], [bpS])
                    cx.step("vector", lambda e_, S=S_sl: e_.tensor_add(S, S, pS[0:64, 0:65]),
                            reads=[bpS], writes=[bS])
                    cx.selfwait("vector")
                    cx.step("vector", lambda e_, S=S_sl, Sb=Sb_sl: e_.tensor_copy(Sb, S),
                            reads=[bS], writes=[bSb])

            for jc in range(DC):
                ps, bps = (p_mm1, bp1) if jc % 2 == 0 else (p_mm2, bp2)
                cx._hazard_waits("tensor", [bol, buf("wo_b")], [bps])
                for c in range(DC):
                    cx.step("tensor", lambda e_, jc=jc, c=c, ps=ps: e_.matmul(
                        ps[:, :], wo_b[:, c * D + jc * 128: c * D + (jc + 1) * 128],
                        ol_fm[:, c * G:(c + 1) * G], start=(c == 0), stop=(c == DC - 1)))
                cx._hazard_update(cx.s["tensor"], cx.s["tensor"].n, [bol, buf("wo_b")], [bps])
                cx.step("scalar", lambda e_, jc=jc, ps=ps: e_.activation(
                    a0_fm[:, jc * G:(jc + 1) * G], ps[:, :], AF.Identity, bias=bout_cols[:, jc:jc + 1]),
                    reads=[bps, buf("bout_cols")], writes=[ba0])

            ln_stats(lambda c: a0_fm[:, c * G:(c + 1) * G], ba0)
            for c in range(DC):
                sl = a0_fm[:, c * G:(c + 1) * G]
                cx.step("vector", lambda e_, sl=sl: e_.tensor_tensor(sl, sl, mb_sb[:, :], ALU.subtract),
                        reads=[bmb], writes=[ba0])
                cx.step("vector", lambda e_, sl=sl: e_.tensor_tensor(sl, sl, rb_sb[:, :], ALU.mult),
                        reads=[brbs], writes=[ba0])
                cx.step("vector", lambda e_, sl=sl, c=c: e_.tensor_scalar(
                    sl, sl, gamma_cols[:, c:c + 1], beta_cols[:, c:c + 1], ALU.mult, ALU.add),
                    reads=[buf("gamma_cols"), buf("beta_cols")], writes=[ba0])
                cx.step("vector", lambda e_, sl=sl, c=c: e_.tensor_add(
                    x_g[:, c * G:(c + 1) * G], x_g[:, c * G:(c + 1) * G], sl),
                    reads=[ba0], writes=[bx])
                cx.dma("sync", cx.ap["acc_T"][c * 128:(c + 1) * 128, g * G:(g + 1) * G],
                       x_g[:, c * G:(c + 1) * G], reads=[bx], sem=cx.s["out"])

        cx.finish(block)
    return nc


L0A_INPUTS = ["xT", "ones", "I128", "U128", "onesb", "I128b", "WqkvT", "bq_cols",
              "WoutT", "bout_cols", "gamma_cols", "beta_cols", "rwT"]


def prep_l0a_inputs(x_core, W, C):
    m = {}
    m["xT"] = np.ascontiguousarray(x_core.T)
    for n in ("ones", "I128", "U128", "onesb", "I128b"):
        m[n] = C[n]
    m["WqkvT"] = np.ascontiguousarray(W["L0_attn_Wqkv"].T)
    m["bq_cols"] = colmat(W["L0_attn_bqkv"], 24)
    m["WoutT"] = np.ascontiguousarray(W["L0_attn_Wout"].T)
    m["bout_cols"] = colmat(W["L0_attn_bout"], 8)
    m["gamma_cols"] = colmat(W["L0_attn_gamma"], 8)
    m["beta_cols"] = colmat(W["L0_attn_beta"], 8)
    m["rwT"] = np.ascontiguousarray(W["L0_ffn_router_w"].T)
    return m





def build_l0f():
    nc = bass.Bass("TRN2", target_bir_lowering=False)
    with ExitStack() as st:
        cx = Ctx(nc, st)
        e = st.enter_context
        inp, outp = cx.inp, cx.outp
        inp("xn_T", (D, T)); inp("acc_T", (D, T)); inp("cwT", (8, T))
        inp("w1", (8, D, 4 * D)); inp("w2", (8, 4 * D, D))
        inp("sel8", (8, 8 * 128))
        outp("x1_T", (D, T))

        sb = {}

        def tile(name, shape, dt=F32):
            sb[name] = e(nc.sbuf_tensor("sb_" + name, list(shape), dt))
            return sb[name]

        xnb = tile("xnb", [128, DC * T], BF16)
        acc = tile("acc", [128, DC * T])
        h_sb = tile("h_sb", [128, 32 * 1024], BF16)
        wbuf = tile("wbuf", [128, 8 * 1024], BF16)
        cw_sb = tile("cw_sb", [8, T])
        sel8 = tile("sel8", [8, 8 * 128])
        cwb_sb = tile("cwb_sb", [128, 2 * G])
        ytmp = tile("ytmp", [128, G])

        hp = [e(nc.psum_tensor("hp0", [128, G], F32)), e(nc.psum_tensor("hp1", [128, G], F32))]
        yp = [e(nc.psum_tensor("yp0", [128, G], F32)), e(nc.psum_tensor("yp1", [128, G], F32))]
        cp = e(nc.psum_tensor("cp", [128, G], F32))

        B = {}

        def buf(name):
            if name not in B:
                B[name] = Buf(cx)
            return B[name]

        block = e(nc.Block())

        bxnb, bacc, bh, bw = buf("xnb"), buf("acc"), buf("h_sb"), buf("wbuf")
        bcws, bsel, bcwb, byt = buf("cw_sb"), buf("sel8"), buf("cwb_sb"), buf("ytmp")
        bhp = [buf("hp0"), buf("hp1")]
        byp = [buf("yp0"), buf("yp1")]
        bcp = buf("cp")

        cx.dma("gpsimd", fm3(xnb, T), fm(cx.ap["xn_T"]), writes=[bxnb], sem=cx.s["gpsimd"])
        cx.dma("sync", fm3(acc, T), fm(cx.ap["acc_T"]), writes=[bacc], sem=cx.s["sync"])
        cx.dma("sync", cw_sb[:, :], cx.ap["cwT"], writes=[bcws], sem=cx.s["sync"])
        cx.dma("sync", sel8[:, :], cx.ap["sel8"], writes=[bsel], sem=cx.s["sync"])

        NTOK = 1024

        for ex in range(8):
            for nt in range(2):
                for fb in range(4):
                    cx.dma("gpsimd",
                           wbuf[:, :].rearrange("p (c f) -> p c f", f=1024),
                           cx.ap["w1"][ex, :, fb * 1024:(fb + 1) * 1024]
                           .rearrange("(c p) f -> p c f", p=128),
                           writes=[bw], sem=cx.s["gpsimd"])
                    for fc in range(8):
                        fcg = fb * 8 + fc
                        for ns in range(2):
                            ps, bps = hp[(fcg * 2 + ns) % 2], bhp[(fcg * 2 + ns) % 2]
                            cx._hazard_waits("tensor", [bw, bxnb], [bps])
                            for c in range(DC):
                                cx.step("tensor", lambda e_, c=c, fc=fc, nt=nt, ns=ns, ps=ps: e_.matmul(
                                    ps[:, :],
                                    wbuf[:, c * 1024 + fc * 128: c * 1024 + (fc + 1) * 128],
                                    xnb[:, c * T + nt * NTOK + ns * G: c * T + nt * NTOK + (ns + 1) * G],
                                    start=(c == 0), stop=(c == DC - 1)))
                            cx._hazard_update(cx.s["tensor"], cx.s["tensor"].n, [bw, bxnb], [bps])
                            cx.step("scalar", lambda e_, fcg=fcg, ns=ns, ps=ps: e_.activation(
                                h_sb[:, fcg * 1024 + ns * G: fcg * 1024 + (ns + 1) * G],
                                ps[:, :], AF.Gelu),
                                reads=[bps], writes=[bh])
                for ns in range(2):
                    cx.step("tensor", lambda e_, ex=ex, nt=nt, ns=ns: e_.matmul(
                        cp[:, :], sel8[:, ex * 128:(ex + 1) * 128],
                        cw_sb[0:8, nt * NTOK + ns * G: nt * NTOK + (ns + 1) * G],
                        start=True, stop=True),
                        reads=[bsel, bcws], writes=[bcp])
                    cx.step("vector", lambda e_, ns=ns: e_.tensor_copy(
                        cwb_sb[:, ns * G:(ns + 1) * G], cp[:, :]),
                        reads=[bcp], writes=[bcwb])
                for jb in range(4):
                    cx.dma("gpsimd",
                           wbuf[:, :].rearrange("p (c f) -> p c f", f=256),
                           cx.ap["w2"][ex, :, jb * 256:(jb + 1) * 256]
                           .rearrange("(c p) j -> p c j", p=128),
                           writes=[bw], sem=cx.s["gpsimd"])
                    for ns in range(2):
                        for jcr in range(2):
                            jc = jb * 2 + jcr
                            ps, bps = yp[jcr], byp[jcr]
                            cx._hazard_waits("tensor", [bw, bh], [bps])
                            for fc in range(32):
                                cx.step("tensor", lambda e_, fc=fc, jcr=jcr, ns=ns, ps=ps: e_.matmul(
                                    ps[:, :],
                                    wbuf[:, fc * 256 + jcr * 128: fc * 256 + (jcr + 1) * 128],
                                    h_sb[:, fc * 1024 + ns * G: fc * 1024 + (ns + 1) * G],
                                    start=(fc == 0), stop=(fc == 31)))
                            cx._hazard_update(cx.s["tensor"], cx.s["tensor"].n, [bw, bh], [bps])
                            cx.step("vector", lambda e_, ns=ns, ps=ps: e_.tensor_tensor(
                                ytmp[:, :], ps[:, :], cwb_sb[:, ns * G:(ns + 1) * G], ALU.mult),
                                reads=[bps, bcwb], writes=[byt])
                            asl = acc[:, jc * T + nt * NTOK + ns * G: jc * T + nt * NTOK + (ns + 1) * G]
                            cx.step("vector", lambda e_, asl=asl: e_.tensor_add(asl, asl, ytmp[:, :]),
                                    reads=[byt], writes=[bacc])

        for c in range(DC):
            cx.dma("sync", cx.ap["x1_T"][c * 128:(c + 1) * 128, :], acc[:, c * T:(c + 1) * T],
                   reads=[bacc], sem=cx.s["out"])
        cx.finish(block)
    return nc


def prep_l0f_inputs(xn_T, acc_T, cwT, W, C):
    return {"xn_T": xn_T, "acc_T": acc_T, "cwT": cwT,
            "w1": W["L0_ffn_w1"], "w2": W["L0_ffn_w2"], "sel8": C["sel8"]}





def build_l1a():
    nc = bass.Bass("TRN2", target_bir_lowering=False)
    with ExitStack() as st:
        cx = Ctx(nc, st)
        e = st.enter_context
        inp, outp = cx.inp, cx.outp
        inp("x1_T", (D, T))
        inp("WqT", (D, D)); inp("WkT", (D, D)); inp("WvT", (D, D)); inp("WoT", (D, D))
        inp("lng_cols", (128, 8)); inp("lnb_cols", (128, 8))
        inp("hs_row", (1, 8)); inp("hd3", (3, 8))
        inp("cos_h", (128, T)); inp("sin_h", (128, T))
        inp("ones", (128, 128)); inp("onesb", (128, 128), BF16)
        inp("I128", (128, 128)); inp("maskT3", (128, 384)); inp("sel8", (8, 8 * 128))
        outp("xn2_T", (D, T)); outp("acc1_T", (D, T))

        sb = {}

        def tile(name, shape, dt=F32):
            sb[name] = e(nc.sbuf_tensor("sb_" + name, list(shape), dt))
            return sb[name]

        xn2ab = tile("xn2ab", [128, DC * T], BF16)
        out_all = tile("out_all", [128, DC * T], BF16)
        cos_h = tile("cos_h", [128, T]); sin_h = tile("sin_h", [128, T])
        x1g = tile("x1g", [128, DC * G])
        xn2g = tile("xn2g", [128, DC * G])
        qr = tile("qr", [128, G]); t1 = tile("t1", [128, G])
        q_hd = tile("q_hd", [128, T], BF16); k_hd = tile("k_hd", [128, T], BF16)
        v_tm = tile("v_tm", [128, 16 * 128], BF16)
        PT = tile("PT", [128, 16 * 256], BF16)
        PgT = tile("PgT", [128, 16 * 256], BF16)
        om = tile("om", [128, 256]); og = tile("og", [128, 256])
        wqh = tile("wqh", [128, DC * 128], BF16); wkh = tile("wkh", [128, DC * 128], BF16)
        wvh = tile("wvh", [128, DC * 128], BF16)
        wo_b = tile("wo_b", [128, DC * 128], BF16)
        x1t = tile("x1t", [128, G])
        ones = tile("ones", [128, 128]); onesb = tile("onesb", [128, 128], BF16)
        I128 = tile("I128", [128, 128]); maskT3 = tile("maskT3", [128, 384])
        sel8 = tile("sel8", [8, 8 * 128])
        lng_cols = tile("lng_cols", [128, 8]); lnb_cols = tile("lnb_cols", [128, 8])
        hs_row = tile("hs_row", [1, 8]); hd3b = tile("hd3b", [3, 8], BF16)
        hd3f = tile("hd3f", [3, 8])
        sq_sb = tile("sq_sb", [128, G])
        row_a = tile("row_a", [1, G]); row_b = tile("row_b", [1, G]); row_c = tile("row_c", [1, G])
        mb_sb = tile("mb_sb", [128, G]); rb_sb = tile("rb_sb", [128, G])
        nm_all = tile("nm_all", [128, 16]); nm_grp = tile("nm_grp", [128, 8])
        mT_sb = tile("mT_sb", [8, 128]); mTr = tile("mTr", [8, 1])
        negm_cols = tile("negm_cols", [128, 8])
        qp_row = tile("qp_row", [1, T]); kp_row = tile("kp_row", [1, T])
        gq_mx = tile("gq_mx", [1, 8]); gq_mn = tile("gq_mn", [1, 8])
        gk_mx = tile("gk_mx", [1, 8]); gk_mn = tile("gk_mn", [1, 8])
        mg_row = tile("mg_row", [1, 8]); mg4 = tile("mg4", [1, 8])
        mg_col = tile("mg_col", [8, 1]); negmg_cols = tile("negmg_cols", [128, 8])
        Zrow = tile("Zrow", [1, 256]); rZrow = tile("rZrow", [1, 256])
        Zgrow = tile("Zgrow", [1, 256]); rZgrow = tile("rZgrow", [1, 256])
        rZb = tile("rZb", [128, 256]); rZgb = tile("rZgb", [128, 256])
        hs_col = tile("hs_col", [128, 1])

        pS1 = e(nc.psum_tensor("pS1", [128, G], F32))
        pS2 = e(nc.psum_tensor("pS2", [128, G], F32))
        pT = e(nc.psum_tensor("pT", [128, 256], F32))
        pT2 = e(nc.psum_tensor("pT2", [128, 256], F32))
        pZ = e(nc.psum_tensor("pZ", [1, 256], F32))
        pZg = e(nc.psum_tensor("pZg", [1, 256], F32))
        pAV = e(nc.psum_tensor("pAV", [128, 256], F32))
        pAVg = e(nc.psum_tensor("pAVg", [128, 256], F32))

        B = {}

        def buf(name):
            if name not in B:
                B[name] = Buf(cx)
            return B[name]

        block = e(nc.Block())

        for n in ("cos_h", "sin_h", "ones", "I128", "maskT3", "sel8", "lng_cols",
                  "lnb_cols", "hs_row", "hd3f"):
            src_name = "hd3" if n == "hd3f" else n
            cx.dma("sync", sb[n][:, :], cx.ap[src_name], writes=[buf(n)], sem=cx.s["sync"])
        cx.dma("gpsimd", onesb[:, :], cx.ap["onesb"], writes=[buf("onesb")], sem=cx.s["gpsimd"])
        cx.dma("gpsimd", hd3b[:, :], cx.ap["hd3"], writes=[buf("hd3b")], sem=cx.s["gpsimd"])

        bx1, bxn2g, bxab = buf("x1g"), buf("xn2g"), buf("xn2ab")
        boa = buf("out_all")
        bqr, bt1, bqh, bkh, bvtm = buf("qr"), buf("t1"), buf("q_hd"), buf("k_hd"), buf("v_tm")
        bPT, bPgT, bom, bog = buf("PT"), buf("PgT"), buf("om"), buf("og")
        bwq, bwk, bwv, bwo = buf("wqh"), buf("wkh"), buf("wvh"), buf("wo_b")
        bx1t = buf("x1t")
        bsq = buf("sq_sb")
        bra, brb, brc = buf("row_a"), buf("row_b"), buf("row_c")
        bmb, brbs = buf("mb_sb"), buf("rb_sb")
        bnm, bnmg = buf("nm_all"), buf("nm_grp")
        bmT, bmTr, bnmc = buf("mT_sb"), buf("mTr"), buf("negm_cols")
        bqp, bkp = buf("qp_row"), buf("kp_row")
        bgq_mx, bgq_mn = buf("gq_mx"), buf("gq_mn")
        bgk_mx, bgk_mn = buf("gk_mx"), buf("gk_mn")
        bmgr, bmg4, bmgc, bnmgc = buf("mg_row"), buf("mg4"), buf("mg_col"), buf("negmg_cols")
        bZ, brZ, bZg, brZg = buf("Zrow"), buf("rZrow"), buf("Zgrow"), buf("rZgrow")
        bZb, bZgb = buf("rZb"), buf("rZgb")
        bhsc = buf("hs_col")
        bpS1, bpS2 = buf("pS1"), buf("pS2")
        bpT, bpT2 = buf("pT"), buf("pT2")
        bpZ, bpZg = buf("pZ"), buf("pZg")
        bpAV, bpAVg = buf("pAV"), buf("pAVg")

        def ln_stats(src_fn, src_buf, p1, p2, bp1, bp2):
            cx._hazard_waits("tensor", [src_buf], [bp1, bp2])
            for c in range(DC):
                src = src_fn(c)
                cx.step("scalar", lambda e_, src=src: e_.activation(sq_sb[:, :], src, AF.Square),
                        reads=[src_buf], writes=[bsq])
                cx.step("tensor", lambda e_, src=src, c=c: e_.matmul(
                    p1[0:1, 0:G], ones[:, 0:1], src, start=(c == 0), stop=(c == DC - 1)))
                cx.step("tensor", lambda e_, c=c: e_.matmul(
                    p2[0:1, 0:G], ones[:, 0:1], sq_sb[:, :], start=(c == 0), stop=(c == DC - 1)),
                    reads=[bsq])
            cx._hazard_update(cx.s["tensor"], cx.s["tensor"].n, [src_buf], [bp1, bp2])
            cx.step("vector", lambda e_: e_.tensor_scalar_mul(row_a[0:1, :], p1[0:1, 0:G], 1.0 / D),
                    reads=[bp1], writes=[bra])
            cx.step("vector", lambda e_: e_.tensor_scalar_mul(row_b[0:1, :], p2[0:1, 0:G], 1.0 / D),
                    reads=[bp2], writes=[brb])
            cx.step("vector", lambda e_: e_.tensor_mul(row_c[0:1, :], row_a[0:1, :], row_a[0:1, :]),
                    reads=[bra], writes=[brc])
            cx.step("vector", lambda e_: e_.tensor_tensor(row_b[0:1, :], row_b[0:1, :], row_c[0:1, :], ALU.subtract),
                    reads=[brc], writes=[brb])
            cx.step("vector", lambda e_: e_.tensor_scalar_add(row_b[0:1, :], row_b[0:1, :], 1e-5),
                    writes=[brb])
            cx.step("scalar", lambda e_: e_.activation(row_c[0:1, :], row_b[0:1, :], AF.Ln),
                    reads=[brb], writes=[brc])
            cx.step("scalar", lambda e_: e_.activation(row_b[0:1, :], row_c[0:1, :], AF.Exp, scale=-0.5),
                    reads=[brc], writes=[brb])
            cx.step("tensor", lambda e_, p1=p1: e_.matmul(p1[:, 0:G], ones[0:1, :], row_a[0:1, :], start=True, stop=True),
                    reads=[bra], writes=[bp1])
            cx.step("vector", lambda e_, p1=p1: e_.tensor_copy(mb_sb[:, :], p1[:, 0:G]),
                    reads=[bp1], writes=[bmb])
            cx.step("tensor", lambda e_, p1=p1: e_.matmul(p1[:, 0:G], ones[0:1, :], row_b[0:1, :], start=True, stop=True),
                    reads=[brb], writes=[bp1])
            cx.step("vector", lambda e_, p1=p1: e_.tensor_copy(rb_sb[:, :], p1[:, 0:G]),
                    reads=[bp1], writes=[brbs])

        for g in range(NT):
            cx.dma("sync", fm3(x1g, G), fm(cx.ap["x1_T"][:, g * G:(g + 1) * G]),
                   writes=[bx1], sem=cx.s["sync"])
            ln_stats(lambda c: x1g[:, c * G:(c + 1) * G], bx1, pS1, pS2, bpS1, bpS2)
            for c in range(DC):
                xsl = xn2g[:, c * G:(c + 1) * G]
                cx.step("vector", lambda e_, c=c, xsl=xsl: e_.tensor_tensor(
                    xsl, x1g[:, c * G:(c + 1) * G], mb_sb[:, :], ALU.subtract),
                    reads=[bx1, bmb], writes=[bxn2g])
                cx.step("vector", lambda e_, xsl=xsl: e_.tensor_tensor(xsl, xsl, rb_sb[:, :], ALU.mult),
                        reads=[brbs], writes=[bxn2g])
                cx.dma("sync", cx.ap["xn2_T"][c * 128:(c + 1) * 128, g * G:(g + 1) * G], xsl,
                       reads=[bxn2g], sem=cx.s["out"])
                cx.step("vector", lambda e_, c=c, xsl=xsl: e_.tensor_add(
                    x1g[:, c * G:(c + 1) * G], x1g[:, c * G:(c + 1) * G], xsl),
                    reads=[bxn2g], writes=[bx1])
                cx.dma("sync", cx.ap["acc1_T"][c * 128:(c + 1) * 128, g * G:(g + 1) * G],
                       x1g[:, c * G:(c + 1) * G], reads=[bx1], sem=cx.s["out"])
            ln_stats(lambda c: xn2g[:, c * G:(c + 1) * G], bxn2g, pS1, pS2, bpS1, bpS2)
            for c in range(DC):
                xsl = xn2g[:, c * G:(c + 1) * G]
                cx.step("vector", lambda e_, xsl=xsl: e_.tensor_tensor(xsl, xsl, mb_sb[:, :], ALU.subtract),
                        reads=[bmb], writes=[bxn2g])
                cx.step("vector", lambda e_, xsl=xsl: e_.tensor_tensor(xsl, xsl, rb_sb[:, :], ALU.mult),
                        reads=[brbs], writes=[bxn2g])
                cx.step("vector", lambda e_, c=c, g=g, xsl=xsl: e_.tensor_scalar(
                    xn2ab[:, c * T + g * G: c * T + (g + 1) * G], xsl,
                    lng_cols[:, c:c + 1], lnb_cols[:, c:c + 1], ALU.mult, ALU.add),
                    reads=[bxn2g, buf("lng_cols"), buf("lnb_cols")], writes=[bxab])


        SC = 0.08838834764831845
        for h in range(8):
            cx.dma("gpsimd", fm3(wqh, 128), fm(cx.ap["WqT"][:, h * 128:(h + 1) * 128]),
                   writes=[bwq], sem=cx.s["gpsimd"])
            cx.dma("gpsimd", fm3(wkh, 128), fm(cx.ap["WkT"][:, h * 128:(h + 1) * 128]),
                   writes=[bwk], sem=cx.s["gpsimd"])
            cx.dma("gpsimd", fm3(wvh, 128), fm(cx.ap["WvT"][:, h * 128:(h + 1) * 128]),
                   writes=[bwv], sem=cx.s["gpsimd"])
            for which, wb, bwb, dst, bdst, scl in (
                    ("q", wqh, bwq, q_hd, bqh, SC), ("k", wkh, bwk, k_hd, bkh, 1.0)):
                for g in range(NT):
                    ps, bps = (pS1, bpS1) if g % 2 == 0 else (pS2, bpS2)
                    cx._hazard_waits("tensor", [bwb, bxab], [bps])
                    for c in range(DC):
                        cx.step("tensor", lambda e_, c=c, g=g, wb=wb, ps=ps: e_.matmul(
                            ps[:, :], wb[:, c * 128:(c + 1) * 128],
                            xn2ab[:, c * T + g * G: c * T + (g + 1) * G],
                            start=(c == 0), stop=(c == DC - 1)))
                    cx._hazard_update(cx.s["tensor"], cx.s["tensor"].n, [bwb, bxab], [bps])
                    if scl == 1.0:
                        cx.step("scalar", lambda e_, g=g, ps=ps: e_.activation(
                            qr[:, :], ps[:, :], AF.Identity),
                            reads=[bps], writes=[bqr])
                    else:
                        cx.step("scalar", lambda e_, g=g, ps=ps, scl=scl: e_.activation(
                            qr[:, :], ps[:, :], AF.Identity, scale=scl),
                            reads=[bps], writes=[bqr])
                    ch = cos_h[:, g * G:(g + 1) * G]
                    sh_lo = sin_h[0:64, g * G:(g + 1) * G]
                    sh_hi = sin_h[64:128, g * G:(g + 1) * G]
                    cx.step("vector", lambda e_, sh_hi=sh_hi: e_.tensor_mul(
                        t1[0:64, :], qr[64:128, :], sh_hi),
                        reads=[bqr, buf("sin_h")], writes=[bt1])
                    cx.step("vector", lambda e_, sh_lo=sh_lo: e_.tensor_mul(
                        t1[64:128, :], qr[0:64, :], sh_lo),
                        reads=[bqr, buf("sin_h")], writes=[bt1])
                    cx.step("vector", lambda e_, ch=ch: e_.tensor_mul(qr[:, :], qr[:, :], ch),
                            reads=[buf("cos_h")], writes=[bqr])
                    cx.step("vector", lambda e_, dst=dst, g=g: e_.tensor_tensor(
                        dst[0:64, g * G:(g + 1) * G], qr[0:64, :], t1[0:64, :], ALU.subtract),
                        reads=[bqr, bt1], writes=[bdst])
                    cx.step("vector", lambda e_, dst=dst, g=g: e_.tensor_add(
                        dst[64:128, g * G:(g + 1) * G], qr[64:128, :], t1[64:128, :]),
                        reads=[bqr, bt1], writes=[bdst])
            for tt in range(16):
                ps, bps = (pS1, bpS1) if tt % 2 == 0 else (pS2, bpS2)
                cx._hazard_waits("tensor", [bwv, bxab], [bps])
                for c in range(DC):
                    cx.step("tensor", lambda e_, c=c, tt=tt, ps=ps: e_.matmul(
                        ps[:, 0:128], xn2ab[:, c * T + tt * 128: c * T + (tt + 1) * 128],
                        wvh[:, c * 128:(c + 1) * 128], start=(c == 0), stop=(c == DC - 1)))
                cx._hazard_update(cx.s["tensor"], cx.s["tensor"].n, [bwv, bxab], [bps])
                cx.step("vector", lambda e_, tt=tt, ps=ps: e_.tensor_copy(
                    v_tm[:, tt * 128:(tt + 1) * 128], ps[:, 0:128]),
                    reads=[bps], writes=[bvtm])
            for src_hd, bsrc, dstrow, bdstrow, scl in (
                    (q_hd, bqh, qp_row, bqp, 11.313708498984761),
                    (k_hd, bkh, kp_row, bkp, None)):
                for g in range(NT):
                    ps, bps = (pS1, bpS1) if g % 2 == 0 else (pS2, bpS2)
                    cx.step("tensor", lambda e_, h=h, g=g, src_hd=src_hd, ps=ps: e_.matmul(
                        ps[0:1, 0:G], hd3b[0:3, h:h + 1], src_hd[0:3, g * G:(g + 1) * G],
                        start=True, stop=True),
                        reads=[buf("hd3b"), bsrc], writes=[bps])
                    if scl is not None:
                        cx.step("scalar", lambda e_, g=g, ps=ps, dstrow=dstrow, scl=scl: e_.activation(
                            dstrow[0:1, g * G:(g + 1) * G], ps[0:1, 0:G], AF.Identity, scale=scl),
                            reads=[bps], writes=[bdstrow])
                    else:
                        cx.step("scalar", lambda e_, g=g, ps=ps, dstrow=dstrow: e_.activation(
                            dstrow[0:1, g * G:(g + 1) * G], ps[0:1, 0:G], AF.Identity),
                            reads=[bps], writes=[bdstrow])
            for it in range(16):
                njb = it // 4 + 1
                for jb in range(njb):
                    ps, bps = (pS1, bpS1) if jb % 2 == 0 else (pS2, bpS2)
                    cx.step("tensor", lambda e_, it=it, jb=jb, ps=ps: e_.matmul(
                        ps[:, :], q_hd[:, it * 128:(it + 1) * 128],
                        k_hd[:, jb * G:(jb + 1) * G], start=True, stop=True),
                        reads=[bqh, bkh], writes=[bps])
                    if jb == 0:
                        cx.step("vector", lambda e_, it=it, ps=ps: e_.reduce_max(
                            nm_all[:, it:it + 1], ps[:, :], axis=AX.X, negate=True),
                            reads=[bps], writes=[bnm])
                    else:
                        cx.step("vector", lambda e_, ps=ps: e_.reduce_max(
                            row_a[0:1, 0:1] if False else sq_sb[:, 0:1], ps[:, :], axis=AX.X, negate=True),
                            reads=[bps], writes=[bsq])
                        cx.step("vector", lambda e_, it=it: e_.tensor_tensor(
                            nm_all[:, it:it + 1], nm_all[:, it:it + 1], sq_sb[:, 0:1], ALU.min),
                            reads=[bsq], writes=[bnm])
            cx.step("vector", lambda e_: e_.tensor_tensor(
                nm_grp[:, :], nm_all[:, 0:16:2], nm_all[:, 1:16:2], ALU.min),
                reads=[bnm], writes=[bnmg])
            cx.step("tensor", lambda e_: e_.matmul(pS1[0:8, 0:128], nm_grp[:, :], I128[:, :],
                                                   start=True, stop=True),
                    reads=[bnmg, buf("I128")], writes=[bpS1])
            cx.step("vector", lambda e_: e_.tensor_copy(mT_sb[:, :], pS1[0:8, 0:128]),
                    reads=[bpS1], writes=[bmT])
            cx.step("vector", lambda e_: e_.tensor_reduce(mTr[:, 0:1], mT_sb[:, :], AX.X, ALU.min),
                    reads=[bmT], writes=[bmTr])
            for gp in range(8):
                cx.step("tensor", lambda e_, gp=gp: e_.matmul(
                    pS2[:, 0:1], sel8[:, gp * 128:(gp + 1) * 128], mTr[:, 0:1],
                    start=True, stop=True),
                    reads=[buf("sel8"), bmTr], writes=[bpS2])
                cx.step("vector", lambda e_, gp=gp: e_.tensor_copy(
                    negm_cols[:, gp:gp + 1], pS2[:, 0:1]),
                    reads=[bpS2], writes=[bnmc])
            for gp in range(8):
                cx.step("vector", lambda e_, gp=gp: e_.reduce_max(
                    gq_mx[0:1, gp:gp + 1], qp_row[0:1, gp * 256:(gp + 1) * 256], axis=AX.X),
                    reads=[bqp], writes=[bgq_mx])
                cx.step("vector", lambda e_, gp=gp: e_.tensor_reduce(
                    gq_mn[0:1, gp:gp + 1], qp_row[0:1, gp * 256:(gp + 1) * 256], AX.X, ALU.min),
                    reads=[bqp], writes=[bgq_mn])
                cx.step("vector", lambda e_, gp=gp: e_.reduce_max(
                    gk_mx[0:1, gp:gp + 1], kp_row[0:1, gp * 256:(gp + 1) * 256], axis=AX.X),
                    reads=[bkp], writes=[bgk_mx])
                cx.step("vector", lambda e_, gp=gp: e_.tensor_reduce(
                    gk_mn[0:1, gp:gp + 1], kp_row[0:1, gp * 256:(gp + 1) * 256], AX.X, ALU.min),
                    reads=[bkp], writes=[bgk_mn])
            for shift in (1, 2, 4):
                cx.step("vector", lambda e_, shift=shift: e_.tensor_tensor(
                    gk_mx[0:1, shift:8], gk_mx[0:1, shift:8], gk_mx[0:1, 0:8 - shift], ALU.max),
                    writes=[bgk_mx])
                cx.step("vector", lambda e_, shift=shift: e_.tensor_tensor(
                    gk_mn[0:1, shift:8], gk_mn[0:1, shift:8], gk_mn[0:1, 0:8 - shift], ALU.min),
                    writes=[bgk_mn])
            cx.step("vector", lambda e_: e_.tensor_mul(mg_row[0:1, :], gq_mx[0:1, :], gk_mx[0:1, :]),
                    reads=[bgq_mx, bgk_mx], writes=[bmgr])
            for a_t, a_b, b_t, b_b in ((gq_mx, bgq_mx, gk_mn, bgk_mn),
                                       (gq_mn, bgq_mn, gk_mx, bgk_mx),
                                       (gq_mn, bgq_mn, gk_mn, bgk_mn)):
                cx.step("vector", lambda e_, a_t=a_t, b_t=b_t: e_.tensor_mul(
                    mg4[0:1, :], a_t[0:1, :], b_t[0:1, :]),
                    reads=[a_b, b_b], writes=[bmg4])
                cx.step("vector", lambda e_: e_.tensor_tensor(
                    mg_row[0:1, :], mg_row[0:1, :], mg4[0:1, :], ALU.max),
                    reads=[bmg4], writes=[bmgr])
            cx.step("vector", lambda e_: e_.tensor_scalar_mul(mg_row[0:1, :], mg_row[0:1, :], -1.0),
                    writes=[bmgr])
            cx.step("tensor", lambda e_: e_.matmul(pS1[0:8, 0:1], mg_row[0:1, :], ones[0:1, 0:1],
                                                   start=True, stop=True),
                    reads=[bmgr, buf("ones")], writes=[bpS1])
            cx.step("vector", lambda e_: e_.tensor_copy(mg_col[:, 0:1], pS1[0:8, 0:1]),
                    reads=[bpS1], writes=[bmgc])
            for gp in range(8):
                cx.step("tensor", lambda e_, gp=gp: e_.matmul(
                    pS2[:, 0:1], sel8[:, gp * 128:(gp + 1) * 128], mg_col[:, 0:1],
                    start=True, stop=True),
                    reads=[buf("sel8"), bmgc], writes=[bpS2])
                cx.step("vector", lambda e_, gp=gp: e_.tensor_copy(
                    negmg_cols[:, gp:gp + 1], pS2[:, 0:1]),
                    reads=[bpS2], writes=[bnmgc])
            cx.step("tensor", lambda e_, h=h: e_.matmul(pS1[:, 0:1], ones[0:1, :], hs_row[0:1, h:h + 1],
                                                        start=True, stop=True),
                    reads=[buf("hs_row"), buf("ones")], writes=[bpS1])
            cx.step("vector", lambda e_: e_.tensor_copy(hs_col[:, 0:1], pS1[:, 0:1]),
                    reads=[bpS1], writes=[bhsc])
            for gp in range(8):
                njt = gp * 2 + 2
                for jt in range(njt):
                    cx.step("tensor", lambda e_, jt=jt, gp=gp: e_.matmul(
                        pT[:, :], k_hd[:, jt * 128:(jt + 1) * 128],
                        q_hd[:, gp * 256:(gp + 1) * 256], start=True, stop=True),
                        reads=[bkh, bqh], writes=[bpT])
                    if jt >= njt - 2:
                        off = 128 if jt == njt - 2 else 0
                        cx.step("vector", lambda e_, off=off: e_.tensor_add(
                            pT[:, :], pT[:, :], maskT3[:, off:off + 256]),
                            reads=[buf("maskT3")], writes=[bpT])
                    cx.step("scalar", lambda e_, jt=jt, gp=gp: e_.activation(
                        PT[:, jt * 256:(jt + 1) * 256], pT[:, :], AF.Exp,
                        bias=negm_cols[:, gp:gp + 1]),
                        reads=[bpT, bnmc], writes=[bPT])
                    cx.step("tensor", lambda e_, jt=jt, gp=gp: e_.matmul(
                        pT2[:, :], kp_row[0:1, jt * 128:(jt + 1) * 128],
                        qp_row[0:1, gp * 256:(gp + 1) * 256], start=True, stop=True),
                        reads=[bqp, bkp], writes=[bpT2])
                    if jt >= njt - 2:
                        off = 128 if jt == njt - 2 else 0
                        cx.step("vector", lambda e_, off=off: e_.tensor_add(
                            pT2[:, :], pT2[:, :], maskT3[:, off:off + 256]),
                            reads=[buf("maskT3")], writes=[bpT2])
                    cx.step("scalar", lambda e_, jt=jt, gp=gp: e_.activation(
                        PgT[:, jt * 256:(jt + 1) * 256], pT2[:, :], AF.Exp,
                        bias=negmg_cols[:, gp:gp + 1]),
                        reads=[bpT2, bnmgc], writes=[bPgT])
                cx._hazard_waits("tensor", [bPT, buf("onesb")], [bpZ])
                for jt in range(njt):
                    cx.step("tensor", lambda e_, jt=jt: e_.matmul(
                        pZ[0:1, :], onesb[:, 0:1], PT[:, jt * 256:(jt + 1) * 256],
                        start=(jt == 0), stop=(jt == njt - 1)))
                cx._hazard_update(cx.s["tensor"], cx.s["tensor"].n, [bPT, buf("onesb")], [bpZ])
                cx._hazard_waits("tensor", [bPgT, buf("onesb")], [bpZg])
                for jt in range(njt):
                    cx.step("tensor", lambda e_, jt=jt: e_.matmul(
                        pZg[0:1, :], onesb[:, 0:1], PgT[:, jt * 256:(jt + 1) * 256],
                        start=(jt == 0), stop=(jt == njt - 1)))
                cx._hazard_update(cx.s["tensor"], cx.s["tensor"].n, [bPgT, buf("onesb")], [bpZg])
                cx._hazard_waits("tensor", [bPT, bvtm], [bpAV])
                for jt in range(njt):
                    cx.step("tensor", lambda e_, jt=jt: e_.matmul(
                        pAV[:, :], v_tm[:, jt * 128:(jt + 1) * 128], PT[:, jt * 256:(jt + 1) * 256],
                        start=(jt == 0), stop=(jt == njt - 1)))
                cx._hazard_update(cx.s["tensor"], cx.s["tensor"].n, [bPT, bvtm], [bpAV])
                cx._hazard_waits("tensor", [bPgT, bvtm], [bpAVg])
                for jt in range(njt):
                    cx.step("tensor", lambda e_, jt=jt: e_.matmul(
                        pAVg[:, :], v_tm[:, jt * 128:(jt + 1) * 128], PgT[:, jt * 256:(jt + 1) * 256],
                        start=(jt == 0), stop=(jt == njt - 1)))
                cx._hazard_update(cx.s["tensor"], cx.s["tensor"].n, [bPgT, bvtm], [bpAVg])
                cx.step("vector", lambda e_: e_.tensor_copy(Zrow[0:1, :], pZ[0:1, :]),
                        reads=[bpZ], writes=[bZ])
                cx.step("vector", lambda e_: e_.reciprocal(rZrow[0:1, :], Zrow[0:1, :]),
                        reads=[bZ], writes=[brZ])
                cx.step("vector", lambda e_: e_.tensor_copy(Zgrow[0:1, :], pZg[0:1, :]),
                        reads=[bpZg], writes=[bZg])
                cx.step("vector", lambda e_: e_.reciprocal(rZgrow[0:1, :], Zgrow[0:1, :]),
                        reads=[bZg], writes=[brZg])
                cx.step("tensor", lambda e_: e_.matmul(pT[:, :], ones[0:1, :], rZrow[0:1, :],
                                                       start=True, stop=True),
                        reads=[brZ, buf("ones")], writes=[bpT])
                cx.step("vector", lambda e_: e_.tensor_copy(rZb[:, :], pT[:, :]),
                        reads=[bpT], writes=[bZb])
                cx.step("tensor", lambda e_: e_.matmul(pT2[:, :], ones[0:1, :], rZgrow[0:1, :],
                                                       start=True, stop=True),
                        reads=[brZg, buf("ones")], writes=[bpT2])
                cx.step("vector", lambda e_: e_.tensor_copy(rZgb[:, :], pT2[:, :]),
                        reads=[bpT2], writes=[bZgb])
                cx.step("vector", lambda e_: e_.tensor_mul(om[:, :], pAV[:, :], rZb[:, :]),
                        reads=[bpAV, bZb], writes=[bom])
                cx.step("vector", lambda e_: e_.tensor_mul(og[:, :], pAVg[:, :], rZgb[:, :]),
                        reads=[bpAVg, bZgb], writes=[bog])
                cx.step("vector", lambda e_: e_.tensor_tensor(og[:, :], og[:, :], om[:, :], ALU.subtract),
                        reads=[bom], writes=[bog])
                cx.step("vector", lambda e_: e_.tensor_scalar_mul(og[:, :], og[:, :], hs_col[:, 0:1]),
                        reads=[bhsc], writes=[bog])
                cx.step("vector", lambda e_, h=h, gp=gp: e_.tensor_tensor(
                    out_all[:, h * T + gp * 256: h * T + (gp + 1) * 256], om[:, :], og[:, :], ALU.add),
                    reads=[bom, bog], writes=[boa])

        for jc in range(DC):
            cx.dma("gpsimd", fm3(wo_b, 128), fm(cx.ap["WoT"][:, jc * 128:(jc + 1) * 128]),
                   writes=[bwo], sem=cx.s["gpsimd"])
            for g in range(NT):
                ps, bps = (pS1, bpS1) if g % 2 == 0 else (pS2, bpS2)
                cx._hazard_waits("tensor", [boa, bwo], [bps])
                for c in range(DC):
                    cx.step("tensor", lambda e_, jc=jc, c=c, g=g, ps=ps: e_.matmul(
                        ps[:, :], wo_b[:, c * 128:(c + 1) * 128],
                        out_all[:, c * T + g * G: c * T + (g + 1) * G],
                        start=(c == 0), stop=(c == DC - 1)))
                cx._hazard_update(cx.s["tensor"], cx.s["tensor"].n, [boa, bwo], [bps])
                cx.dma("sync", x1t[:, :], cx.ap["acc1_T"][jc * 128:(jc + 1) * 128, g * G:(g + 1) * G],
                       writes=[bx1t], sem=cx.s["sync"])
                cx.step("vector", lambda e_, ps=ps: e_.tensor_tensor(x1t[:, :], x1t[:, :], ps[:, :], ALU.add),
                        reads=[bps], writes=[bx1t])
                cx.dma("sync", cx.ap["acc1_T"][jc * 128:(jc + 1) * 128, g * G:(g + 1) * G], x1t[:, :],
                       reads=[bx1t], sem=cx.s["out"])
        cx.finish(block)
    return nc


def prep_l1a_inputs(x1_T, W, C):
    m = {"x1_T": x1_T}
    m["WqT"] = np.ascontiguousarray(W["L1_attn_Wq"].T)
    m["WkT"] = np.ascontiguousarray(W["L1_attn_Wk"].T)
    m["WvT"] = np.ascontiguousarray(W["L1_attn_Wv"].T)
    m["WoT"] = np.ascontiguousarray(W["L1_attn_Wo"].T)
    m["lng_cols"] = colmat(W["L1_attn_ln_g"], 8)
    m["lnb_cols"] = colmat(W["L1_attn_ln_b"], 8)
    m["hs_row"] = np.ascontiguousarray(np.asarray(W["L1_attn_head_scales"], np.float32)[None, :])
    m["hd3"] = np.ascontiguousarray(np.asarray(W["L1_attn_head_dirs"], np.float32).T)
    m["cos_h"] = C["cosf"]
    m["sin_h"] = np.ascontiguousarray(np.abs(C["sinm"]))
    m["ones"] = C["ones"]; m["onesb"] = C["onesb"]; m["I128"] = C["I128"]
    m["sel8"] = C["sel8"]
    jj, ii = np.meshgrid(np.arange(128), np.arange(256), indexing="ij")
    tri = np.where(jj <= ii - 128, 0.0, BIGNEG).astype(np.float32)
    m["maskT3"] = np.concatenate([tri, np.zeros((128, 128), np.float32)], axis=1)
    return m





def build_l1g():
    nc = bass.Bass("TRN2", target_bir_lowering=False)
    with ExitStack() as st:
        cx = Ctx(nc, st)
        e = st.enter_context
        inp, outp = cx.inp, cx.outp
        inp("xn2_T", (D, T))
        inp("gwT", (D, 8)); inp("gb_row", (1, 8))
        inp("prefmask", (8, 1))
        inp("ones", (128, 128)); inp("I128", (128, 128)); inp("U128", (128, 128))
        outp("cwT1", (8, T))

        cc_in = nc.dram_tensor("cc_in", [1, 8], F32)
        cc_out = nc.dram_tensor("cc_out", [8, 8], F32)

        sb = {}

        def tile(name, shape, dt=F32):
            sb[name] = e(nc.sbuf_tensor("sb_" + name, list(shape), dt))
            return sb[name]

        xn2f = tile("xn2f", [128, DC * T])
        gw_f = tile("gw_f", [128, DC * 8])
        gb_row = tile("gb_row", [1, 8])
        prefmask = tile("prefmask", [8, 1])
        ones = tile("ones", [128, 128]); I128 = tile("I128", [128, 128])
        U128 = tile("U128", [128, 128])
        lg = tile("lg", [128, 8])
        pr = {n: tile("g_" + n, [128, w]) for n, w in
              [("m", 1), ("e", 8), ("s", 1), ("rs", 1), ("p", 8), ("p1", 1), ("oh", 8)]}
        cs_sb = tile("cs_sb", [128, 8])
        sel_sb = tile("sel_sb", [128, 8])
        oh_all = tile("oh_all", [128, 16 * 8])
        pos_all = tile("pos_all", [128, 16])
        tp_all = tile("tp_all", [128, 16])
        carry = tile("carry", [1, 8])
        ag_sb = tile("ag_sb", [8, 8])
        offs_row = tile("offs_row", [1, 8])
        offs_b = tile("offs_b", [128, 8])
        po = tile("po", [128, 1])
        posg = tile("posg", [128, 1])
        keep = tile("keep", [128, 1])
        cw1 = tile("cw1", [128, 1])
        cwf = tile("cwf", [128, 8])
        cwT_st = tile("cwT_st", [8, 128])

        pA = e(nc.psum_tensor("pA", [128, G], F32))
        pB = e(nc.psum_tensor("pB", [128, G], F32))

        B = {}

        def buf(name):
            if name not in B:
                B[name] = Buf(cx)
            return B[name]

        block = e(nc.Block())

        for n in ("gb_row", "prefmask", "ones", "I128", "U128"):
            cx.dma("sync", sb[n][:, :], cx.ap[n if n != "gb_row" else "gb_row"],
                   writes=[buf(n)], sem=cx.s["sync"])
        cx.dma("sync", fm3(xn2f, T), fm(cx.ap["xn2_T"]), writes=[buf("xn2f")], sem=cx.s["sync"])
        cx.dma("sync", fm3(gw_f, 8), fm(cx.ap["gwT"]), writes=[buf("gw_f")], sem=cx.s["sync"])

        bxn, bgw = buf("xn2f"), buf("gw_f")
        bpA, bpB = buf("pA"), buf("pB")
        blg = buf("lg")
        bpr = {n: buf("g_" + n) for n in pr}
        bcs, bsel = buf("cs_sb"), buf("sel_sb")
        boh, bpos, btp = buf("oh_all"), buf("pos_all"), buf("tp_all")
        bcarry, bag = buf("carry"), buf("ag_sb")
        boffr, boffb = buf("offs_row"), buf("offs_b")
        bpo, bposg, bkeep, bcw1, bcwf = buf("po"), buf("posg"), buf("keep"), buf("cw1"), buf("cwf")
        bcwst = buf("cwT_st")

        cx.step("vector", lambda e_: e_.memset(carry[0:1, :], 0.0), writes=[bcarry])

        for blk in range(16):
            cx._hazard_waits("tensor", [bxn, bgw, buf("gb_row"), buf("ones")], [bpA])
            for c in range(DC):
                cx.step("tensor", lambda e_, c=c, blk=blk: e_.matmul(
                    pA[:, 0:8], xn2f[:, c * T + blk * 128: c * T + (blk + 1) * 128],
                    gw_f[:, c * 8:(c + 1) * 8], start=(c == 0), stop=False))
            cx.step("tensor", lambda e_: e_.matmul(
                pA[:, 0:8], ones[0:1, :], gb_row[0:1, :], start=False, stop=True))
            cx._hazard_update(cx.s["tensor"], cx.s["tensor"].n,
                              [bxn, bgw, buf("gb_row"), buf("ones")], [bpA])
            cx.step("vector", lambda e_: e_.tensor_copy(lg[:, :], pA[:, 0:8]),
                    reads=[bpA], writes=[blg])
            cx.step("vector", lambda e_: e_.reduce_max(pr["m"][:, 0:1], lg[:, :], axis=AX.X, negate=True),
                    reads=[blg], writes=[bpr["m"]])
            cx.step("scalar", lambda e_: e_.activation(
                pr["e"][:, :], lg[:, :], AF.Exp, bias=pr["m"][:, 0:1], accum_out=pr["s"][:, 0:1]),
                reads=[blg, bpr["m"]], writes=[bpr["e"], bpr["s"]])
            cx.step("vector", lambda e_: e_.reciprocal(pr["rs"][:, 0:1], pr["s"][:, 0:1]),
                    reads=[bpr["s"]], writes=[bpr["rs"]])
            cx.step("vector", lambda e_: e_.tensor_scalar_mul(pr["p"][:, :], pr["e"][:, :], pr["rs"][:, 0:1]),
                    reads=[bpr["e"], bpr["rs"]], writes=[bpr["p"]])
            cx.step("vector", lambda e_, blk=blk: e_.reduce_max(
                tp_all[:, blk:blk + 1], pr["p"][:, :], axis=AX.X),
                reads=[bpr["p"]], writes=[btp])
            cx.step("vector", lambda e_, blk=blk: e_.tensor_scalar(
                oh_all[:, blk * 8:(blk + 1) * 8], pr["p"][:, :], tp_all[:, blk:blk + 1], None, ALU.is_ge),
                reads=[bpr["p"], btp], writes=[boh])
            cx._hazard_waits("tensor", [boh, buf("U128"), bcarry, buf("ones")], [bpB])
            cx.step("tensor", lambda e_, blk=blk: e_.matmul(
                pB[:, 0:8], U128[:, :], oh_all[:, blk * 8:(blk + 1) * 8], start=True, stop=False))
            cx.step("tensor", lambda e_: e_.matmul(
                pB[:, 0:8], ones[0:1, :], carry[0:1, :], start=False, stop=True))
            cx._hazard_update(cx.s["tensor"], cx.s["tensor"].n,
                              [boh, buf("U128"), bcarry, buf("ones")], [bpB])
            cx.step("vector", lambda e_: e_.tensor_copy(cs_sb[:, :], pB[:, 0:8]),
                    reads=[bpB], writes=[bcs])
            cx.step("vector", lambda e_, blk=blk: e_.tensor_mul(
                sel_sb[:, :], oh_all[:, blk * 8:(blk + 1) * 8], cs_sb[:, :]),
                reads=[boh, bcs], writes=[bsel])
            cx.step("vector", lambda e_, blk=blk: e_.reduce_sum(
                pos_all[:, blk:blk + 1], sel_sb[:, :], axis=AX.X),
                reads=[bsel], writes=[bpos])
            cx.step("tensor", lambda e_: e_.matmul(pA[0:1, 0:8], I128[:, 127:128], cs_sb[:, :],
                                                   start=True, stop=True),
                    reads=[buf("I128"), bcs], writes=[bpA])
            cx.step("vector", lambda e_: e_.tensor_copy(carry[0:1, :], pA[0:1, 0:8]),
                    reads=[bpA], writes=[bcarry])

        cx.dma("gpsimd", cc_in.ap(), carry[0:1, :], reads=[bcarry], sem=cx.s["gpsimd"])
        gp_sem = cx.s["gpsimd"]
        cc_sem = Sem(e(nc.semaphore("s_cc")), "gpsimd")
        cx.q["gpsimd"].append(lambda e_, t=gp_sem.n: e_.wait_ge(gp_sem.h, t))
        cc_sem.n += 1

        def do_cc(e_):
            return e_.collective_compute(
                "AllGather", ALU.bypass, replica_groups=[list(range(8))],
                ins=[cc_in.ap().opt()], outs=[cc_out.ap().opt()])
        cx.q["gpsimd"].append(lambda e_: do_cc(e_).then_inc(cc_sem.h))
        cx.q["gpsimd"].append(lambda e_: e_.wait_ge(cc_sem.h, 1))
        cx.dma("gpsimd", ag_sb[:, :], cc_out.ap(), writes=[bag], sem=cx.s["gpsimd"])

        cx.step("tensor", lambda e_: e_.matmul(pA[0:1, 0:8], prefmask[:, 0:1], ag_sb[:, :],
                                               start=True, stop=True),
                reads=[buf("prefmask"), bag], writes=[bpA])
        cx.step("vector", lambda e_: e_.tensor_copy(offs_row[0:1, :], pA[0:1, 0:8]),
                reads=[bpA], writes=[boffr])
        cx.step("tensor", lambda e_: e_.matmul(pA[:, 0:8], ones[0:1, :], offs_row[0:1, :],
                                               start=True, stop=True),
                reads=[boffr, buf("ones")], writes=[bpA])
        cx.step("vector", lambda e_: e_.tensor_copy(offs_b[:, :], pA[:, 0:8]),
                reads=[bpA], writes=[boffb])

        for blk in range(16):
            cx.step("vector", lambda e_, blk=blk: e_.tensor_mul(
                sel_sb[:, :], oh_all[:, blk * 8:(blk + 1) * 8], offs_b[:, :]),
                reads=[boh, boffb], writes=[bsel])
            cx.step("vector", lambda e_: e_.reduce_sum(po[:, 0:1], sel_sb[:, :], axis=AX.X),
                    reads=[bsel], writes=[bpo])
            cx.step("vector", lambda e_, blk=blk: e_.tensor_add(
                posg[:, 0:1], pos_all[:, blk:blk + 1], po[:, 0:1]),
                reads=[bpos, bpo], writes=[bposg])
            cx.step("vector", lambda e_: e_.tensor_single_scalar(
                keep[:, 0:1], posg[:, 0:1], 2560.0, ALU.is_le),
                reads=[bposg], writes=[bkeep])
            cx.step("vector", lambda e_, blk=blk: e_.tensor_mul(
                cw1[:, 0:1], keep[:, 0:1], tp_all[:, blk:blk + 1]),
                reads=[bkeep, btp], writes=[bcw1])
            cx.step("vector", lambda e_, blk=blk: e_.tensor_scalar_mul(
                cwf[:, :], oh_all[:, blk * 8:(blk + 1) * 8], cw1[:, 0:1]),
                reads=[boh, bcw1], writes=[bcwf])
            cx.step("tensor", lambda e_: e_.transpose(pB[0:8, 0:128], cwf[:, :], I128[:, :]),
                    reads=[bcwf, buf("I128")], writes=[bpB])
            cx.step("vector", lambda e_: e_.tensor_copy(cwT_st[:, :], pB[0:8, 0:128]),
                    reads=[bpB], writes=[bcwst])
            cx.dma("sync", cx.ap["cwT1"][0:8, blk * 128:(blk + 1) * 128], cwT_st[:, :],
                   reads=[bcwst], sem=cx.s["out"])
        cx.finish(block)
    return nc


def prep_l1g_inputs(xn2_T, W, C, core):
    pm = np.zeros((8, 1), np.float32)
    pm[:core, 0] = 1.0
    return {"xn2_T": xn2_T,
            "gwT": np.ascontiguousarray(W["L1_ffn_gate_w"].T),
            "gb_row": np.ascontiguousarray(np.asarray(W["L1_ffn_gate_b"], np.float32)[None, :]),
            "prefmask": pm, "ones": C["ones"], "I128": C["I128"], "U128": C["U128"]}



def build_l1f():
    nc = bass.Bass("TRN2", target_bir_lowering=False)
    with ExitStack() as st:
        cx = Ctx(nc, st)
        e = st.enter_context
        inp, outp = cx.inp, cx.outp
        inp("xn2_T", (D, T)); inp("acc1_T", (D, T)); inp("cwT1", (8, T))
        inp("w1", (8, D, 4 * D)); inp("w2", (8, 4 * D, D))
        inp("b1_cols", (128, 8 * 32)); inp("b2_cols", (128, 8 * 8))
        inp("sel8", (8, 8 * 128))
        outp("out_T", (D, T))

        sb = {}

        def tile(name, shape, dt=F32):
            sb[name] = e(nc.sbuf_tensor("sb_" + name, list(shape), dt))
            return sb[name]

        xnb = tile("xnb", [128, DC * T], BF16)
        acc = tile("acc", [128, DC * T])
        h_sb = tile("h_sb", [128, 32 * 1024], BF16)
        wbuf = tile("wbuf", [128, 8 * 1024], BF16)
        cw_sb = tile("cw_sb", [8, T])
        sel8 = tile("sel8", [8, 8 * 128])
        b1_cols = tile("b1_cols", [128, 8 * 32])
        b2_cols = tile("b2_cols", [128, 8 * 8])
        cwb_sb = tile("cwb_sb", [128, 2 * G])
        ytmp = tile("ytmp", [128, G])
        yts = tile("yts", [128, G])

        hp = [e(nc.psum_tensor("hp0", [128, G], F32)), e(nc.psum_tensor("hp1", [128, G], F32))]
        yp = [e(nc.psum_tensor("yp0", [128, G], F32)), e(nc.psum_tensor("yp1", [128, G], F32))]
        cp = e(nc.psum_tensor("cp", [128, G], F32))

        B = {}

        def buf(name):
            if name not in B:
                B[name] = Buf(cx)
            return B[name]

        block = e(nc.Block())

        bxnb, bacc, bh, bw = buf("xnb"), buf("acc"), buf("h_sb"), buf("wbuf")
        bcws, bsel, bcwb, byt = buf("cw_sb"), buf("sel8"), buf("cwb_sb"), buf("ytmp")
        byts = buf("yts")
        bhp = [buf("hp0"), buf("hp1")]
        byp = [buf("yp0"), buf("yp1")]
        bcp = buf("cp")

        cx.dma("gpsimd", fm3(xnb, T), fm(cx.ap["xn2_T"]), writes=[bxnb], sem=cx.s["gpsimd"])
        cx.dma("sync", fm3(acc, T), fm(cx.ap["acc1_T"]), writes=[bacc], sem=cx.s["sync"])
        cx.dma("sync", cw_sb[:, :], cx.ap["cwT1"], writes=[bcws], sem=cx.s["sync"])
        cx.dma("sync", sel8[:, :], cx.ap["sel8"], writes=[bsel], sem=cx.s["sync"])
        cx.dma("sync", b1_cols[:, :], cx.ap["b1_cols"], writes=[buf("b1_cols")], sem=cx.s["sync"])
        cx.dma("sync", b2_cols[:, :], cx.ap["b2_cols"], writes=[buf("b2_cols")], sem=cx.s["sync"])

        NTOK = 1024

        for ex in range(8):
            for nt in range(2):
                for fb in range(4):
                    cx.dma("gpsimd",
                           wbuf[:, :].rearrange("p (c f) -> p c f", f=1024),
                           cx.ap["w1"][ex, :, fb * 1024:(fb + 1) * 1024]
                           .rearrange("(c p) f -> p c f", p=128),
                           writes=[bw], sem=cx.s["gpsimd"])
                    for fc in range(8):
                        fcg = fb * 8 + fc
                        for ns in range(2):
                            ps, bps = hp[(fcg * 2 + ns) % 2], bhp[(fcg * 2 + ns) % 2]
                            cx._hazard_waits("tensor", [bw, bxnb], [bps])
                            for c in range(DC):
                                cx.step("tensor", lambda e_, c=c, fc=fc, nt=nt, ns=ns, ps=ps: e_.matmul(
                                    ps[:, :],
                                    wbuf[:, c * 1024 + fc * 128: c * 1024 + (fc + 1) * 128],
                                    xnb[:, c * T + nt * NTOK + ns * G: c * T + nt * NTOK + (ns + 1) * G],
                                    start=(c == 0), stop=(c == DC - 1)))
                            cx._hazard_update(cx.s["tensor"], cx.s["tensor"].n, [bw, bxnb], [bps])
                            cx.step("scalar", lambda e_, ex=ex, fcg=fcg, ns=ns, ps=ps: e_.activation(
                                h_sb[:, fcg * 1024 + ns * G: fcg * 1024 + (ns + 1) * G],
                                ps[:, :], AF.Relu, bias=b1_cols[:, ex * 32 + fcg: ex * 32 + fcg + 1]),
                                reads=[bps, buf("b1_cols")], writes=[bh])
                for ns in range(2):
                    cx.step("tensor", lambda e_, ex=ex, nt=nt, ns=ns: e_.matmul(
                        cp[:, :], sel8[:, ex * 128:(ex + 1) * 128],
                        cw_sb[0:8, nt * NTOK + ns * G: nt * NTOK + (ns + 1) * G],
                        start=True, stop=True),
                        reads=[bsel, bcws], writes=[bcp])
                    cx.step("vector", lambda e_, ns=ns: e_.tensor_copy(
                        cwb_sb[:, ns * G:(ns + 1) * G], cp[:, :]),
                        reads=[bcp], writes=[bcwb])
                for jb in range(4):
                    cx.dma("gpsimd",
                           wbuf[:, :].rearrange("p (c f) -> p c f", f=256),
                           cx.ap["w2"][ex, :, jb * 256:(jb + 1) * 256]
                           .rearrange("(c p) j -> p c j", p=128),
                           writes=[bw], sem=cx.s["gpsimd"])
                    for ns in range(2):
                        for jcr in range(2):
                            jc = jb * 2 + jcr
                            ps, bps = yp[jcr], byp[jcr]
                            cx._hazard_waits("tensor", [bw, bh], [bps])
                            for fc in range(32):
                                cx.step("tensor", lambda e_, fc=fc, jcr=jcr, ns=ns, ps=ps: e_.matmul(
                                    ps[:, :],
                                    wbuf[:, fc * 256 + jcr * 128: fc * 256 + (jcr + 1) * 128],
                                    h_sb[:, fc * 1024 + ns * G: fc * 1024 + (ns + 1) * G],
                                    start=(fc == 0), stop=(fc == 31)))
                            cx._hazard_update(cx.s["tensor"], cx.s["tensor"].n, [bw, bh], [bps])
                            cx.step("scalar", lambda e_, ex=ex, jc=jc, ps=ps: e_.activation(
                                yts[:, :], ps[:, :], AF.Identity,
                                bias=b2_cols[:, ex * 8 + jc: ex * 8 + jc + 1]),
                                reads=[bps, buf("b2_cols")], writes=[byts])
                            cx.step("vector", lambda e_, ns=ns: e_.tensor_tensor(
                                ytmp[:, :], yts[:, :], cwb_sb[:, ns * G:(ns + 1) * G], ALU.mult),
                                reads=[byts, bcwb], writes=[byt])
                            asl = acc[:, jc * T + nt * NTOK + ns * G: jc * T + nt * NTOK + (ns + 1) * G]
                            cx.step("vector", lambda e_, asl=asl: e_.tensor_add(asl, asl, ytmp[:, :]),
                                    reads=[byt], writes=[bacc])

        for c in range(DC):
            cx.dma("sync", cx.ap["out_T"][c * 128:(c + 1) * 128, :], acc[:, c * T:(c + 1) * T],
                   reads=[bacc], sem=cx.s["out"])
        cx.finish(block)
    return nc


def prep_l1f_inputs(xn2_T, acc1_T, cwT1, W, C):
    b1 = np.asarray(W["L1_ffn_b1"], np.float32)
    b2 = np.asarray(W["L1_ffn_b2"], np.float32)
    b1c = b1.reshape(8, 32, 128).transpose(2, 0, 1).reshape(128, 8 * 32)
    b2c = b2.reshape(8, 8, 128).transpose(2, 0, 1).reshape(128, 8 * 8)
    return {"xn2_T": xn2_T, "acc1_T": acc1_T, "cwT1": cwT1,
            "w1": W["L1_ffn_w1"], "w2": W["L1_ffn_w2"],
            "b1_cols": np.ascontiguousarray(b1c), "b2_cols": np.ascontiguousarray(b2c),
            "sel8": C["sel8"]}





def emit_moe(cx, *, ex, nt, T_, G_, xnb, bxnb, h_sb, bh, wbufs, bwbufs,
             accsl_fn, bacc, cw_tile, bcw, sel8, bsel8, cwb_sb, bcwb,
             ytmp, byt, yts, byts, b1_cols, bb1, b2_cols, bb2,
             hp, bhp, yp4, byp4, cp, bcp, w1_ap, w2_ap, act):
    NTOK = 1024
    AF_ = mybir.ActivationFunctionType

    def h_dma(s):
        fb, half = divmod(s, 2)
        wb, bw = wbufs[s % 2], bwbufs[s % 2]
        cx.dma("gpsimd", wb[:, :].rearrange("p (c f) -> p c f", f=512),
               w1_ap[ex, :, fb * 1024 + half * 512: fb * 1024 + (half + 1) * 512]
               .rearrange("(c p) f -> p c f", p=128),
               writes=[bw], sem=cx.s["ws%d" % (s % 2)])

    def h_mms(s):
        fb, half = divmod(s, 2)
        wb, bw = wbufs[s % 2], bwbufs[s % 2]
        for fcrel in range(4):
            fcg = fb * 8 + half * 4 + fcrel
            for ns in range(2):
                ps, bps = hp[(fcg * 2 + ns) % 2], bhp[(fcg * 2 + ns) % 2]
                cx._hazard_waits("tensor", [bw, bxnb], [bps])
                for c in range(DC):
                    cx.step("tensor", lambda e_, c=c, fcrel=fcrel, ns=ns, ps=ps, wb=wb: e_.matmul(
                        ps[:, :], wb[:, c * 512 + fcrel * 128: c * 512 + (fcrel + 1) * 128],
                        xnb[:, c * T_ + nt * NTOK + ns * G_: c * T_ + nt * NTOK + (ns + 1) * G_],
                        start=(c == 0), stop=(c == DC - 1)))
                cx._hazard_update(cx.s["tensor"], cx.s["tensor"].n, [bw, bxnb], [bps])
                if act == "gelu":
                    cx.step("scalar", lambda e_, fcg=fcg, ns=ns, ps=ps: e_.activation(
                        h_sb[:, fcg * 1024 + ns * G_: fcg * 1024 + (ns + 1) * G_],
                        ps[:, :], AF_.Gelu),
                        reads=[bps], writes=[bh])
                else:
                    cx.step("scalar", lambda e_, fcg=fcg, ns=ns, ps=ps: e_.activation(
                        h_sb[:, fcg * 1024 + ns * G_: fcg * 1024 + (ns + 1) * G_],
                        ps[:, :], AF_.Relu, bias=b1_cols[:, ex * 32 + fcg: ex * 32 + fcg + 1]),
                        reads=[bps, bb1], writes=[bh])

    def y_dma(s, bufsel):
        jb, half = divmod(s, 2)
        wb, bw = wbufs[bufsel], bwbufs[bufsel]
        cx.dma("gpsimd", wb[:, :].rearrange("p (c f) -> p c f", f=256),
               w2_ap[ex, half * 2048:(half + 1) * 2048, jb * 256:(jb + 1) * 256]
               .rearrange("(c p) j -> p c j", p=128),
               writes=[bw], sem=cx.s["ws%d" % bufsel])

    def y_mms(s, bufsel):
        jb, half = divmod(s, 2)
        wb, bw = wbufs[bufsel], bwbufs[bufsel]
        for ns in range(2):
            for jcr in range(2):
                ps, bps = yp4[ns * 2 + jcr], byp4[ns * 2 + jcr]
                if half == 0:
                    cx._hazard_waits("tensor", [bw, bh], [bps])
                else:
                    cx._hazard_waits("tensor", [bw, bh], [])
                for fcrel in range(16):
                    fc = half * 16 + fcrel
                    cx.step("tensor", lambda e_, fcrel=fcrel, fc=fc, jcr=jcr, ns=ns, ps=ps, wb=wb: e_.matmul(
                        ps[:, :], wb[:, fcrel * 256 + jcr * 128: fcrel * 256 + (jcr + 1) * 128],
                        h_sb[:, fc * 1024 + ns * G_: fc * 1024 + (ns + 1) * G_],
                        start=(half == 0 and fcrel == 0), stop=(half == 1 and fcrel == 15)))
                if half == 1:
                    cx._hazard_update(cx.s["tensor"], cx.s["tensor"].n, [bw, bh], [bps])
                else:
                    bw.readers.append((cx.s["tensor"], cx.s["tensor"].n))
                    bh.readers.append((cx.s["tensor"], cx.s["tensor"].n))
                    bps.w = (cx.s["tensor"], cx.s["tensor"].n)
        if half == 1:
            for ns in range(2):
                for jcr in range(2):
                    jc = jb * 2 + jcr
                    ps, bps = yp4[ns * 2 + jcr], byp4[ns * 2 + jcr]
                    if b2_cols is not None:
                        cx.step("scalar", lambda e_, jc=jc, ps=ps: e_.activation(
                            yts[:, :], ps[:, :], AF_.Identity,
                            bias=b2_cols[:, ex * 8 + jc: ex * 8 + jc + 1]),
                            reads=[bps, bb2], writes=[byts])
                        ysrc, bysrc = yts, byts
                    else:
                        ysrc, bysrc = ps, bps
                    cx.step("vector", lambda e_, ns=ns, ysrc=ysrc: e_.tensor_tensor(
                        ytmp[:, :], ysrc[:, :], cwb_sb[:, ns * G_:(ns + 1) * G_], ALU.mult),
                        reads=[bysrc, bcwb], writes=[byt])
                    asl = accsl_fn(jc, ns)
                    cx.step("vector", lambda e_, asl=asl: e_.tensor_add(asl, asl, ytmp[:, :]),
                            reads=[byt], writes=[bacc])

    for s in range(8):
        h_dma(s)
        if s >= 1:
            h_mms(s - 1)
    h_mms(7)
    for ns in range(2):
        cx.step("tensor", lambda e_, ns=ns: e_.matmul(
            cp[:, :], sel8[:, ex * 128:(ex + 1) * 128],
            cw_tile[0:8, nt * NTOK + ns * G_: nt * NTOK + (ns + 1) * G_],
            start=True, stop=True),
            reads=[bsel8, bcw], writes=[bcp])
        cx.step("vector", lambda e_, ns=ns: e_.tensor_copy(
            cwb_sb[:, ns * G_:(ns + 1) * G_], cp[:, :]),
            reads=[bcp], writes=[bcwb])
    for s in range(8):
        y_dma(s, s % 2)
        if s >= 1:
            y_mms(s - 1, (s - 1) % 2)
    y_mms(7, 1)



def build_fused(upto=5, dbg=False, p3_alloc_only=False):
    nc = bass.Bass("TRN2", target_bir_lowering=False)
    with ExitStack() as st:
        cx = Ctx(nc, st)
        e = st.enter_context
        inp, outp = cx.inp, cx.outp

        inp("xT", (D, T))
        inp("WqkvT", (D, 3 * D)); inp("bq_cols", (128, 24))
        inp("WoutT", (D, D)); inp("bout_cols", (128, 8))
        inp("gamma_cols", (128, 8)); inp("beta_cols", (128, 8))
        inp("rwT", (D, 8))
        inp("w1_0", (8, D, 4 * D)); inp("w2_0", (8, 4 * D, D))
        inp("WqT", (D, D)); inp("WkT", (D, D)); inp("WvT", (D, D)); inp("WoT", (D, D))
        inp("lng_cols", (128, 8)); inp("lnb_cols", (128, 8))
        inp("hs_row", (1, 8)); inp("hd3", (3, 8))
        inp("gwT", (D, 8)); inp("gb_row", (1, 8)); inp("prefmask", (8, 1))
        inp("w1_1", (8, D, 4 * D)); inp("w2_1", (8, 4 * D, D))
        inp("b1_cols", (128, 8 * 32)); inp("b2_cols", (128, 8 * 8))
        inp("ones", (128, 128)); inp("onesb", (128, 128), BF16)
        inp("I128", (128, 128)); inp("I128b", (128, 128), BF16)
        inp("U128", (128, 128)); inp("maskT3", (128, 384)); inp("sel8", (8, 8 * 128))
        inp("U512", (128, 512)); inp("maskTW", (128, 7 * 128))
        inp("cos_h", (128, T)); inp("sin_h", (128, T))
        outp("OUT_T", (D, T))

        if dbg:
            acc_d = outp("dbg_acc", (D, T))
            cw1_dbg = outp("dbg_cw1", (8, T))
            dbg_p2 = outp("dbg_p2", (D, T))
            dbg_xnb2 = outp("dbg_xnb2", (D, T), BF16)
        else:
            acc_d = nc.dram_tensor("acc_d", [D, T], F32).ap()
        block = st.enter_context(nc.Block())
        cc_in = nc.dram_tensor("cc_in", [1, 8], F32)
        cc_out = nc.dram_tensor("cc_out", [8, 8], F32)

        B = {}

        def buf(name):
            if name not in B:
                B[name] = Buf(cx)
            return B[name]

        sb = {}

        _nseq = [0]

        def mktile(scope, name, shape, dt=F32):
            _nseq[0] += 1
            sb[name] = scope.enter_context(nc.sbuf_tensor(f"sb_{name}_{_nseq[0]}", list(shape), dt))
            return sb[name]

        tile = lambda n, s, d=F32: mktile(st, n, s, d)
        ones = tile("ones", [128, 128]); onesb = tile("onesb", [128, 128], BF16)
        I128 = tile("I128", [128, 128]); I128b = tile("I128b", [128, 128], BF16)
        U128 = tile("U128", [128, 128])
        sel8 = tile("sel8", [8, 8 * 128])
        bq_cols = tile("bq_cols", [128, 24]); nbq_cols = tile("nbq_cols", [128, 24])
        bout_cols = tile("bout_cols", [128, 8])
        gamma_cols = tile("gamma_cols", [128, 8]); beta_cols = tile("beta_cols", [128, 8])
        lng_cols = tile("lng_cols", [128, 8]); lnb_cols = tile("lnb_cols", [128, 8])
        hs_row = tile("hs_row", [1, 8]); hd3f = tile("hd3f", [3, 8]); hd3b = tile("hd3b", [3, 8], BF16)
        gb_row = tile("gb_row", [1, 8]); prefmask = tile("prefmask", [8, 1])
        rw_f = tile("rw_f", [128, DC * 8]); gw_f = tile("gw_f", [128, DC * 8])
        cwT_sb = tile("cwT_sb", [8, T])
        oh_all = tile("oh_all", [128, 16 * 8]); pos_all = tile("pos_all", [128, 16])
        tp_all = tile("tp_all", [128, 16]); carry = tile("carry", [1, 8])
        ag_sb = tile("ag_sb", [8, 8])

        for n in ("ones", "I128", "U128", "sel8", "bq_cols", "bout_cols",
                  "gamma_cols", "beta_cols", "lng_cols", "lnb_cols", "hs_row",
                  "gb_row", "prefmask"):
            cx.dma("sync", sb[n][:, :], cx.ap[n], writes=[buf(n)], sem=cx.s["sync"])
        cx.dma("sync", sb["hd3f"][:, :], cx.ap["hd3"], writes=[buf("hd3f")], sem=cx.s["sync"])
        U512 = tile("U512", [128, 512])
        cx.dma("sync", U512[:, :], cx.ap["U512"], writes=[buf("U512")], sem=cx.s["sync"])
        cx.dma("sync", fm3(rw_f, 8), fm(cx.ap["rwT"]), writes=[buf("rw_f")], sem=cx.s["sync"])
        cx.dma("sync", fm3(gw_f, 8), fm(cx.ap["gwT"]), writes=[buf("gw_f")], sem=cx.s["sync"])
        cx.dma("gpsimd", onesb[:, :], cx.ap["onesb"], writes=[buf("onesb")], sem=cx.s["gpsimd"])
        cx.dma("gpsimd", I128b[:, :], cx.ap["I128b"], writes=[buf("I128b")], sem=cx.s["gpsimd"])
        cx.dma("gpsimd", hd3b[:, :], cx.ap["hd3"], writes=[buf("hd3b")], sem=cx.s["gpsimd"])

        cx.step("vector", lambda e_: e_.tensor_scalar_mul(nbq_cols[:, :], bq_cols[:, :], -1.0),
                reads=[buf("bq_cols")], writes=[buf("nbq_cols")])

        bacc_d = buf("acc_d")
        bcw = buf("cwT_sb")
        bsq = buf("sq_sb")
        bra, brb, brc, brd = buf("row_a"), buf("row_b"), buf("row_c"), buf("row_d")
        bmb, brbs, bs2 = buf("mb_sb"), buf("rb_sb"), buf("s2_sb")
        bxn = buf("xn_sc")
        boh, bpos, btp = buf("oh_all"), buf("pos_all"), buf("tp_all")
        bcarry, bag = buf("carry"), buf("ag_sb")

        def ln_stats(src_fn, src_buf, p1, p2, bp1, bp2, want_s2=False):
            sq_sb = sb["sq_sb"]; row_a = sb["row_a"]; row_b = sb["row_b"]
            row_c = sb["row_c"]; row_d = sb["row_d"]
            mb_sb = sb["mb_sb"]; rb_sb = sb["rb_sb"]; s2_sb = sb["s2_sb"]
            cx._hazard_waits("tensor", [src_buf], [bp1, bp2])
            for c in range(DC):
                srcv = src_fn(c)
                cx.step("scalar", lambda e_, srcv=srcv: e_.activation(sq_sb[:, :], srcv, AF.Square),
                        reads=[src_buf], writes=[bsq])
                cx.step("tensor", lambda e_, srcv=srcv, c=c: e_.matmul(
                    p1[0:1, 0:G], ones[:, 0:1], srcv, start=(c == 0), stop=(c == DC - 1)))
                cx.step("tensor", lambda e_, c=c: e_.matmul(
                    p2[0:1, 0:G], ones[:, 0:1], sq_sb[:, :], start=(c == 0), stop=(c == DC - 1)),
                    reads=[bsq])
            cx._hazard_update(cx.s["tensor"], cx.s["tensor"].n, [src_buf], [bp1, bp2])
            cx.step("vector", lambda e_: e_.tensor_scalar_mul(row_a[0:1, :], p1[0:1, 0:G], 1.0 / D),
                    reads=[bp1], writes=[bra])
            cx.step("vector", lambda e_: e_.tensor_scalar_mul(row_b[0:1, :], p2[0:1, 0:G], 1.0 / D),
                    reads=[bp2], writes=[brb])
            cx.step("vector", lambda e_: e_.tensor_mul(row_c[0:1, :], row_a[0:1, :], row_a[0:1, :]),
                    reads=[bra], writes=[brc])
            cx.step("vector", lambda e_: e_.tensor_tensor(row_b[0:1, :], row_b[0:1, :], row_c[0:1, :], ALU.subtract),
                    reads=[brc], writes=[brb])
            if want_s2:
                cx.step("vector", lambda e_: e_.tensor_copy(row_d[0:1, :], row_b[0:1, :]),
                        reads=[brb], writes=[brd])
            cx.step("vector", lambda e_: e_.tensor_scalar_add(row_b[0:1, :], row_b[0:1, :], 1e-5),
                    writes=[brb])
            cx.step("scalar", lambda e_: e_.activation(row_c[0:1, :], row_b[0:1, :], AF.Ln),
                    reads=[brb], writes=[brc])
            cx.step("scalar", lambda e_: e_.activation(row_b[0:1, :], row_c[0:1, :], AF.Exp, scale=-0.5),
                    reads=[brc], writes=[brb])
            cx.step("tensor", lambda e_, p1=p1: e_.matmul(p1[:, 0:G], ones[0:1, :], row_a[0:1, :], start=True, stop=True),
                    reads=[bra], writes=[bp1])
            cx.step("vector", lambda e_, p1=p1: e_.tensor_copy(mb_sb[:, :], p1[:, 0:G]),
                    reads=[bp1], writes=[bmb])
            cx.step("tensor", lambda e_, p1=p1: e_.matmul(p1[:, 0:G], ones[0:1, :], row_b[0:1, :], start=True, stop=True),
                    reads=[brb], writes=[bp1])
            cx.step("vector", lambda e_, p1=p1: e_.tensor_copy(rb_sb[:, :], p1[:, 0:G]),
                    reads=[bp1], writes=[brbs])
            if want_s2:
                cx.step("vector", lambda e_: e_.tensor_mul(row_c[0:1, :], row_b[0:1, :], row_b[0:1, :]),
                        reads=[brb], writes=[brc])
                cx.step("vector", lambda e_: e_.tensor_mul(row_d[0:1, :], row_d[0:1, :], row_c[0:1, :]),
                        reads=[brc], writes=[brd])
                cx.step("vector", lambda e_: e_.tensor_scalar_add(row_d[0:1, :], row_d[0:1, :], 1e-5),
                        writes=[brd])
                cx.step("scalar", lambda e_: e_.activation(row_c[0:1, :], row_d[0:1, :], AF.Ln),
                        reads=[brd], writes=[brc])
                cx.step("scalar", lambda e_: e_.activation(row_d[0:1, :], row_c[0:1, :], AF.Exp, scale=-0.5),
                        reads=[brc], writes=[brd])
                cx.step("tensor", lambda e_, p2=p2: e_.matmul(p2[:, 0:G], ones[0:1, :], row_d[0:1, :], start=True, stop=True),
                        reads=[brd], writes=[bp2])
                cx.step("vector", lambda e_, p2=p2: e_.tensor_copy(s2_sb[:, :], p2[:, 0:G]),
                        reads=[bp2], writes=[bs2])

        stA = ExitStack()
        xnb = mktile(stA, "xnb", [128, DC * T], BF16)
        stA1 = ExitStack()
        t1_ = lambda n, s, d=F32: mktile(stA1, n, s, d)
        sq_sb = t1_("sq_sb", [128, G])
        row_a = t1_("row_a", [1, G]); row_b = t1_("row_b", [1, G])
        row_c = t1_("row_c", [1, G]); row_d = t1_("row_d", [1, G])
        mb_sb = t1_("mb_sb", [128, G]); rb_sb = t1_("rb_sb", [128, G])
        s2_sb = t1_("s2_sb", [128, G]); xn_sc = t1_("xn_sc", [128, G])
        x_g = t1_("x_g", [128, DC * G])
        wjc = [t1_("wjc0", [128, DC * 128], BF16), t1_("wjc1", [128, DC * 128], BF16)]
        wojc = t1_("wojc", [128, DC * 128], BF16)
        phiq = t1_("phiq", [64, 16 * G], BF16)
        phik = t1_("phik", [64, 16 * G], BF16)
        v_fm = t1_("v_fm", [128, 8 * G], BF16)
        t_r1 = t1_("t_r1", [128, G]); t_r2 = t1_("t_r2", [128, G])
        t_e2 = t1_("t_e2", [128, G]); t_s = t1_("t_s", [128, G])
        v_tm = t1_("v_tm", [128, 4 * 1024], BF16)
        k_tm = t1_("k_tm", [128, 4 * 1024], BF16)
        A4_sb = t1_("A4_sb", [128, 512], BF16)
        A4_sb2 = t1_("A4_sb2", [128, 512], BF16)
        den2 = t1_("den2", [128, 512])
        S_all = t1_("S_all", [64, 16 * 128])
        S_allb = t1_("S_allb", [64, 16 * 128], BF16)
        dn4 = t1_("dn4", [1, 512]); rdn4 = t1_("rdn4", [1, 512])
        rdnb4 = t1_("rdnb4", [64, 512])
        ol_fm = t1_("ol_fm", [128, 8 * G], BF16)
        a0_fm = t1_("a0_fm", [128, 8 * G])
        lg_all = t1_("lg_all", [128, 32])
        pr4 = [{n: t1_("pr%d_" % blk + n, [128, w]) for n, w in
                [("m", 1), ("e", 8), ("s", 1), ("rs", 1), ("p", 8), ("p1", 1),
                 ("oh", 8), ("pm", 8), ("p2", 1), ("mk", 8), ("dn", 1), ("rd", 1), ("cw", 8)]}
               for blk in range(4)]
        cwst = t1_("cwst", [8, 512])

        stAp = ExitStack()
        p_mm1 = stAp.enter_context(nc.psum_tensor("p_mm1", [128, G], F32))
        p_mm2 = stAp.enter_context(nc.psum_tensor("p_mm2", [128, G], F32))
        p_bc = stAp.enter_context(nc.psum_tensor("p_bc", [128, G], F32))
        p_tp = stAp.enter_context(nc.psum_tensor("p_tp", [128, 1024], BF16))
        pA = stAp.enter_context(nc.psum_tensor("pA", [128, G], F32))
        pN = stAp.enter_context(nc.psum_tensor("pN", [128, G], F32))
        pB = stAp.enter_context(nc.psum_tensor("pB", [128, G], F32))
        pS = stAp.enter_context(nc.psum_tensor("pS", [128, G], F32))

        bxnb, bx = buf("xnb"), buf("x_g")
        bwjc = [buf("wjc0"), buf("wjc1")]
        bwo = buf("wojc")
        bphiq, bphik, bvfm = buf("phiq"), buf("phik"), buf("v_fm")
        br1, br2, be2, bts = buf("t_r1"), buf("t_r2"), buf("t_e2"), buf("t_s")
        bvtm, bktm = buf("v_tm"), buf("k_tm")
        bA = buf("A4_sb"); bA2 = buf("A4_sb2"); bden2 = buf("den2")
        bS, bSb = buf("S_all"), buf("S_allb")
        bdn, brdn, brdnb = buf("dn4"), buf("rdn4"), buf("rdnb4")
        bol = buf("ol_fm"); ba0 = buf("a0_fm")
        blga = buf("lg_all")
        bpr4 = [{n: buf("pr%d_" % blk + n) for n in pr4[blk]} for blk in range(4)]
        bcwst = buf("cwst")
        bp1, bp2, bpbc = buf("p_mm1"), buf("p_mm2"), buf("p_bc")
        bptp = buf("p_tp")
        bpA, bpN, bpB, bpS = buf("pA"), buf("pN"), buf("pB"), buf("pS")

        cx.step("vector", lambda e_: e_.memset(S_all[:, :], 0.0), writes=[bS])
        cx.step("vector", lambda e_: e_.memset(S_allb[:, :], 0.0), writes=[bSb])

        def router_tails(g):
            for blk in range(4):
                cx.step("vector", lambda e_, blk=blk: e_.reduce_max(
                    pr4[blk]["m"][:, 0:1], lg_all[:, blk * 8:(blk + 1) * 8], axis=AX.X, negate=True),
                    reads=[blga], writes=[bpr4[blk]["m"]])
            for blk in range(4):
                cx.step("scalar", lambda e_, blk=blk: e_.activation(
                    pr4[blk]["e"][:, :], lg_all[:, blk * 8:(blk + 1) * 8], AF.Exp,
                    bias=pr4[blk]["m"][:, 0:1], accum_out=pr4[blk]["s"][:, 0:1]),
                    reads=[blga, bpr4[blk]["m"]], writes=[bpr4[blk]["e"], bpr4[blk]["s"]])
            for blk in range(4):
                cx.step("vector", lambda e_, blk=blk: e_.reciprocal(
                    pr4[blk]["rs"][:, 0:1], pr4[blk]["s"][:, 0:1]),
                    reads=[bpr4[blk]["s"]], writes=[bpr4[blk]["rs"]])
            for blk in range(4):
                cx.step("vector", lambda e_, blk=blk: e_.tensor_scalar_mul(
                    pr4[blk]["p"][:, :], pr4[blk]["e"][:, :], pr4[blk]["rs"][:, 0:1]),
                    reads=[bpr4[blk]["e"], bpr4[blk]["rs"]], writes=[bpr4[blk]["p"]])
            for blk in range(4):
                cx.step("vector", lambda e_, blk=blk: e_.reduce_max(
                    pr4[blk]["p1"][:, 0:1], pr4[blk]["p"][:, :], axis=AX.X),
                    reads=[bpr4[blk]["p"]], writes=[bpr4[blk]["p1"]])
            for blk in range(4):
                cx.step("vector", lambda e_, blk=blk: e_.tensor_scalar(
                    pr4[blk]["oh"][:, :], pr4[blk]["p"][:, :], pr4[blk]["p1"][:, 0:1], None, ALU.is_ge),
                    reads=[bpr4[blk]["p"], bpr4[blk]["p1"]], writes=[bpr4[blk]["oh"]])
            for blk in range(4):
                cx.step("vector", lambda e_, blk=blk: e_.scalar_tensor_tensor(
                    pr4[blk]["pm"][:, :], pr4[blk]["oh"][:, :], BIGNEG, pr4[blk]["p"][:, :],
                    ALU.mult, ALU.add),
                    reads=[bpr4[blk]["oh"], bpr4[blk]["p"]], writes=[bpr4[blk]["pm"]])
            for blk in range(4):
                cx.step("vector", lambda e_, blk=blk: e_.reduce_max(
                    pr4[blk]["p2"][:, 0:1], pr4[blk]["pm"][:, :], axis=AX.X),
                    reads=[bpr4[blk]["pm"]], writes=[bpr4[blk]["p2"]])
            for blk in range(4):
                cx.step("vector", lambda e_, blk=blk: e_.tensor_scalar(
                    pr4[blk]["mk"][:, :], pr4[blk]["p"][:, :], pr4[blk]["p2"][:, 0:1], None, ALU.is_ge),
                    reads=[bpr4[blk]["p"], bpr4[blk]["p2"]], writes=[bpr4[blk]["mk"]])
            for blk in range(4):
                cx.step("vector", lambda e_, blk=blk: e_.tensor_add(
                    pr4[blk]["dn"][:, 0:1], pr4[blk]["p1"][:, 0:1], pr4[blk]["p2"][:, 0:1]),
                    reads=[bpr4[blk]["p1"], bpr4[blk]["p2"]], writes=[bpr4[blk]["dn"]])
            for blk in range(4):
                cx.step("vector", lambda e_, blk=blk: e_.tensor_scalar_add(
                    pr4[blk]["dn"][:, 0:1], pr4[blk]["dn"][:, 0:1], 1e-9),
                    writes=[bpr4[blk]["dn"]])
            for blk in range(4):
                cx.step("vector", lambda e_, blk=blk: e_.reciprocal(
                    pr4[blk]["rd"][:, 0:1], pr4[blk]["dn"][:, 0:1]),
                    reads=[bpr4[blk]["dn"]], writes=[bpr4[blk]["rd"]])
            for blk in range(4):
                cx.step("vector", lambda e_, blk=blk: e_.tensor_mul(
                    pr4[blk]["cw"][:, :], pr4[blk]["p"][:, :], pr4[blk]["mk"][:, :]),
                    reads=[bpr4[blk]["p"], bpr4[blk]["mk"]], writes=[bpr4[blk]["cw"]])
            for blk in range(4):
                cx.step("vector", lambda e_, blk=blk: e_.tensor_scalar_mul(
                    pr4[blk]["cw"][:, :], pr4[blk]["cw"][:, :], pr4[blk]["rd"][:, 0:1]),
                    reads=[bpr4[blk]["rd"]], writes=[bpr4[blk]["cw"]])
            for blk in range(4):
                cx.step("tensor", lambda e_, blk=blk: e_.transpose(
                    p_mm1[0:8, blk * 128:(blk + 1) * 128], pr4[blk]["cw"][:, :], I128[:, :]),
                    reads=[bpr4[blk]["cw"], buf("I128")], writes=[bp1])
            for blk in range(4):
                cx.step("vector", lambda e_, blk=blk: e_.tensor_copy(
                    cwst[:, blk * 128:(blk + 1) * 128], p_mm1[0:8, blk * 128:(blk + 1) * 128]),
                    reads=[bp1], writes=[bcwst])
            for blk in range(4):
                tl = g * 4 + blk
                cx.step("vector", lambda e_, blk=blk, tl=tl: e_.tensor_copy(
                    cwT_sb[0:8, tl * 128:(tl + 1) * 128], cwst[:, blk * 128:(blk + 1) * 128]),
                    reads=[bcwst], writes=[bcw])

        for g in range(NT):
            cx.dma("sync", fm3(x_g, G), fm(cx.ap["xT"][:, g * G:(g + 1) * G]),
                   writes=[bx], sem=cx.s["sync"])
            ln_stats(lambda c: x_g[:, c * G:(c + 1) * G], bx, p_mm1, p_mm2, bp1, bp2)
            p_rt = [pA, pN, pB, pS]
            b_rt = [bpA, bpN, bpB, bpS]
            for c in range(DC):
                cx.step("vector", lambda e_, c=c, xn_sc=xn_sc, mb_sb=mb_sb: e_.tensor_tensor(
                    xn_sc[:, :], x_g[:, c * G:(c + 1) * G], mb_sb[:, :], ALU.subtract),
                    reads=[bx, bmb], writes=[bxn])
                cx.step("vector", lambda e_, xn_sc=xn_sc, rb_sb=rb_sb: e_.tensor_tensor(
                    xn_sc[:, :], xn_sc[:, :], rb_sb[:, :], ALU.mult),
                    reads=[brbs], writes=[bxn])
                cx.step("vector", lambda e_, c=c, g=g, xn_sc=xn_sc: e_.tensor_copy(
                    xnb[:, c * T + g * G: c * T + (g + 1) * G], xn_sc[:, :]),
                    reads=[bxn], writes=[bxnb])
                cx._hazard_waits("tensor", [bxn, buf("rw_f")], [])
                for blk in range(4):
                    if c == 0:
                        cx._hazard_waits("tensor", [], [b_rt[blk]])
                    cx.step("tensor", lambda e_, c=c, blk=blk, xn_sc=xn_sc: e_.matmul(
                        p_rt[blk][:, 0:8], xn_sc[:, blk * 128:(blk + 1) * 128],
                        rw_f[:, c * 8:(c + 1) * 8], start=(c == 0), stop=(c == DC - 1)))
                    if c == DC - 1:
                        cx._hazard_update(cx.s["tensor"], cx.s["tensor"].n, [bxn, buf("rw_f")], [b_rt[blk]])
                    else:
                        b_rt[blk].readers.append((cx.s["tensor"], cx.s["tensor"].n))
                        bxn.readers.append((cx.s["tensor"], cx.s["tensor"].n))
            for blk in range(4):
                cx.step("vector", lambda e_, blk=blk: e_.tensor_copy(
                    lg_all[:, blk * 8:(blk + 1) * 8], p_rt[blk][:, 0:8]),
                    reads=[b_rt[blk]], writes=[blga])

            prot = [(p_mm1, bp1), (p_mm2, bp2), (p_bc, bpbc), (pA, bpA)]
            for jc in range(24):
                wb, bwb = wjc[jc % 2], bwjc[jc % 2]
                cx.dma("gpsimd", fm3(wb, 128), fm(cx.ap["WqkvT"][:, jc * 128:(jc + 1) * 128]),
                       writes=[bwb], sem=cx.s["ws%d" % (jc % 2)])
                ps, bps = prot[jc % 4]
                cx._hazard_waits("tensor", [bxnb, bwb], [bps])
                for c in range(DC):
                    cx.step("tensor", lambda e_, c=c, jc=jc, g=g, wb=wb, ps=ps: e_.matmul(
                        ps[:, :], wb[:, c * 128:(c + 1) * 128],
                        xnb[:, c * T + g * G: c * T + (g + 1) * G],
                        start=(c == 0), stop=(c == DC - 1)))
                cx._hazard_update(cx.s["tensor"], cx.s["tensor"].n, [bxnb, bwb], [bps])
                if jc < 16:
                    is_q = jc < 8
                    dst, bdst, dj = (phiq, bphiq, jc) if is_q else (phik, bphik, jc - 8)
                    cx.step("scalar", lambda e_, jc=jc, ps=ps: e_.activation(
                        t_r1[:, :], ps[:, :], AF.Relu, bias=bq_cols[:, jc:jc + 1]),
                        reads=[bps, buf("bq_cols")], writes=[br1])
                    cx.step("scalar", lambda e_, jc=jc, ps=ps: e_.activation(
                        t_r2[:, :], ps[:, :], AF.Relu, bias=nbq_cols[:, jc:jc + 1], scale=-1.0),
                        reads=[bps, buf("nbq_cols")], writes=[br2])
                    cx.step("scalar", lambda e_: e_.activation(t_e2[:, :], t_r2[:, :], AF.Exp, scale=-1.0),
                            reads=[br2], writes=[be2])
                    cx.step("vector", lambda e_: e_.tensor_add(t_s[:, :], t_r1[:, :], t_e2[:, :]),
                            reads=[br1, be2], writes=[bts])
                    for half in range(2):
                        h_ = dj * 2 + half
                        dsl = dst[0:64, h_ * G:(h_ + 1) * G]
                        ssl = t_s[half * 64:(half + 1) * 64, :]
                        if is_q:
                            cx.step("vector", lambda e_, dsl=dsl, ssl=ssl: e_.tensor_scalar(
                                dsl, ssl, 0.125, 1.2501250e-05, ALU.mult, ALU.add),
                                reads=[bts], writes=[bdst])
                        else:
                            cx.step("vector", lambda e_, dsl=dsl, ssl=ssl: e_.tensor_scalar_add(
                                dsl, ssl, 1e-4),
                                reads=[bts], writes=[bdst])
                else:
                    dj = jc - 16
                    cx.step("scalar", lambda e_, jc=jc, dj=dj, ps=ps: e_.activation(
                        v_fm[:, dj * G:(dj + 1) * G], ps[:, :], AF.Identity, bias=bq_cols[:, jc:jc + 1]),
                        reads=[bps, buf("bq_cols")], writes=[bvfm])

            jobs = []
            for h in range(16):
                for tc in range(4):
                    jobs.append((phik[0:64, h * G + tc * 128: h * G + (tc + 1) * 128],
                                 I128b[0:64, 0:64], 64,
                                 k_tm[:, tc * 1024 + h * 64: tc * 1024 + (h + 1) * 64],
                                 bphik, bktm))
            for fc in range(8):
                for tc in range(4):
                    jobs.append((v_fm[:, fc * G + tc * 128: fc * G + (tc + 1) * 128],
                                 I128b[:, :], 128,
                                 v_tm[:, tc * 1024 + fc * 128: tc * 1024 + (fc + 1) * 128],
                                 bvfm, bvtm))
            bi = 0
            while bi < len(jobs):
                batch, used = [], 0
                while bi < len(jobs) and used + jobs[bi][2] <= 1024:
                    batch.append((used, jobs[bi]))
                    used += jobs[bi][2]
                    bi += 1
                cx._hazard_waits("tensor", [j[4] for (_, j) in batch] + [buf("I128b")], [bptp])
                for off, (in_ap, id_ap, w, dst_ap, bsrc, bdst) in batch:
                    cx.step("tensor", lambda e_, in_ap=in_ap, id_ap=id_ap, off=off, w=w: e_.transpose(
                        p_tp[:, off:off + w], in_ap, id_ap))
                cx._hazard_update(cx.s["tensor"], cx.s["tensor"].n,
                                  [j[4] for (_, j) in batch], [bptp])
                for off, (in_ap, id_ap, w, dst_ap, bsrc, bdst) in batch:
                    cx.step("vector", lambda e_, dst_ap=dst_ap, off=off, w=w: e_.tensor_copy(
                        dst_ap, p_tp[:, off:off + w]),
                        reads=[bptp], writes=[bdst])

            WAVES = [[0, 2, 4, 6], [8, 10, 12, 14], [1, 3, 5, 7], [9, 11, 13, 15]]
            wsets = [(pA, bpA, pN, bpN, pB, bpB, A4_sb, bA,
                      dn4[0:1, :], rdn4[0:1, :], rdnb4[:, :], bdn, brdn, brdnb),
                     (p_mm1, bp1, p_mm2, bp2, p_bc, bpbc, A4_sb2, bA2,
                      den2[32:33, :], den2[0:1, :], den2[64:128, :], bden2, bden2, bden2)]
            widx = 0
            for tc in range(4):
                for wv in WAVES:
                    (pAx, bpAx, pNx, bpNx, pBx, bpBx, Ax, bAx,
                     dnv, rdnv, rbv, bdnx, brdnx, brbx) = wsets[widx % 2]
                    widx += 1
                    rb0 = (wv[0] % 2) * 64
                    cx._hazard_waits("tensor", [bphik, bphiq], [bpAx])
                    for i, h in enumerate(wv):
                        cx.step("tensor", lambda e_, h=h, i=i, tc=tc, pAx=pAx: e_.matmul(
                            pAx[:, i * 128:(i + 1) * 128],
                            phik[0:64, h * G + tc * 128: h * G + (tc + 1) * 128],
                            phiq[0:64, h * G + tc * 128: h * G + (tc + 1) * 128],
                            start=True, stop=True))
                    cx._hazard_update(cx.s["tensor"], cx.s["tensor"].n, [bphik, bphiq], [bpAx])
                    cx.step("vector", lambda e_, Ax=Ax, pAx=pAx: e_.tensor_mul(
                        Ax[:, :], pAx[:, 0:512], U512[:, :]),
                        reads=[bpAx, buf("U512")], writes=[bAx])
                    cx._hazard_waits("tensor", [bSb, bAx, bvtm, bphiq, buf("onesb")], [bpNx])
                    for i, h in enumerate(wv):
                        cx.step("tensor", lambda e_, h=h, i=i, tc=tc, pNx=pNx: e_.matmul(
                            pNx[0:65, i * 128:(i + 1) * 128],
                            S_allb[0:64, h * 128: h * 128 + 65],
                            phiq[0:64, h * G + tc * 128: h * G + (tc + 1) * 128],
                            start=True, stop=False))
                        cx.step("tensor", lambda e_, h=h, i=i, tc=tc, pNx=pNx, Ax=Ax: e_.matmul(
                            pNx[0:64, i * 128:(i + 1) * 128],
                            v_tm[:, tc * 1024 + h * 64: tc * 1024 + (h + 1) * 64],
                            Ax[:, i * 128:(i + 1) * 128], start=False, stop=False))
                        cx.step("tensor", lambda e_, i=i, pNx=pNx, Ax=Ax: e_.matmul(
                            pNx[64:65, i * 128:(i + 1) * 128], onesb[:, 0:1],
                            Ax[:, i * 128:(i + 1) * 128], start=False, stop=True))
                    cx._hazard_update(cx.s["tensor"], cx.s["tensor"].n,
                                      [bSb, bAx, bvtm, bphiq, buf("onesb")], [bpNx])
                    cx.step("vector", lambda e_, dnv=dnv, pNx=pNx: e_.tensor_scalar_add(
                        dnv, pNx[64:65, 0:512], 1e-4),
                        reads=[bpNx], writes=[bdnx])
                    cx.step("vector", lambda e_, rdnv=rdnv, dnv=dnv: e_.reciprocal(rdnv, dnv),
                            reads=[bdnx], writes=[brdnx])
                    cx.step("tensor", lambda e_, pBx=pBx, rdnv=rdnv: e_.matmul(
                        pBx[0:64, 0:512], ones[0:1, 0:64], rdnv, start=True, stop=True),
                        reads=[brdnx], writes=[bpBx])
                    cx.step("vector", lambda e_, rbv=rbv, pBx=pBx: e_.tensor_copy(rbv, pBx[0:64, 0:512]),
                            reads=[bpBx], writes=[brbx])
                    hh0 = wv[0] // 2
                    ol_v = ol_fm[rb0:rb0 + 64, :].rearrange("p (hh t) -> p hh t", t=G)[
                        :, hh0:hh0 + 4, tc * 128:(tc + 1) * 128]
                    cx.step("vector", lambda e_, ol_v=ol_v, pNx=pNx, rbv=rbv: e_.tensor_mul(
                        ol_v, pNx[0:64, 0:512].rearrange("p (i t) -> p i t", t=128),
                        rbv.rearrange("p (i t) -> p i t", t=128)),
                        reads=[bpNx, brbx], writes=[bol])
                    cx._hazard_waits("tensor", [bktm, bvtm, buf("onesb")], [bpS])
                    for i, h in enumerate(wv):
                        cx.step("tensor", lambda e_, h=h, i=i, tc=tc: e_.matmul(
                            pS[0:64, i * 128: i * 128 + 64],
                            k_tm[:, tc * 1024 + h * 64: tc * 1024 + (h + 1) * 64],
                            v_tm[:, tc * 1024 + h * 64: tc * 1024 + (h + 1) * 64],
                            start=True, stop=False))
                        cx.step("tensor", lambda e_, h=h, i=i, tc=tc: e_.matmul(
                            pS[0:64, i * 128 + 64: i * 128 + 65],
                            k_tm[:, tc * 1024 + h * 64: tc * 1024 + (h + 1) * 64],
                            onesb[:, 0:1], start=False, stop=True))
                    cx._hazard_update(cx.s["tensor"], cx.s["tensor"].n,
                                      [bktm, bvtm, buf("onesb")], [bpS])
                    S_v = S_all[0:64, :].rearrange("p (h c) -> p h c", c=128)[:, wv[0]:min(wv[0] + 8, 16):2, :]
                    Sb_v = S_allb[0:64, :].rearrange("p (h c) -> p h c", c=128)[:, wv[0]:min(wv[0] + 8, 16):2, :]
                    ps_v = pS[0:64, 0:512].rearrange("p (i c) -> p i c", c=128)
                    cx.step("vector", lambda e_, S_v=S_v: e_.tensor_add(S_v, S_v, ps_v),
                            reads=[bpS], writes=[bS])
                    cx.step("vector", lambda e_, S_v=S_v, Sb_v=Sb_v: e_.tensor_copy(Sb_v, S_v),
                            reads=[bS], writes=[bSb])

            router_tails(g)

            for jc in range(DC):
                cx.dma("gpsimd", fm3(wojc, 128), fm(cx.ap["WoutT"][:, jc * 128:(jc + 1) * 128]),
                       writes=[bwo], sem=cx.s["gpsimd"])
                ps, bps = prot[jc % 4]
                cx._hazard_waits("tensor", [bol, bwo], [bps])
                for c in range(DC):
                    cx.step("tensor", lambda e_, jc=jc, c=c, ps=ps: e_.matmul(
                        ps[:, :], wojc[:, c * 128:(c + 1) * 128],
                        ol_fm[:, c * G:(c + 1) * G], start=(c == 0), stop=(c == DC - 1)))
                cx._hazard_update(cx.s["tensor"], cx.s["tensor"].n, [bol, bwo], [bps])
                cx.step("scalar", lambda e_, jc=jc, ps=ps: e_.activation(
                    a0_fm[:, jc * G:(jc + 1) * G], ps[:, :], AF.Identity, bias=bout_cols[:, jc:jc + 1]),
                    reads=[bps, buf("bout_cols")], writes=[ba0])

            ln_stats(lambda c: a0_fm[:, c * G:(c + 1) * G], ba0, p_mm1, p_mm2, bp1, bp2)
            for c in range(DC):
                sl = a0_fm[:, c * G:(c + 1) * G]
                cx.step("vector", lambda e_, sl=sl, mb_sb=mb_sb: e_.tensor_tensor(
                    sl, sl, mb_sb[:, :], ALU.subtract),
                    reads=[bmb], writes=[ba0])
                cx.step("vector", lambda e_, sl=sl, rb_sb=rb_sb: e_.tensor_tensor(
                    sl, sl, rb_sb[:, :], ALU.mult),
                    reads=[brbs], writes=[ba0])
                cx.step("vector", lambda e_, sl=sl, c=c: e_.tensor_scalar(
                    sl, sl, gamma_cols[:, c:c + 1], beta_cols[:, c:c + 1], ALU.mult, ALU.add),
                    reads=[buf("gamma_cols"), buf("beta_cols")], writes=[ba0])
                cx.step("vector", lambda e_, sl=sl, c=c: e_.tensor_add(
                    x_g[:, c * G:(c + 1) * G], x_g[:, c * G:(c + 1) * G], sl),
                    reads=[ba0], writes=[bx])
                cx.dma("sync", acc_d[c * 128:(c + 1) * 128, g * G:(g + 1) * G],
                       x_g[:, c * G:(c + 1) * G], reads=[bx, bacc_d], writes=[bacc_d], sem=cx.s["out"])

        stAp.close()
        stA1.close()
        cx.barrier(B)

        if upto >= 2:
            stB = ExitStack()
            t2_ = lambda n, s, d=F32: mktile(stB, n, s, d)
            acc_nt = t2_("acc_nt", [128, DC * 1024])
            h_sb = t2_("h_sb", [128, 32 * 1024], BF16)
            wbufs = [t2_("wbufA", [128, 8 * 512], BF16), t2_("wbufB", [128, 8 * 512], BF16)]
            cwb_sb = t2_("cwb_sb", [128, 2 * G])
            ytmp = t2_("ytmp", [128, G])
            stBp = ExitStack()
            hp = [stBp.enter_context(nc.psum_tensor("hp0", [128, G], F32)),
                  stBp.enter_context(nc.psum_tensor("hp1", [128, G], F32))]
            yp = [stBp.enter_context(nc.psum_tensor("yp0", [128, G], F32)),
                  stBp.enter_context(nc.psum_tensor("yp1", [128, G], F32))]
            cp = stBp.enter_context(nc.psum_tensor("cp", [128, G], F32))
            bacc = buf("acc_nt"); bh = buf("h_sb")
            bwbufs = [buf("wbufA"), buf("wbufB")]
            bcwb, byt = buf("cwb_sb"), buf("ytmp")
            bhp = [buf("hp0"), buf("hp1")]
            byp = [buf("yp0"), buf("yp1")]
            bcp = buf("cp")
            yp4 = [hp[0], hp[1], yp[0], yp[1]]
            byp4 = [bhp[0], bhp[1], byp[0], byp[1]]
            NTOK = 1024
            for nt in range(2):
                for c in range(DC):
                    cx.dma("sync", acc_nt[:, c * 1024:(c + 1) * 1024],
                           acc_d[c * 128:(c + 1) * 128, nt * NTOK:(nt + 1) * NTOK],
                           reads=[bacc_d], writes=[bacc], sem=cx.s["sync"])
                for ex in range(8):
                    emit_moe(cx, ex=ex, nt=nt, T_=T, G_=G, xnb=xnb, bxnb=bxnb,
                             h_sb=h_sb, bh=bh, wbufs=wbufs, bwbufs=bwbufs,
                             accsl_fn=lambda jc, ns, nt=nt: acc_nt[:, jc * 1024 + ns * G: jc * 1024 + (ns + 1) * G],
                             bacc=bacc, cw_tile=cwT_sb, bcw=bcw, sel8=sel8, bsel8=buf("sel8"),
                             cwb_sb=cwb_sb, bcwb=bcwb, ytmp=ytmp, byt=byt,
                             yts=None, byts=None, b1_cols=None, bb1=None,
                             b2_cols=None, bb2=None, hp=hp, bhp=bhp, yp4=yp4, byp4=byp4,
                             cp=cp, bcp=bcp, w1_ap=cx.ap["w1_0"], w2_ap=cx.ap["w2_0"], act="gelu")
                for c in range(DC):
                    cx.dma("sync", acc_d[c * 128:(c + 1) * 128, nt * NTOK:(nt + 1) * NTOK],
                           acc_nt[:, c * 1024:(c + 1) * 1024],
                           reads=[bacc, bacc_d], writes=[bacc_d], sem=cx.s["out"])
            stBp.close()
            stB.close()
        stA.close()
        cx.barrier(B)
        if dbg and upto >= 2:
            cx.dma("sync", cx.ap["dbg_p2"], acc_d, reads=[bacc_d], sem=cx.s["out"])
            cx.barrier(B)

        ones = sb["ones"]; onesb = sb["onesb"]; I128 = sb["I128"]; I128b = sb["I128b"]
        U128 = sb["U128"]; sel8 = sb["sel8"]
        cc_sem = Sem(e(nc.semaphore("s_cc")), "gpsimd")
        xnb2 = tile("xnb2", [128, DC * T], BF16)
        bxnb2 = buf("xnb2")

        if upto >= 3 or p3_alloc_only:
            _alloc_sbuf = p3_alloc_only in (True, "sbuf", "scratch", "xn2ab")
            _alloc_big = p3_alloc_only in (True, "sbuf", "xn2ab")
            _alloc_scr = p3_alloc_only in (True, "sbuf", "scratch")
            _alloc_psum = p3_alloc_only in (True, "psum")
            _do_memset = p3_alloc_only in (True, "sbuf", "memset")
            stC = ExitStack()
            stC1 = ExitStack()
            stCp = ExitStack()
            if (not p3_alloc_only) or _alloc_big:
                xn2ab = mktile(stC, "xn2ab", [128, DC * T], BF16)
            t3_ = lambda n, s, d=F32: mktile(stC1, n, s, d)
            if p3_alloc_only and not _alloc_scr:
                t3_ = lambda n, s, d=F32: None
            sq_sb = t3_("sq_sb", [128, G])
            row_a = t3_("row_a", [1, G]); row_b = t3_("row_b", [1, G])
            row_c = t3_("row_c", [1, G]); row_d = t3_("row_d", [1, G])
            mb_sb = t3_("mb_sb", [128, G]); rb_sb = t3_("rb_sb", [128, G])
            s2_sb = t3_("s2_sb", [128, G]); xn_sc = t3_("xn_sc", [128, G])
            x1g = t3_("x1g", [128, DC * G])
            lg = t3_("lg", [128, 8])
            _ = None
            g_m = t3_("g_m", [128, 1]); g_e = t3_("g_e", [128, 8]); g_s = t3_("g_s", [128, 1])
            g_rs = t3_("g_rs", [128, 1]); g_p = t3_("g_p", [128, 8])
            cs_sb = t3_("cs_sb", [128, 8]); sel_sb = t3_("sel_sb", [128, 8])
            if not p3_alloc_only or _alloc_psum:
                pS1 = stCp.enter_context(nc.psum_tensor("c_pS1", [128, G], F32))
                pS2 = stCp.enter_context(nc.psum_tensor("c_pS2", [128, G], F32))
                g_ps = [stCp.enter_context(nc.psum_tensor(f"c_g{i}", [128, 8], F32)) for i in range(4)]
            bx1, bxab = buf("x1g"), buf("xn2ab")
            blg = buf("lg")
            bgm, bge, bgs, bgrs, bgp = buf("g_m"), buf("g_e"), buf("g_s"), buf("g_rs"), buf("g_p")
            bcs, bsel = buf("cs_sb"), buf("sel_sb")
            bpS1, bpS2 = buf("c_pS1"), buf("c_pS2")
            bg_ps = [buf(f"c_g{i}") for i in range(4)]

            if (not p3_alloc_only) or _do_memset:
                cx.step("vector", lambda e_: e_.memset(carry[0:1, :], 0.0), writes=[bcarry])

            for g in range(NT if not p3_alloc_only else 0):
                for c in range(DC):
                    cx.dma("sync", x1g[:, c * G:(c + 1) * G],
                           acc_d[c * 128:(c + 1) * 128, g * G:(g + 1) * G],
                           reads=[bacc_d], writes=[bx1], sem=cx.s["sync"])
                ln_stats(lambda c: x1g[:, c * G:(c + 1) * G], bx1, pS1, pS2, bpS1, bpS2,
                         want_s2=True)
                for c in range(DC):
                    cx.step("vector", lambda e_, c=c: e_.tensor_tensor(
                        xn_sc[:, :], x1g[:, c * G:(c + 1) * G], mb_sb[:, :], ALU.subtract),
                        reads=[bx1, bmb], writes=[bxn])
                    cx.step("vector", lambda e_: e_.tensor_tensor(xn_sc[:, :], xn_sc[:, :], rb_sb[:, :], ALU.mult),
                            reads=[brbs], writes=[bxn])
                    cx.step("vector", lambda e_, c=c, g=g: e_.tensor_copy(
                        xnb2[:, c * T + g * G: c * T + (g + 1) * G], xn_sc[:, :]),
                        reads=[bxn], writes=[bxnb2])
                    cx.step("vector", lambda e_, c=c: e_.tensor_add(
                        x1g[:, c * G:(c + 1) * G], x1g[:, c * G:(c + 1) * G], xn_sc[:, :]),
                        reads=[bxn], writes=[bx1])
                    cx._hazard_waits("tensor", [bxn, buf("gw_f")], [])
                    for blk in range(4):
                        if c == 0:
                            cx._hazard_waits("tensor", [], [bg_ps[blk]])
                        cx.step("tensor", lambda e_, c=c, blk=blk: e_.matmul(
                            g_ps[blk][:, 0:8], xn_sc[:, blk * 128:(blk + 1) * 128],
                            gw_f[:, c * 8:(c + 1) * 8], start=(c == 0), stop=False))
                        if c == DC - 1:
                            cx.step("tensor", lambda e_, blk=blk: e_.matmul(
                                g_ps[blk][:, 0:8], ones[0:1, :], gb_row[0:1, :],
                                start=False, stop=True),
                                reads=[buf("ones"), buf("gb_row")])
                            cx._hazard_update(cx.s["tensor"], cx.s["tensor"].n,
                                              [bxn, buf("gw_f")], [bg_ps[blk]])
                        else:
                            bg_ps[blk].readers.append((cx.s["tensor"], cx.s["tensor"].n))
                            bxn.readers.append((cx.s["tensor"], cx.s["tensor"].n))
                    cx.step("vector", lambda e_: e_.tensor_tensor(xn_sc[:, :], xn_sc[:, :], s2_sb[:, :], ALU.mult),
                            reads=[bs2], writes=[bxn])
                    cx.step("vector", lambda e_, c=c, g=g: e_.tensor_scalar(
                        xn2ab[:, c * T + g * G: c * T + (g + 1) * G], xn_sc[:, :],
                        lng_cols[:, c:c + 1], lnb_cols[:, c:c + 1], ALU.mult, ALU.add),
                        reads=[bxn, buf("lng_cols"), buf("lnb_cols")], writes=[bxab])
                for c in range(DC):
                    cx.dma("sync", acc_d[c * 128:(c + 1) * 128, g * G:(g + 1) * G],
                           x1g[:, c * G:(c + 1) * G], reads=[bx1, bacc_d], writes=[bacc_d],
                           sem=cx.s["out"])
                for blk in range(4):
                    tl = g * 4 + blk
                    cx.step("vector", lambda e_, blk=blk: e_.tensor_copy(lg[:, :], g_ps[blk][:, 0:8]),
                            reads=[bg_ps[blk]], writes=[blg])
                    cx.step("vector", lambda e_: e_.reduce_max(g_m[:, 0:1], lg[:, :], axis=AX.X, negate=True),
                            reads=[blg], writes=[bgm])
                    cx.step("scalar", lambda e_: e_.activation(
                        g_e[:, :], lg[:, :], AF.Exp, bias=g_m[:, 0:1], accum_out=g_s[:, 0:1]),
                        reads=[blg, bgm], writes=[bge, bgs])
                    cx.step("vector", lambda e_: e_.reciprocal(g_rs[:, 0:1], g_s[:, 0:1]),
                            reads=[bgs], writes=[bgrs])
                    cx.step("vector", lambda e_: e_.tensor_scalar_mul(g_p[:, :], g_e[:, :], g_rs[:, 0:1]),
                            reads=[bge, bgrs], writes=[bgp])
                    cx.step("vector", lambda e_, tl=tl: e_.reduce_max(
                        tp_all[:, tl:tl + 1], g_p[:, :], axis=AX.X),
                        reads=[bgp], writes=[btp])
                    cx.step("vector", lambda e_, tl=tl: e_.tensor_scalar(
                        oh_all[:, tl * 8:(tl + 1) * 8], g_p[:, :], tp_all[:, tl:tl + 1], None, ALU.is_ge),
                        reads=[bgp, btp], writes=[boh])
                    cx._hazard_waits("tensor", [boh, buf("U128"), bcarry, buf("ones")], [bpS1])
                    cx.step("tensor", lambda e_, tl=tl: e_.matmul(
                        pS1[:, 0:8], U128[:, :], oh_all[:, tl * 8:(tl + 1) * 8], start=True, stop=False))
                    cx.step("tensor", lambda e_: e_.matmul(
                        pS1[:, 0:8], ones[0:1, :], carry[0:1, :], start=False, stop=True))
                    cx._hazard_update(cx.s["tensor"], cx.s["tensor"].n,
                                      [boh, buf("U128"), bcarry, buf("ones")], [bpS1])
                    cx.step("vector", lambda e_: e_.tensor_copy(cs_sb[:, :], pS1[:, 0:8]),
                            reads=[bpS1], writes=[bcs])
                    cx.step("vector", lambda e_, tl=tl: e_.tensor_mul(
                        sel_sb[:, :], oh_all[:, tl * 8:(tl + 1) * 8], cs_sb[:, :]),
                        reads=[boh, bcs], writes=[bsel])
                    cx.step("vector", lambda e_, tl=tl: e_.reduce_sum(
                        pos_all[:, tl:tl + 1], sel_sb[:, :], axis=AX.X),
                        reads=[bsel], writes=[bpos])
                    cx.step("tensor", lambda e_: e_.matmul(pS2[0:1, 0:8], I128[:, 127:128], cs_sb[:, :],
                                                           start=True, stop=True),
                            reads=[buf("I128"), bcs], writes=[bpS2])
                    cx.step("vector", lambda e_: e_.tensor_copy(carry[0:1, :], pS2[0:1, 0:8]),
                            reads=[bpS2], writes=[bcarry])
            if dbg:
                for c in range(DC):
                    cx.dma("sync", cx.ap["dbg_xnb2"][c * 128:(c + 1) * 128, :],
                           xnb2[:, c * T:(c + 1) * T], reads=[bxnb2], sem=cx.s["out"])
            if p3_alloc_only:
                stCp.close(); stC1.close(); stC.close()
                cx.finish(block)
                return nc
            cx.dma("gpsimd", cc_in.ap(), carry[0:1, :], reads=[bcarry], sem=cx.s["gpsimd"])
            gn = cx.s["gpsimd"].n
            cx.q["gpsimd"].append(lambda e_, t=gn: e_.wait_ge(cx.s["gpsimd"].h, t))

            def do_cc(e_):
                return e_.collective_compute(
                    "AllGather", ALU.bypass, replica_groups=[list(range(8))],
                    ins=[cc_in.ap().opt()], outs=[cc_out.ap().opt()])
            cx.q["gpsimd"].append(lambda e_: do_cc(e_).then_inc(cc_sem.h))
            cc_sem.n += 1
            stCp.close()
            stC1.close()
            cx.barrier(B, exclude=())

        if upto >= 4:
            stD = ExitStack()
            t4_ = lambda n, s, d=F32: mktile(stD, n, s, d)
            out_all = t4_("out_all", [128, 4 * T], BF16)
            maskTW = t4_("maskTW", [128, 7 * 128])
            cx.dma("sync", maskTW[:, :], cx.ap["maskTW"], writes=[buf("maskTW")], sem=cx.s["sync"])
            cos_h = t4_("cos_h", [128, T]); sin_h = t4_("sin_h", [128, T])
            qr = t4_("qr", [128, G]); t1 = t4_("t1", [128, G])
            q_hd = t4_("q_hd", [128, T], BF16); k_hd = t4_("k_hd", [128, T], BF16)
            v_tm4 = t4_("v_tm4", [128, 16 * 128], BF16)
            PT = t4_("PT", [128, 16 * G], BF16)
            PgT = t4_("PgT", [128, 16 * G], BF16)
            om = t4_("om", [128, G]); og = t4_("og", [128, G])
            wqh = t4_("wqh", [128, DC * 128], BF16); wkh = t4_("wkh", [128, DC * 128], BF16)
            wvh = t4_("wvh", [128, DC * 128], BF16)
            wo4 = t4_("wo4", [128, DC * 128], BF16)
            nm_all = t4_("nm_all", [128, 16]); nm_grp = t4_("nm_grp", [128, 8])
            nm_g4 = t4_("nm_g4", [128, 4])
            mT_sb = t4_("mT_sb", [8, 128]); mTr = t4_("mTr", [8, 1])
            negm_cols = t4_("negm_cols", [128, 4])
            qp_row = t4_("qp_row", [1, T]); kp_row = t4_("kp_row", [1, T])
            gq_mx = t4_("gq_mx", [1, 8]); gq_mn = t4_("gq_mn", [1, 8])
            gk_mx = t4_("gk_mx", [1, 8]); gk_mn = t4_("gk_mn", [1, 8])
            mg_row = t4_("mg_row", [1, 8]); mg4 = t4_("mg4", [1, 8])
            mg_col = t4_("mg_col", [8, 1]); negmg_cols = t4_("negmg_cols", [128, 4])
            Zrow = t4_("Zrow", [1, G]); Zgrow = t4_("Zgrow", [1, G])
            rZrow = Zrow; rZgrow = Zgrow
            rZb = t4_("rZb", [128, G]); rZgb = t4_("rZgb", [128, G])
            hs_col = t4_("hs_col", [128, 1])
            stDp = ExitStack()
            pS1 = stDp.enter_context(nc.psum_tensor("d_pS1", [128, G], F32))
            pS2 = stDp.enter_context(nc.psum_tensor("d_pS2", [128, G], F32))
            pT = stDp.enter_context(nc.psum_tensor("d_pT", [128, G], F32))
            pT2 = stDp.enter_context(nc.psum_tensor("d_pT2", [128, G], F32))
            pZ = stDp.enter_context(nc.psum_tensor("d_pZ", [1, G], F32))
            pZg = stDp.enter_context(nc.psum_tensor("d_pZg", [1, G], F32))
            pAV = stDp.enter_context(nc.psum_tensor("d_pAV", [128, G], F32))
            pAVg = stDp.enter_context(nc.psum_tensor("d_pAVg", [128, G], F32))

            cx.dma("sync", cos_h[:, :], cx.ap["cos_h"], writes=[buf("cos_h")], sem=cx.s["sync"])
            cx.dma("sync", sin_h[:, :], cx.ap["sin_h"], writes=[buf("sin_h")], sem=cx.s["sync"])
            sq_sb = t4_("sq4", [128, 1])
            bsq = buf("sq4")

            boa = buf("out_all")
            bqr, bt1, bqh, bkh, bvtm = buf("qr"), buf("t1"), buf("q_hd"), buf("k_hd"), buf("v_tm4")
            bPT, bPgT, bom, bog = buf("PT"), buf("PgT"), buf("om"), buf("og")
            bwq, bwk, bwv, bwo4 = buf("wqh"), buf("wkh"), buf("wvh"), buf("wo4")

            bnm, bnmg = buf("nm_all"), buf("nm_grp")
            bnm4 = buf("nm_g4")
            bmT, bmTr, bnmc = buf("mT_sb"), buf("mTr"), buf("negm_cols")
            bqp, bkp = buf("qp_row"), buf("kp_row")
            bgq_mx, bgq_mn = buf("gq_mx"), buf("gq_mn")
            bgk_mx, bgk_mn = buf("gk_mx"), buf("gk_mn")
            bmgr, bmg4, bmgc, bnmgc = buf("mg_row"), buf("mg4"), buf("mg_col"), buf("negmg_cols")
            bZ, bZg = buf("Zrow"), buf("Zgrow")
            brZ, brZg = bZ, bZg
            bZb, bZgb = buf("rZb"), buf("rZgb")
            bhsc = buf("hs_col")
            bpS1, bpS2 = buf("d_pS1"), buf("d_pS2")
            bpT, bpT2 = buf("d_pT"), buf("d_pT2")
            bpZ, bpZg = buf("d_pZ"), buf("d_pZg")
            bpAV, bpAVg = buf("d_pAV"), buf("d_pAVg")
            bxab = buf("xn2ab")

            SC = 0.08838834764831845
            for hh in range(8):
                half4, h = divmod(hh, 4)
                h = hh
                cx.dma("gpsimd", fm3(wqh, 128), fm(cx.ap["WqT"][:, h * 128:(h + 1) * 128]),
                       writes=[bwq], sem=cx.s["gpsimd"])
                cx.dma("gpsimd", fm3(wkh, 128), fm(cx.ap["WkT"][:, h * 128:(h + 1) * 128]),
                       writes=[bwk], sem=cx.s["gpsimd"])
                cx.dma("gpsimd", fm3(wvh, 128), fm(cx.ap["WvT"][:, h * 128:(h + 1) * 128]),
                       writes=[bwv], sem=cx.s["gpsimd"])
                prot4 = [(pS1, bpS1), (pS2, bpS2), (pT, bpT), (pT2, bpT2)]
                for which, wb, bwb, dst, bdst, scl in (
                        ("q", wqh, bwq, q_hd, bqh, SC), ("k", wkh, bwk, k_hd, bkh, 1.0)):
                    for g in range(NT):
                        ps, bps = prot4[g % 4]
                        cx._hazard_waits("tensor", [bwb, bxab], [bps])
                        for c in range(DC):
                            cx.step("tensor", lambda e_, c=c, g=g, wb=wb, ps=ps: e_.matmul(
                                ps[:, :], wb[:, c * 128:(c + 1) * 128],
                                xn2ab[:, c * T + g * G: c * T + (g + 1) * G],
                                start=(c == 0), stop=(c == DC - 1)))
                        cx._hazard_update(cx.s["tensor"], cx.s["tensor"].n, [bwb, bxab], [bps])
                        if scl == 1.0:
                            cx.step("scalar", lambda e_, ps=ps: e_.activation(
                                qr[:, :], ps[:, :], AF.Identity),
                                reads=[bps], writes=[bqr])
                        else:
                            cx.step("scalar", lambda e_, ps=ps, scl=scl: e_.activation(
                                qr[:, :], ps[:, :], AF.Identity, scale=scl),
                                reads=[bps], writes=[bqr])
                        ch = cos_h[:, g * G:(g + 1) * G]
                        sh_lo = sin_h[0:64, g * G:(g + 1) * G]
                        sh_hi = sin_h[64:128, g * G:(g + 1) * G]
                        cx.step("vector", lambda e_, sh_hi=sh_hi: e_.tensor_mul(
                            t1[0:64, :], qr[64:128, :], sh_hi),
                            reads=[bqr, buf("sin_h")], writes=[bt1])
                        cx.step("vector", lambda e_, sh_lo=sh_lo: e_.tensor_mul(
                            t1[64:128, :], qr[0:64, :], sh_lo),
                            reads=[bqr, buf("sin_h")], writes=[bt1])
                        cx.step("vector", lambda e_, ch=ch: e_.tensor_mul(qr[:, :], qr[:, :], ch),
                                reads=[buf("cos_h")], writes=[bqr])
                        cx.step("vector", lambda e_, dst=dst, g=g: e_.tensor_tensor(
                            dst[0:64, g * G:(g + 1) * G], qr[0:64, :], t1[0:64, :], ALU.subtract),
                            reads=[bqr, bt1], writes=[bdst])
                        cx.step("vector", lambda e_, dst=dst, g=g: e_.tensor_add(
                            dst[64:128, g * G:(g + 1) * G], qr[64:128, :], t1[64:128, :]),
                            reads=[bqr, bt1], writes=[bdst])
                for tt in range(16):
                    ps, bps = prot4[tt % 4]
                    cx._hazard_waits("tensor", [bwv, bxab], [bps])
                    for c in range(DC):
                        cx.step("tensor", lambda e_, c=c, tt=tt, ps=ps: e_.matmul(
                            ps[:, 0:128], xn2ab[:, c * T + tt * 128: c * T + (tt + 1) * 128],
                            wvh[:, c * 128:(c + 1) * 128], start=(c == 0), stop=(c == DC - 1)))
                    cx._hazard_update(cx.s["tensor"], cx.s["tensor"].n, [bwv, bxab], [bps])
                    cx.step("vector", lambda e_, tt=tt, ps=ps: e_.tensor_copy(
                        v_tm4[:, tt * 128:(tt + 1) * 128], ps[:, 0:128]),
                        reads=[bps], writes=[bvtm])
                for src_hd, bsrc, dstrow, bdstrow, scl in (
                        (q_hd, bqh, qp_row, bqp, 11.313708498984761),
                        (k_hd, bkh, kp_row, bkp, None)):
                    for g in range(NT):
                        ps, bps = (pS1, bpS1) if g % 2 == 0 else (pS2, bpS2)
                        cx.step("tensor", lambda e_, h=h, g=g, src_hd=src_hd, ps=ps: e_.matmul(
                            ps[0:1, 0:G], hd3b[0:3, h:h + 1], src_hd[0:3, g * G:(g + 1) * G],
                            start=True, stop=True),
                            reads=[buf("hd3b"), bsrc], writes=[bps])
                        if scl is not None:
                            cx.step("scalar", lambda e_, g=g, ps=ps, dstrow=dstrow, scl=scl: e_.activation(
                                dstrow[0:1, g * G:(g + 1) * G], ps[0:1, 0:G], AF.Identity, scale=scl),
                                reads=[bps], writes=[bdstrow])
                        else:
                            cx.step("scalar", lambda e_, g=g, ps=ps, dstrow=dstrow: e_.activation(
                                dstrow[0:1, g * G:(g + 1) * G], ps[0:1, 0:G], AF.Identity),
                                reads=[bps], writes=[bdstrow])
                for it in range(16):
                    njb = it // 4 + 1
                    for jb in range(njb):
                        ps, bps = (pS1, bpS1) if jb % 2 == 0 else (pS2, bpS2)
                        cx.step("tensor", lambda e_, it=it, jb=jb, ps=ps: e_.matmul(
                            ps[:, :], q_hd[:, it * 128:(it + 1) * 128],
                            k_hd[:, jb * G:(jb + 1) * G], start=True, stop=True),
                            reads=[bqh, bkh], writes=[bps])
                        if jb == 0:
                            cx.step("vector", lambda e_, it=it, ps=ps: e_.reduce_max(
                                nm_all[:, it:it + 1], ps[:, :], axis=AX.X, negate=True),
                                reads=[bps], writes=[bnm])
                        else:
                            cx.step("vector", lambda e_, ps=ps: e_.reduce_max(
                                sq_sb[:, 0:1], ps[:, :], axis=AX.X, negate=True),
                                reads=[bps], writes=[bsq])
                            cx.step("vector", lambda e_, it=it: e_.tensor_tensor(
                                nm_all[:, it:it + 1], nm_all[:, it:it + 1], sq_sb[:, 0:1], ALU.min),
                                reads=[bsq], writes=[bnm])
                cx.step("vector", lambda e_: e_.tensor_tensor(
                    nm_grp[:, :], nm_all[:, 0:16:2], nm_all[:, 1:16:2], ALU.min),
                    reads=[bnm], writes=[bnmg])
                cx.step("vector", lambda e_: e_.tensor_tensor(
                    nm_g4[:, :], nm_grp[:, 0:8:2], nm_grp[:, 1:8:2], ALU.min),
                    reads=[bnmg], writes=[bnm4])
                cx.step("tensor", lambda e_: e_.matmul(pS1[0:4, 0:128], nm_g4[:, :], I128[:, :],
                                                       start=True, stop=True),
                        reads=[bnm4, buf("I128")], writes=[bpS1])
                cx.step("vector", lambda e_: e_.tensor_copy(mT_sb[0:4, :], pS1[0:4, 0:128]),
                        reads=[bpS1], writes=[bmT])
                cx.step("vector", lambda e_: e_.tensor_reduce(mTr[0:4, 0:1], mT_sb[0:4, :], AX.X, ALU.min),
                        reads=[bmT], writes=[bmTr])
                for gp in range(4):
                    cx.step("tensor", lambda e_, gp=gp: e_.matmul(
                        pS2[:, 0:1], sel8[0:4, gp * 128:(gp + 1) * 128], mTr[0:4, 0:1],
                        start=True, stop=True),
                        reads=[buf("sel8"), bmTr], writes=[bpS2])
                    cx.step("vector", lambda e_, gp=gp: e_.tensor_copy(
                        negm_cols[:, gp:gp + 1], pS2[:, 0:1]),
                        reads=[bpS2], writes=[bnmc])
                for gp in range(4):
                    cx.step("vector", lambda e_, gp=gp: e_.reduce_max(
                        gq_mx[0:1, gp:gp + 1], qp_row[0:1, gp * G:(gp + 1) * G], axis=AX.X),
                        reads=[bqp], writes=[bgq_mx])
                    cx.step("vector", lambda e_, gp=gp: e_.tensor_reduce(
                        gq_mn[0:1, gp:gp + 1], qp_row[0:1, gp * G:(gp + 1) * G], AX.X, ALU.min),
                        reads=[bqp], writes=[bgq_mn])
                    cx.step("vector", lambda e_, gp=gp: e_.reduce_max(
                        gk_mx[0:1, gp:gp + 1], kp_row[0:1, gp * G:(gp + 1) * G], axis=AX.X),
                        reads=[bkp], writes=[bgk_mx])
                    cx.step("vector", lambda e_, gp=gp: e_.tensor_reduce(
                        gk_mn[0:1, gp:gp + 1], kp_row[0:1, gp * G:(gp + 1) * G], AX.X, ALU.min),
                        reads=[bkp], writes=[bgk_mn])
                for shift in (1, 2):
                    cx.step("vector", lambda e_, shift=shift: e_.tensor_tensor(
                        gk_mx[0:1, shift:4], gk_mx[0:1, shift:4], gk_mx[0:1, 0:4 - shift], ALU.max),
                        writes=[bgk_mx])
                    cx.step("vector", lambda e_, shift=shift: e_.tensor_tensor(
                        gk_mn[0:1, shift:4], gk_mn[0:1, shift:4], gk_mn[0:1, 0:4 - shift], ALU.min),
                        writes=[bgk_mn])
                cx.step("vector", lambda e_: e_.tensor_mul(mg_row[0:1, 0:4], gq_mx[0:1, 0:4], gk_mx[0:1, 0:4]),
                        reads=[bgq_mx, bgk_mx], writes=[bmgr])
                for a_t, a_b, b_t, b_b in ((gq_mx, bgq_mx, gk_mn, bgk_mn),
                                           (gq_mn, bgq_mn, gk_mx, bgk_mx),
                                           (gq_mn, bgq_mn, gk_mn, bgk_mn)):
                    cx.step("vector", lambda e_, a_t=a_t, b_t=b_t: e_.tensor_mul(
                        mg4[0:1, 0:4], a_t[0:1, 0:4], b_t[0:1, 0:4]),
                        reads=[a_b, b_b], writes=[bmg4])
                    cx.step("vector", lambda e_: e_.tensor_tensor(
                        mg_row[0:1, 0:4], mg_row[0:1, 0:4], mg4[0:1, 0:4], ALU.max),
                        reads=[bmg4], writes=[bmgr])
                cx.step("vector", lambda e_: e_.tensor_scalar_mul(mg_row[0:1, 0:4], mg_row[0:1, 0:4], -1.0),
                        writes=[bmgr])
                cx.step("tensor", lambda e_: e_.matmul(pS1[0:4, 0:1], mg_row[0:1, 0:4], ones[0:1, 0:1],
                                                       start=True, stop=True),
                        reads=[bmgr, buf("ones")], writes=[bpS1])
                cx.step("vector", lambda e_: e_.tensor_copy(mg_col[0:4, 0:1], pS1[0:4, 0:1]),
                        reads=[bpS1], writes=[bmgc])
                for gp in range(4):
                    cx.step("tensor", lambda e_, gp=gp: e_.matmul(
                        pS2[:, 0:1], sel8[0:4, gp * 128:(gp + 1) * 128], mg_col[0:4, 0:1],
                        start=True, stop=True),
                        reads=[buf("sel8"), bmgc], writes=[bpS2])
                    cx.step("vector", lambda e_, gp=gp: e_.tensor_copy(
                        negmg_cols[:, gp:gp + 1], pS2[:, 0:1]),
                        reads=[bpS2], writes=[bnmgc])
                cx.step("tensor", lambda e_, h=h: e_.matmul(pS1[:, 0:1], ones[0:1, :], hs_row[0:1, h:h + 1],
                                                            start=True, stop=True),
                        reads=[buf("hs_row"), buf("ones")], writes=[bpS1])
                cx.step("vector", lambda e_: e_.tensor_copy(hs_col[:, 0:1], pS1[:, 0:1]),
                        reads=[bpS1], writes=[bhsc])
                for gp in range(4):
                    njt = gp * 4 + 4
                    mrot = [(pT, bpT), (pS1, bpS1)]
                    grot = [(pT2, bpT2), (pS2, bpS2)]
                    for jt in range(njt):
                        mp, bmp = mrot[jt % 2]
                        gpp, bgpp = grot[jt % 2]
                        cx.step("tensor", lambda e_, jt=jt, gp=gp, mp=mp: e_.matmul(
                            mp[:, :], k_hd[:, jt * 128:(jt + 1) * 128],
                            q_hd[:, gp * G:(gp + 1) * G], start=True, stop=True),
                            reads=[bkh, bqh], writes=[bmp])
                        if jt >= njt - 4:
                            off = (3 - (jt - (njt - 4))) * 128
                            cx.step("vector", lambda e_, off=off, mp=mp: e_.tensor_add(
                                mp[:, :], mp[:, :], maskTW[:, off:off + G]),
                                reads=[buf("maskTW")], writes=[bmp])
                        cx.step("scalar", lambda e_, jt=jt, gp=gp, mp=mp: e_.activation(
                            PT[:, jt * G:(jt + 1) * G], mp[:, :], AF.Exp,
                            bias=negm_cols[:, gp:gp + 1]),
                            reads=[bmp, bnmc], writes=[bPT])
                        cx.step("tensor", lambda e_, jt=jt, gp=gp, gpp=gpp: e_.matmul(
                            gpp[:, :], kp_row[0:1, jt * 128:(jt + 1) * 128],
                            qp_row[0:1, gp * G:(gp + 1) * G], start=True, stop=True),
                            reads=[bqp, bkp], writes=[bgpp])
                        if jt >= njt - 4:
                            off = (3 - (jt - (njt - 4))) * 128
                            cx.step("vector", lambda e_, off=off, gpp=gpp: e_.tensor_add(
                                gpp[:, :], gpp[:, :], maskTW[:, off:off + G]),
                                reads=[buf("maskTW")], writes=[bgpp])
                        cx.step("scalar", lambda e_, jt=jt, gp=gp, gpp=gpp: e_.activation(
                            PgT[:, jt * G:(jt + 1) * G], gpp[:, :], AF.Exp,
                            bias=negmg_cols[:, gp:gp + 1]),
                            reads=[bgpp, bnmgc], writes=[bPgT])
                    cx._hazard_waits("tensor", [bPT, buf("onesb")], [bpZ])
                    for jt in range(njt):
                        cx.step("tensor", lambda e_, jt=jt: e_.matmul(
                            pZ[0:1, :], onesb[:, 0:1], PT[:, jt * G:(jt + 1) * G],
                            start=(jt == 0), stop=(jt == njt - 1)))
                    cx._hazard_update(cx.s["tensor"], cx.s["tensor"].n, [bPT, buf("onesb")], [bpZ])
                    cx._hazard_waits("tensor", [bPgT, buf("onesb")], [bpZg])
                    for jt in range(njt):
                        cx.step("tensor", lambda e_, jt=jt: e_.matmul(
                            pZg[0:1, :], onesb[:, 0:1], PgT[:, jt * G:(jt + 1) * G],
                            start=(jt == 0), stop=(jt == njt - 1)))
                    cx._hazard_update(cx.s["tensor"], cx.s["tensor"].n, [bPgT, buf("onesb")], [bpZg])
                    cx._hazard_waits("tensor", [bPT, bvtm], [bpAV])
                    for jt in range(njt):
                        cx.step("tensor", lambda e_, jt=jt: e_.matmul(
                            pAV[:, :], v_tm4[:, jt * 128:(jt + 1) * 128], PT[:, jt * G:(jt + 1) * G],
                            start=(jt == 0), stop=(jt == njt - 1)))
                    cx._hazard_update(cx.s["tensor"], cx.s["tensor"].n, [bPT, bvtm], [bpAV])
                    cx._hazard_waits("tensor", [bPgT, bvtm], [bpAVg])
                    for jt in range(njt):
                        cx.step("tensor", lambda e_, jt=jt: e_.matmul(
                            pAVg[:, :], v_tm4[:, jt * 128:(jt + 1) * 128], PgT[:, jt * G:(jt + 1) * G],
                            start=(jt == 0), stop=(jt == njt - 1)))
                    cx._hazard_update(cx.s["tensor"], cx.s["tensor"].n, [bPgT, bvtm], [bpAVg])
                    cx.step("vector", lambda e_: e_.tensor_copy(Zrow[0:1, :], pZ[0:1, :]),
                            reads=[bpZ], writes=[bZ])
                    cx.step("vector", lambda e_: e_.reciprocal(rZrow[0:1, :], Zrow[0:1, :]),
                            reads=[bZ], writes=[brZ])
                    cx.step("vector", lambda e_: e_.tensor_copy(Zgrow[0:1, :], pZg[0:1, :]),
                            reads=[bpZg], writes=[bZg])
                    cx.step("vector", lambda e_: e_.reciprocal(rZgrow[0:1, :], Zgrow[0:1, :]),
                            reads=[bZg], writes=[brZg])
                    cx.step("tensor", lambda e_: e_.matmul(pT[:, :], ones[0:1, :], rZrow[0:1, :],
                                                           start=True, stop=True),
                            reads=[brZ, buf("ones")], writes=[bpT])
                    cx.step("vector", lambda e_: e_.tensor_copy(rZb[:, :], pT[:, :]),
                            reads=[bpT], writes=[bZb])
                    cx.step("tensor", lambda e_: e_.matmul(pT2[:, :], ones[0:1, :], rZgrow[0:1, :],
                                                           start=True, stop=True),
                            reads=[brZg, buf("ones")], writes=[bpT2])
                    cx.step("vector", lambda e_: e_.tensor_copy(rZgb[:, :], pT2[:, :]),
                            reads=[bpT2], writes=[bZgb])
                    cx.step("vector", lambda e_: e_.tensor_mul(om[:, :], pAV[:, :], rZb[:, :]),
                            reads=[bpAV, bZb], writes=[bom])
                    cx.step("vector", lambda e_: e_.tensor_mul(og[:, :], pAVg[:, :], rZgb[:, :]),
                            reads=[bpAVg, bZgb], writes=[bog])
                    cx.step("vector", lambda e_: e_.tensor_tensor(og[:, :], og[:, :], om[:, :], ALU.subtract),
                            reads=[bom], writes=[bog])
                    cx.step("vector", lambda e_: e_.tensor_scalar_mul(og[:, :], og[:, :], hs_col[:, 0:1]),
                            reads=[bhsc], writes=[bog])
                    cx.step("vector", lambda e_, h=h, gp=gp: e_.tensor_tensor(
                        out_all[:, (h % 4) * T + gp * G: (h % 4) * T + (gp + 1) * G],
                        om[:, :], og[:, :], ALU.add),
                        reads=[bom, bog], writes=[boa])
                if h % 4 == 3:
                    half = h // 4
                    for jc in range(DC):
                        cx.dma("gpsimd",
                               wo4[:, 0:512].rearrange("p (c t) -> p c t", t=128),
                               fm(cx.ap["WoT"][half * 512:(half + 1) * 512, jc * 128:(jc + 1) * 128]),
                               writes=[bwo4], sem=cx.s["gpsimd"])
                        for g in range(NT):
                            ps, bps = (pS1, bpS1) if g % 2 == 0 else (pS2, bpS2)
                            cx._hazard_waits("tensor", [boa, bwo4], [bps])
                            for c in range(4):
                                cx.step("tensor", lambda e_, jc=jc, c=c, g=g, ps=ps: e_.matmul(
                                    ps[:, :], wo4[:, c * 128:(c + 1) * 128],
                                    out_all[:, c * T + g * G: c * T + (g + 1) * G],
                                    start=(c == 0), stop=(c == 3)))
                            cx._hazard_update(cx.s["tensor"], cx.s["tensor"].n, [boa, bwo4], [bps])
                            cx.dma("sync", qr[:, :], acc_d[jc * 128:(jc + 1) * 128, g * G:(g + 1) * G],
                                   reads=[bacc_d], writes=[bqr], sem=cx.s["sync"])
                            cx.step("vector", lambda e_, ps=ps: e_.tensor_tensor(
                                qr[:, :], qr[:, :], ps[:, :], ALU.add),
                                reads=[bps], writes=[bqr])
                            cx.dma("sync", acc_d[jc * 128:(jc + 1) * 128, g * G:(g + 1) * G], qr[:, :],
                                   reads=[bqr, bacc_d], writes=[bacc_d], sem=cx.s["out"])
            stDp.close()
            stD.close()
            if upto >= 3:
                stC.close()
            cx.barrier(B)

        if upto >= 5:
            stE = ExitStack()
            t5_ = lambda n, s, d=F32: mktile(stE, n, s, d)
            acc_f = t5_("acc_f", [128, DC * T])
            h_sb5 = t5_("h_sb5", [128, 32 * 1024], BF16)
            wbufs5 = [t5_("wbuf5A", [128, 8 * 512], BF16), t5_("wbuf5B", [128, 8 * 512], BF16)]
            b1_cols = t5_("b1_cols", [128, 8 * 32]); b2_cols = t5_("b2_cols", [128, 8 * 8])
            cwb5 = t5_("cwb5", [128, 2 * G])
            ytmp5 = t5_("ytmp5", [128, G]); yts5 = t5_("yts5", [128, G])
            offs_row = t5_("offs_row", [1, 8]); offs_b = t5_("offs_b", [128, 8])
            po = t5_("po", [128, 1]); posg = t5_("posg", [128, 1])
            keep = t5_("keep", [128, 1]); cw1 = t5_("cw1", [128, 1])
            cwf = t5_("cwf", [128, 8]); sel_sb5 = t5_("sel_sb5", [128, 8])
            cwst5 = t5_("cwst5", [8, 128])
            stEp = ExitStack()
            hp = [stEp.enter_context(nc.psum_tensor("e_hp0", [128, G], F32)),
                  stEp.enter_context(nc.psum_tensor("e_hp1", [128, G], F32))]
            yp = [stEp.enter_context(nc.psum_tensor("e_yp0", [128, G], F32)),
                  stEp.enter_context(nc.psum_tensor("e_yp1", [128, G], F32))]
            cp = stEp.enter_context(nc.psum_tensor("e_cp", [128, G], F32))
            yp4 = [hp[0], hp[1], yp[0], yp[1]]
            bacc = buf("acc_f"); bh = buf("h_sb5")
            bwbufs5 = [buf("wbuf5A"), buf("wbuf5B")]
            bcwb, byt, byts = buf("cwb5"), buf("ytmp5"), buf("yts5")
            boffr, boffb = buf("offs_row"), buf("offs_b")
            bpo, bposg, bkeep, bcw1, bcwf = buf("po"), buf("posg"), buf("keep"), buf("cw1"), buf("cwf")
            bsel5, bcwst5 = buf("sel_sb5"), buf("cwst5")
            bhp = [buf("e_hp0"), buf("e_hp1")]
            byp = [buf("e_yp0"), buf("e_yp1")]
            bcp = buf("e_cp")
            byp4 = [bhp[0], bhp[1], byp[0], byp[1]]

            cx.dma("sync", b1_cols[:, :], cx.ap["b1_cols"], writes=[buf("b1_cols")], sem=cx.s["sync"])
            cx.dma("sync", b2_cols[:, :], cx.ap["b2_cols"], writes=[buf("b2_cols")], sem=cx.s["sync"])

            cx.q["gpsimd"].append(lambda e_: e_.wait_ge(cc_sem.h, 1))
            cx.dma("gpsimd", ag_sb[:, :], cc_out.ap(), writes=[bag], sem=cx.s["gpsimd"])
            cx.step("tensor", lambda e_: e_.matmul(cp[0:1, 0:8], prefmask[:, 0:1], ag_sb[:, :],
                                                   start=True, stop=True),
                    reads=[buf("prefmask"), bag], writes=[bcp])
            cx.step("vector", lambda e_: e_.tensor_copy(offs_row[0:1, :], cp[0:1, 0:8]),
                    reads=[bcp], writes=[boffr])
            cx.step("tensor", lambda e_: e_.matmul(cp[:, 0:8], ones[0:1, :], offs_row[0:1, :],
                                                   start=True, stop=True),
                    reads=[boffr, buf("ones")], writes=[bcp])
            cx.step("vector", lambda e_: e_.tensor_copy(offs_b[:, :], cp[:, 0:8]),
                    reads=[bcp], writes=[boffb])
            for blk in range(16):
                cx.step("vector", lambda e_, blk=blk: e_.tensor_mul(
                    sel_sb5[:, :], oh_all[:, blk * 8:(blk + 1) * 8], offs_b[:, :]),
                    reads=[boh, boffb], writes=[bsel5])
                cx.step("vector", lambda e_: e_.reduce_sum(po[:, 0:1], sel_sb5[:, :], axis=AX.X),
                        reads=[bsel5], writes=[bpo])
                cx.step("vector", lambda e_, blk=blk: e_.tensor_add(
                    posg[:, 0:1], pos_all[:, blk:blk + 1], po[:, 0:1]),
                    reads=[bpos, bpo], writes=[bposg])
                cx.step("vector", lambda e_: e_.tensor_single_scalar(
                    keep[:, 0:1], posg[:, 0:1], 2560.0, ALU.is_le),
                    reads=[bposg], writes=[bkeep])
                cx.step("vector", lambda e_, blk=blk: e_.tensor_mul(
                    cw1[:, 0:1], keep[:, 0:1], tp_all[:, blk:blk + 1]),
                    reads=[bkeep, btp], writes=[bcw1])
                cx.step("vector", lambda e_, blk=blk: e_.tensor_scalar_mul(
                    cwf[:, :], oh_all[:, blk * 8:(blk + 1) * 8], cw1[:, 0:1]),
                    reads=[boh, bcw1], writes=[bcwf])
                cx.step("tensor", lambda e_: e_.transpose(cp[0:8, 0:128], cwf[:, :], I128[:, :]),
                        reads=[bcwf, buf("I128")], writes=[bcp])
                cx.step("vector", lambda e_: e_.tensor_copy(cwst5[:, :], cp[0:8, 0:128]),
                        reads=[bcp], writes=[bcwst5])
                cx.step("vector", lambda e_, blk=blk: e_.tensor_copy(
                    cwT_sb[0:8, blk * 128:(blk + 1) * 128], cwst5[:, :]),
                    reads=[bcwst5], writes=[bcw])
            if dbg:
                cx.dma("sync", cw1_dbg, cwT_sb[:, :], reads=[bcw], sem=cx.s["out"])
            for c in range(DC):
                cx.dma("sync", acc_f[:, c * T:(c + 1) * T], acc_d[c * 128:(c + 1) * 128, :],
                       reads=[bacc_d], writes=[bacc], sem=cx.s["sync"])
            for ex in range(8):
                for nt in range(2):
                    emit_moe(cx, ex=ex, nt=nt, T_=T, G_=G, xnb=xnb2, bxnb=bxnb2,
                             h_sb=h_sb5, bh=bh, wbufs=wbufs5, bwbufs=bwbufs5,
                             accsl_fn=lambda jc, ns, nt=nt: acc_f[:, jc * T + nt * 1024 + ns * G: jc * T + nt * 1024 + (ns + 1) * G],
                             bacc=bacc, cw_tile=cwT_sb, bcw=bcw, sel8=sel8, bsel8=buf("sel8"),
                             cwb_sb=cwb5, bcwb=bcwb, ytmp=ytmp5, byt=byt,
                             yts=yts5, byts=byts, b1_cols=b1_cols, bb1=buf("b1_cols"),
                             b2_cols=b2_cols, bb2=buf("b2_cols"), hp=hp, bhp=bhp,
                             yp4=yp4, byp4=byp4, cp=cp, bcp=bcp,
                             w1_ap=cx.ap["w1_1"], w2_ap=cx.ap["w2_1"], act="relu")
            for c in range(DC):
                cx.dma("sync", cx.ap["OUT_T"][c * 128:(c + 1) * 128, :], acc_f[:, c * T:(c + 1) * T],
                       reads=[bacc], sem=cx.s["out"])
            stEp.close()
            stE.close()
        elif upto == 3:
            stC.close()

        cx.finish(block)
    return nc


def prep_fused_inputs(x_core, W, C, core):
    m = {"xT": np.ascontiguousarray(x_core.T)}
    m["WqkvT"] = np.ascontiguousarray(W["L0_attn_Wqkv"].T)
    m["bq_cols"] = colmat(W["L0_attn_bqkv"], 24)
    m["WoutT"] = np.ascontiguousarray(W["L0_attn_Wout"].T)
    m["bout_cols"] = colmat(W["L0_attn_bout"], 8)
    m["gamma_cols"] = colmat(W["L0_attn_gamma"], 8)
    m["beta_cols"] = colmat(W["L0_attn_beta"], 8)
    m["rwT"] = np.ascontiguousarray(W["L0_ffn_router_w"].T)
    m["w1_0"] = W["L0_ffn_w1"]; m["w2_0"] = W["L0_ffn_w2"]
    m["WqT"] = np.ascontiguousarray(W["L1_attn_Wq"].T)
    m["WkT"] = np.ascontiguousarray(W["L1_attn_Wk"].T)
    m["WvT"] = np.ascontiguousarray(W["L1_attn_Wv"].T)
    m["WoT"] = np.ascontiguousarray(W["L1_attn_Wo"].T)
    m["lng_cols"] = colmat(W["L1_attn_ln_g"], 8)
    m["lnb_cols"] = colmat(W["L1_attn_ln_b"], 8)
    m["hs_row"] = np.ascontiguousarray(np.asarray(W["L1_attn_head_scales"], np.float32)[None, :])
    m["hd3"] = np.ascontiguousarray(np.asarray(W["L1_attn_head_dirs"], np.float32).T)
    m["gwT"] = np.ascontiguousarray(W["L1_ffn_gate_w"].T)
    m["gb_row"] = np.ascontiguousarray(np.asarray(W["L1_ffn_gate_b"], np.float32)[None, :])
    pm = np.zeros((8, 1), np.float32); pm[:core, 0] = 1.0
    m["prefmask"] = pm
    m["w1_1"] = W["L1_ffn_w1"]; m["w2_1"] = W["L1_ffn_w2"]
    b1 = np.asarray(W["L1_ffn_b1"], np.float32)
    b2 = np.asarray(W["L1_ffn_b2"], np.float32)
    m["b1_cols"] = np.ascontiguousarray(b1.reshape(8, 32, 128).transpose(2, 0, 1).reshape(128, 8 * 32))
    m["b2_cols"] = np.ascontiguousarray(b2.reshape(8, 8, 128).transpose(2, 0, 1).reshape(128, 8 * 8))
    m["ones"] = C["ones"]; m["onesb"] = C["onesb"]
    m["I128"] = C["I128"]; m["I128b"] = C["I128b"]; m["U128"] = C["U128"]
    m["U512"] = C["U512"]
    jj, ii = np.meshgrid(np.arange(128), np.arange(256), indexing="ij")
    tri = np.where(jj <= ii - 128, 0.0, BIGNEG).astype(np.float32)
    m["maskT3"] = np.concatenate([tri, np.zeros((128, 128), np.float32)], axis=1)
    jj2, ii2 = np.meshgrid(np.arange(128), np.arange(128), indexing="ij")
    triW = np.where(jj2 <= ii2, 0.0, BIGNEG).astype(np.float32)
    m["maskTW"] = np.ascontiguousarray(np.concatenate(
        [np.full((128, 384), BIGNEG, np.float32), triW, np.zeros((128, 384), np.float32)], axis=1))
    m["sel8"] = C["sel8"]
    m["cos_h"] = C["cosf"]
    m["sin_h"] = np.ascontiguousarray(np.abs(C["sinm"]))
    return m


_BUILD_CACHE = {}


def _get(name, builder):
    if name not in _BUILD_CACHE:
        _BUILD_CACHE[name] = builder()
    return _BUILD_CACHE[name]


def kernel(**inputs):
    W = inputs
    C = host_constants()
    x = np.asarray(W["L0_attn_x"])
    cores = list(range(NCORES))
    nc = _get("fused", build_fused)
    res = run_bass_kernel_spmd(nc, [prep_fused_inputs(x[c], W, C, c) for c in cores],
                               core_ids=cores)
    out = np.stack([np.ascontiguousarray(r["OUT_T"].T) for r in res.results], axis=0)
    return out.astype(np.float32)


def kernel_multilaunch(**inputs):
    W = inputs
    C = host_constants()
    x = np.asarray(W["L0_attn_x"])
    cores = list(range(NCORES))

    nc_a = _get("l0a", build_l0a)
    res_a = run_bass_kernel_spmd(nc_a, [prep_l0a_inputs(x[c], W, C) for c in cores], core_ids=cores)

    nc_f = _get("l0f", build_l0f)
    res_f = run_bass_kernel_spmd(
        nc_f, [prep_l0f_inputs(r["xn_T"], r["acc_T"], r["cwT"], W, C) for r in res_a.results],
        core_ids=cores)

    nc_1a = _get("l1a", build_l1a)
    res_1a = run_bass_kernel_spmd(
        nc_1a, [prep_l1a_inputs(r["x1_T"], W, C) for r in res_f.results], core_ids=cores)

    nc_g = _get("l1g", build_l1g)
    res_g = run_bass_kernel_spmd(
        nc_g, [prep_l1g_inputs(res_1a.results[c]["xn2_T"], W, C, c) for c in cores], core_ids=cores)

    nc_1f = _get("l1f", build_l1f)
    res_1f = run_bass_kernel_spmd(
        nc_1f, [prep_l1f_inputs(res_1a.results[c]["xn2_T"], res_1a.results[c]["acc1_T"],
                                res_g.results[c]["cwT1"], W, C) for c in cores],
        core_ids=cores)

    out = np.stack([np.ascontiguousarray(r["out_T"].T) for r in res_1f.results], axis=0)
    return out.astype(np.float32)
